# version 06
# baseline (speedup 1.0000x reference)
from contextlib import ExitStack
from collections import defaultdict
import concourse.bass as bass
import concourse.mybir as mybir

from collections import defaultdict

ENGINES = ("tensor", "vector", "scalar", "gpsimd", "sync")


class Prog:
    def __init__(self, nc):
        self.nc = nc
        self.ops = {e: [] for e in ENGINES}
        self.eng_count = {e: 0 for e in ENGINES}
        self.dma_count = defaultdict(int)
        self.res = {}
        self.high_water = {e: defaultdict(int) for e in ENGINES}
        self.sem_handles = {}
        self._es = None

    def init_sems(self, es, dma_keys):
        for e in ENGINES:
            self.sem_handles["@" + e] = es.enter_context(self.nc.semaphore("sem_" + e))
        for k in dma_keys:
            self.sem_handles[k] = es.enter_context(self.nc.semaphore("sem_" + k))

    def _want(self, waits, token):
        if token is None:
            return
        sem, val = token
        if waits.get(sem, 0) < val:
            waits[sem] = val

    def _emit(self, engine, fn, reads, writes, inc_sem, inc_amt, accum_key=None,
              extra_waits=()):
        waits = {}
        pf = getattr(self, "_pending_fence", None)
        if pf and engine in pf:
            self._want(waits, pf.pop(engine))
        for tok in extra_waits:
            self._want(waits, tok)
        for r in reads:
            e = self.res.get(r)
            if e:
                self._want(waits, e["w"])
        for w in writes:
            e = self.res.get(w)
            if e:
                if accum_key is None or e.get("acc") != accum_key:
                    self._want(waits, e["w"])
                    for t in e["r"]:
                        self._want(waits, t)
        hw = self.high_water[engine]
        final_waits = {}
        for sem, val in waits.items():
            if sem == "@" + engine:
                pass
            if hw[sem] < val:
                final_waits[sem] = val
                hw[sem] = val
        if inc_sem is None:
            inc_sem = "@" + engine
        if inc_sem.startswith("@"):
            self.eng_count[engine] += inc_amt
            token = (inc_sem, self.eng_count[engine])
        else:
            self.dma_count[inc_sem] += inc_amt
            token = (inc_sem, self.dma_count[inc_sem])
        for r in reads:
            self.res.setdefault(r, {"w": None, "r": [], "acc": None})["r"].append(token)
        for w in writes:
            e = self.res.setdefault(w, {"w": None, "r": [], "acc": None})
            e["w"] = token
            e["r"] = []
            e["acc"] = accum_key
        self.ops[engine].append((final_waits, fn, inc_sem if not inc_sem.startswith("@") else "@" + engine, inc_amt))
        return token

    def op(self, engine, fn, reads=(), writes=()):
        return self._emit(engine, fn, reads, writes, None, 1)

    def mm_group(self, fns, reads=(), writes=(), key=None):
        key = key or object()
        tok = None
        for i, fn in enumerate(fns):
            last = i == len(fns) - 1
            if last:
                tok = self._emit("tensor", fn, reads, writes, None, 1, accum_key=key)
            else:
                waits = {}
                pf = getattr(self, "_pending_fence", None)
                if pf and "tensor" in pf:
                    self._want(waits, pf.pop("tensor"))
                for r in reads:
                    e = self.res.get(r)
                    if e:
                        self._want(waits, e["w"])
                for w in writes:
                    e = self.res.get(w)
                    if e and e.get("acc") != key:
                        self._want(waits, e["w"])
                        for t in e["r"]:
                            self._want(waits, t)
                        e["acc"] = key
                hw = self.high_water["tensor"]
                fw = {}
                for sem, val in waits.items():
                    if hw[sem] < val:
                        fw[sem] = val
                        hw[sem] = val
                for w in writes:
                    e = self.res.setdefault(w, {"w": None, "r": [], "acc": key})
                    e["acc"] = key
                self.ops["tensor"].append((fw, fn, None, 0))
        return tok

    def dma(self, engine, sem_key, fn, reads=(), writes=()):
        prev = self.dma_count.get(sem_key, 0)
        extra = ((sem_key, prev),) if prev else ()
        return self._emit(engine, fn, reads, writes, sem_key, 16,
                          extra_waits=extra)

    def read_barrier(self, engine, resources):
        raise NotImplementedError

    def barrier_op(self, engine, fn, resources):
        return self.op(engine, fn, reads=tuple(resources), writes=())

    def fence(self):
        all_waits = {}
        for e in ENGINES:
            if self.eng_count[e]:
                all_waits["@" + e] = self.eng_count[e]
        for k, v in self.dma_count.items():
            if v:
                all_waits[k] = v
        self._pending_fence = {}
        for e in ENGINES:
            waits = {s: v for s, v in all_waits.items() if s != "@" + e}
            self.eng_count[e] += 1
            tok = ("@" + e, self.eng_count[e])
            self.ops[e].append((waits, lambda eng: eng.drain(), "@" + e, 1))
            self._pending_fence[e] = tok
            hw = self.high_water[e]
            for s, v in waits.items():
                if hw[s] < v:
                    hw[s] = v
        self.res = {}

    def finalize(self, block):
        sems = self.sem_handles
        for engine in ENGINES:
            stream = self.ops[engine]
            if not stream:
                continue

            def make_body(stream=stream, engine=engine):
                def body(eng):
                    for waits, fn, inc_sem, inc_amt in stream:
                        for sem, val in waits.items():
                            eng.wait_ge(sems[sem], val)
                        inst = fn(eng)
                        if inc_amt and inst is not None:
                            inst.then_inc(sems[inc_sem], inc_amt)
                return body

            getattr(block, engine)(make_body())

    def pending_counts(self):
        return dict(self.eng_count), dict(self.dma_count)


F32 = mybir.dt.float32
BF16 = mybir.dt.bfloat16
AF = mybir.ActivationFunctionType
ALU = mybir.AluOpType
AX = mybir.AxisListType

NEG = -1e30
EPS = 1e-5


class Cfg:
    def __init__(self, S=2048, NRE=7, stage="full", NP=8):
        self.S = S
        self.NRE = NRE
        self.stage = stage
        self.NP = NP


def build(cfg: Cfg):
    S = cfg.S
    SB = S // 128
    SC = S // 512
    NI = 22
    nc = bass.Bass("TRN2", target_bir_lowering=False)

    x_d = nc.dram_tensor("x", [S, 1024], F32, kind="ExternalInput").ap()
    wq_d = nc.dram_tensor("wq", [1024, 1024], F32, kind="ExternalInput").ap()
    wk_d = nc.dram_tensor("wk", [1024, 1024], F32, kind="ExternalInput").ap()
    wv_d = nc.dram_tensor("wv", [1024, 1024], F32, kind="ExternalInput").ap()
    wo_d = nc.dram_tensor("wo", [1024, 1024], F32, kind="ExternalInput").ap()
    bq_d = nc.dram_tensor("bq", [1024], F32, kind="ExternalInput").ap()
    bk_d = nc.dram_tensor("bk", [1024], F32, kind="ExternalInput").ap()
    bv_d = nc.dram_tensor("bv", [1024], F32, kind="ExternalInput").ap()
    bo_d = nc.dram_tensor("bo", [1024], F32, kind="ExternalInput").ap()
    sg_d = nc.dram_tensor("sg", [2816, 1024], F32, kind="ExternalInput").ap()
    su_d = nc.dram_tensor("su", [2816, 1024], F32, kind="ExternalInput").ap()
    sd_d = nc.dram_tensor("sd", [1024, 2816], F32, kind="ExternalInput").ap()
    rg_d = nc.dram_tensor("rg", [7, 2816, 1024], F32, kind="ExternalInput").ap()
    ru_d = nc.dram_tensor("ru", [7, 2816, 1024], F32, kind="ExternalInput").ap()
    rd_d = nc.dram_tensor("rd", [7, 1024, 2816], F32, kind="ExternalInput").ap()
    wr_d = nc.dram_tensor("wr", [7, 1024], F32, kind="ExternalInput").ap()
    rb_d = nc.dram_tensor("rb", [7], F32, kind="ExternalInput").ap()
    identf_d = nc.dram_tensor("identf", [128, 128], F32, kind="ExternalInput").ap()
    identb_d = nc.dram_tensor("identb", [128, 128], F32, kind="ExternalInput").ap()
    maskA_d = nc.dram_tensor("maskA", [128, 256], F32, kind="ExternalInput").ap()
    maskB_d = nc.dram_tensor("maskB", [128, 768], F32, kind="ExternalInput").ap()
    onesrow_d = nc.dram_tensor("onesrow", [1, 128], F32, kind="ExternalInput").ap()

    out_d = nc.dram_tensor("out", [S, 1024], F32, kind="ExternalOutput").ap()
    dbg = {}
    if cfg.stage in ("ln1", "attn", "router"):
        dbg["h"] = nc.dram_tensor("hdbg", [S, 1024], F32, kind="ExternalOutput").ap()
    if cfg.stage == "router":
        dbg["w"] = nc.dram_tensor("wdbg", [128, 7 * SB], F32, kind="ExternalOutput").ap()
    x1_d = nc.dram_tensor("x1d", [8, 128, S], F32, kind="Internal").ap()

    with ExitStack() as es:
        Tz = lambda nm, sh, dt=F32: es.enter_context(nc.sbuf_tensor(nm, sh, dt))
        H = Tz("H", [128, 8 * S], BF16)
        identf = Tz("identf_s", [128, 128], F32)
        identb = Tz("identb_s", [128, 128], BF16)
        onesrow = Tz("onesrow_s", [1, 128], F32)
        biases = Tz("biases", [128, 32], F32)
        brow = Tz("brow", [1, 1024], F32)
        wrstage = Tz("wrstage", [7, 1024], BF16)
        wrT = Tz("wrT", [128, 56], BF16)
        rbrow = Tz("rbrow", [1, 7], F32)
        rbbc = Tz("rbbc", [128, 7], F32)
        wtok = Tz("wtok", [128, 7 * SB], F32)
        wrows = Tz("wrows", [7, S], F32)
        lbuf = Tz("lbuf", [128, 7], F32)
        tbuf = Tz("tbuf", [128, 7], F32)
        mbuf = Tz("mbuf", [128, 7], F32)
        l2buf = Tz("l2buf", [128, 7], F32)
        ebuf = Tz("ebuf", [128, 7], F32)
        r1 = Tz("r1", [128, 1], F32)
        eps_col = Tz("eps_col", [128, 1], F32)
        r2 = Tz("r2", [128, 1], F32)
        xin = [Tz(f"xin{s}", [128, 1024], F32) for s in range(2)]
        st6 = [Tz(f"st6{s}", [128, 12], F32) for s in range(2)]
        mv = [Tz(f"mv{s}", [128, 2], F32) for s in range(2)]
        rv = [Tz(f"rv{s}", [128, 1], F32) for s in range(2)]
        htok = [Tz(f"htok{s}", [128, 1024], BF16) for s in range(2)]

        psc = es.enter_context(nc.psum_tensor("psc", [128, 1024], F32))
        pb = [psc[:, 0:512], psc[:, 512:1024]] + [
            es.enter_context(nc.psum_tensor(f"pb{i}", [128, 512], F32)) for i in range(2, 6)]
        pbb = [es.enter_context(nc.psum_tensor(f"pbb{i}", [128, 1024], BF16)) for i in range(2)]

        prog = Prog(nc)
        dma_keys = ["dx0", "dx1", "dwt0", "dwt1", "dsg0", "dsg1", "dsd0", "dsd1",
                    "dcst", "dwb", "dri0", "dri1", "dro0", "dro1", "dxw", "dout",
                    "dfi0", "dfi1"]
        prog.init_sems(es, dma_keys)
        block = es.enter_context(nc.Block())

        P = prog

        def mm(psum_ap, lhsT_ap, rhs_ap, reads, writes, start=True, stop=True):
            return P.op("tensor", lambda t: t.matmul(psum_ap, lhsT_ap, rhs_ap,
                                                     start=start, stop=stop),
                        reads=reads, writes=writes)

        def Hs(hc, lo, n):
            return H[:, hc * S + lo: hc * S + lo + n]

        P.op("vector", lambda v: v.memset(eps_col[:], EPS), writes=["eps_col"])
        P.dma("sync", "dcst", lambda s: s.dma_start(identf[:], identf_d), writes=["identf"])
        P.dma("sync", "dcst", lambda s: s.dma_start(onesrow[:], onesrow_d), writes=["onesrow"])
        P.dma("sync", "dcst", lambda s: s.dma_start(rbrow[:], rb_d[None, :]), writes=["rbrow"])
        P.dma("gpsimd", "dwb", lambda g: g.dma_start(identb[:], identb_d), writes=["identb"])
        P.dma("gpsimd", "dwb", lambda g: g.dma_start(wrstage[:], wr_d), writes=["wrstage"])
        for bi, b_d in enumerate([bq_d, bk_d, bv_d, bo_d]):
            P.dma("sync", "dcst", lambda s, b_d=b_d: s.dma_start(brow[:], b_d[None, :]),
                  reads=["biasesp"], writes=["brow"])
            for c in range(8):
                mm(pb[5][:, bi * 8 + c: bi * 8 + c + 1], brow[0:1, c * 128:(c + 1) * 128],
                   onesrow[0:1, 0:1], reads=["brow", "onesrow"], writes=["pb5"])
            P.op("vector", lambda v, bi=bi: v.tensor_copy(biases[:, bi * 8:(bi + 1) * 8],
                                                          pb[5][:, bi * 8:(bi + 1) * 8]),
                 reads=["pb5"], writes=["biases", "biasesp"])
        mm(pb[5][:, 40:47], onesrow[:], rbrow[0:1, 0:7], reads=["onesrow", "rbrow"], writes=["pb5"])
        P.op("vector", lambda v: v.tensor_copy(rbbc[:], pb[5][:, 40:47]), reads=["pb5"], writes=["rbbc"])
        for c in range(8):
            mm(pb[5][:, 64 + c * 8: 64 + c * 8 + 7], wrstage[0:7, c * 128:(c + 1) * 128],
               identb[0:7, 0:7], reads=["wrstage", "identb"], writes=["pb5"])
        P.op("vector", lambda v: v.tensor_copy(
            wrT[:].rearrange("p (c f) -> p c f", f=7),
            pb[5][:, 64:128].rearrange("p (c f) -> p c f", f=8)[:, :, 0:7]),
            reads=["pb5"], writes=["wrT"])

        with ExitStack() as es_a:
            Ta = lambda nm, sh, dt=F32: es_a.enter_context(nc.sbuf_tensor(nm, sh, dt))
            X = Ta("X", [128, 8 * S], F32)
            maskA = Ta("maskA_s", [128, 256], F32)
            maskB = Ta("maskB_s", [128, 768], F32)
            P.dma("sync", "dcst", lambda s: s.dma_start(maskA[:], maskA_d), writes=["maskA"])
            P.dma("sync", "dcst", lambda s: s.dma_start(maskB[:], maskB_d), writes=["maskB"])
            wstage = [Ta(f"wstage{s}", [128, 4096], BF16) for s in range(2)]
            WT = [Ta(f"WT{s}", [128, 4096], BF16) for s in range(2)]
            qT = [Ta(f"qT{s}", [128, S], BF16) for s in range(2)]
            kT = [Ta(f"kT{s}", [128, S], BF16) for s in range(2)]
            vT = [Ta("vT0", [128, S], BF16)] * 2
            vtok = [Ta(f"vtok{s}", [128, S], BF16) for s in range(2)]
            opair = [Ta(f"opair{s}", [128, S], BF16) for s in range(2)]
            s2b2 = Ta("s2b2", [128, 768], F32)
            nmx2 = Ta("nmx2", [128, 2], F32)
            nm82 = Ta("nm82", [128, 2], F32)
            rz2 = Ta("rz2", [128, 2], F32)
            Pb = [[Ta(f"Pb{h}_{s}", [128, 384], BF16) for s in range(2)] for h in range(2)]
            PT = [Ta(f"PT{h}", [128, 384], BF16) for h in range(2)]
            zz = [Ta(f"zz{h}", [128, 1], F32) for h in range(2)]

            def Xs(hc, lo, n):
                return X[:, hc * S + lo: hc * S + lo + n]

            def ln_pass(src_is_dram, dest_prefix):
                for sb in range(SB):
                    sl = sb % 2
                    xk = f"xin{sl}"
                    if src_is_dram:
                        P.dma("sync", f"dx{sl}",
                              lambda s, sb=sb, sl=sl: s.dma_start(
                                  xin[sl][:], x_d[sb * 128:(sb + 1) * 128, :]),
                              writes=[xk])
                    else:
                        for hc in range(8):
                            bank = pb[hc // 4]
                            P.op("tensor", lambda t, hc=hc, sb=sb: t.transpose(
                                bank[:, (hc % 4) * 128:(hc % 4 + 1) * 128] if False else
                                pb[hc // 4][:, (hc % 4) * 128:(hc % 4 + 1) * 128],
                                Xs(hc, sb * 128, 128), identf[:]),
                                reads=[f"X{hc}", "identf"], writes=[f"pb{hc // 4}"])
                        for g in range(2):
                            P.op("vector", lambda v, g=g, sl=sl: v.tensor_copy(
                                xin[sl][:, g * 512:(g + 1) * 512], pb[g][:]),
                                reads=[f"pb{g}"], writes=[xk])
                    P.op("vector", lambda v, sl=sl: v.bn_stats(st6[sl][:, 0:6], xin[sl][:, 0:512]),
                         reads=[xk], writes=[f"st6{sl}"])
                    P.op("vector", lambda v, sl=sl: v.bn_stats(st6[sl][:, 6:12], xin[sl][:, 512:1024]),
                         reads=[xk], writes=[f"st6{sl}"])
                    P.op("vector", lambda v, sl=sl: v.bn_aggr(mv[sl][:], st6[sl][:]),
                         reads=[f"st6{sl}"], writes=[f"mv{sl}"])
                    P.op("scalar", lambda s, sl=sl: s.activation(rv[sl][:], mv[sl][:, 1:2],
                                                                 AF.Ln, bias=eps_col[:], scale=1.0),
                         reads=[f"mv{sl}", "eps_col"], writes=[f"rv{sl}"])
                    P.op("scalar", lambda s, sl=sl: s.activation(rv[sl][:], rv[sl][:],
                                                                 AF.Exp, bias=0.0, scale=-0.5),
                         reads=[f"rv{sl}"], writes=[f"rv{sl}"])
                    P.op("vector", lambda v, sl=sl: v.tensor_scalar(
                        htok[sl][:], xin[sl][:], mv[sl][:, 0:1], rv[sl][:],
                        ALU.subtract, ALU.mult),
                        reads=[xk, f"mv{sl}", f"rv{sl}"], writes=[f"htok{sl}"])
                    if src_is_dram:
                        for hc in range(8):
                            P.op("tensor", lambda t, hc=hc, sl=sl: t.transpose(
                                pb[hc // 4][:, (hc % 4) * 128:(hc % 4 + 1) * 128],
                                xin[sl][:, hc * 128:(hc + 1) * 128], identf[:]),
                                reads=[xk, "identf"], writes=[f"pb{hc // 4}"])
                        for g in range(2):
                            dst = X[:, 0:4 * S].rearrange("p (c f) -> p c f", f=S)[
                                :, :, sb * 128:(sb + 1) * 128] if g == 0 else \
                                X[:, 4 * S:8 * S].rearrange("p (c f) -> p c f", f=S)[
                                :, :, sb * 128:(sb + 1) * 128]
                            P.op("vector", lambda v, g=g, dst=dst: v.tensor_copy(
                                dst, pb[g][:].rearrange("p (c f) -> p c f", f=128)),
                                reads=[f"pb{g}"],
                                writes=[f"X{c}" for c in range(4 * g, 4 * g + 4)])
                    for hc in range(8):
                        P.op("tensor", lambda t, hc=hc, sl=sl: t.transpose(
                            pbb[sl][:, hc * 128:(hc + 1) * 128],
                            htok[sl][:, hc * 128:(hc + 1) * 128], identb[:]),
                            reads=[f"htok{sl}", "identb"], writes=[f"pbb{sl}"])
                    dstH = H[:].rearrange("p (c f) -> p c f", f=S)[:, :, sb * 128:(sb + 1) * 128]
                    P.op("vector", lambda v, sl=sl, dstH=dstH: v.tensor_copy(
                        dstH, pbb[sl][:].rearrange("p (c f) -> p c f", f=128)),
                        reads=[f"pbb{sl}"], writes=[f"H{c}" for c in range(8)])

            ln_pass(True, "ln1")

            def emit_transposed_out(src_slicer, dram_ap, src_keys, bf16=False):
                for sb in range(SB):
                    sl = sb % 2
                    if not bf16:
                        for hc in range(8):
                            P.op("tensor", lambda t, hc=hc, sb=sb: t.transpose(
                                pb[hc // 4][:, (hc % 4) * 128:(hc % 4 + 1) * 128],
                                src_slicer(hc, sb * 128, 128), identf[:]),
                                reads=[src_keys[hc], "identf"], writes=[f"pb{hc // 4}"])
                        for g in range(2):
                            P.op("vector", lambda v, g=g, sl=sl: v.tensor_copy(
                                xin[sl][:, g * 512:(g + 1) * 512], pb[g][:]),
                                reads=[f"pb{g}"], writes=[f"xin{sl}"])
                    else:
                        for hc in range(8):
                            P.op("tensor", lambda t, hc=hc, sb=sb, sl=sl: t.transpose(
                                pbb[sl][:, hc * 128:(hc + 1) * 128],
                                src_slicer(hc, sb * 128, 128), identb[:]),
                                reads=[src_keys[hc], "identb"], writes=[f"pbb{sl}"])
                        P.op("vector", lambda v, sl=sl: v.tensor_copy(
                            xin[sl][:], pbb[sl][:]),
                            reads=[f"pbb{sl}"], writes=[f"xin{sl}"])
                    P.dma("scalar", "dout", lambda s, sb=sb, sl=sl: s.dma_start(
                        dram_ap[sb * 128:(sb + 1) * 128, :], xin[sl][:]),
                        reads=[f"xin{sl}"])

            if cfg.stage == "ln1":
                emit_transposed_out(Xs, out_d, [f"X{c}" for c in range(8)])
                emit_transposed_out(Hs, dbg["h"], [f"H{c}" for c in range(8)], bf16=True)
                _finish(P, block)
                return nc

            NP = cfg.NP

            def emit_wdma(hp):
                sl = hp % 2
                wsk = f"wstage{sl}"
                for m, w_d in enumerate([wq_d, wk_d, wv_d]):
                    P.dma("gpsimd", f"dwt{sl}",
                          lambda g, m=m, w_d=w_d, sl=sl, hp=hp: g.dma_start(
                              wstage[sl][:, m * 1024:(m + 1) * 1024],
                              w_d[hp * 128:(hp + 1) * 128, :]),
                          writes=[wsk])
                src_wo = wo_d.rearrange("(c p) f -> p c f", p=128)[:, :, hp * 128:(hp + 1) * 128]
                P.dma("gpsimd", f"dwt{sl}",
                      lambda g, sl=sl, src_wo=src_wo: g.dma_start(
                          wstage[sl][:, 3072:4096].rearrange("p (c f) -> p c f", f=128), src_wo),
                      writes=[wsk])

            def filler_items(hp):
                items = []
                sl = hp % 2
                wsk, wtk = f"wstage{sl}", f"WT{sl}"
                for seg in range(4):
                    def wt_seg(seg=seg, sl=sl, wsk=wsk, wtk=wtk):
                        for c in range(8):
                            P.op("tensor", lambda t, seg=seg, c=c, sl=sl: t.transpose(
                                pbb[1][:, c * 128:(c + 1) * 128],
                                wstage[sl][:, seg * 1024 + c * 128: seg * 1024 + (c + 1) * 128],
                                identb[:]),
                                reads=[wsk, "identb"], writes=["pbb1"])
                        P.op("scalar", lambda s, seg=seg, sl=sl: s.copy(
                            WT[sl][:, seg * 1024:(seg + 1) * 1024], pbb[1][:]),
                            reads=["pbb1"], writes=[wtk])
                    items.append(wt_seg)
                dstmap = {"qT": qT, "kT": kT, "vT": vT}
                for m, (dstk, bcol) in enumerate([("qT", 0), ("kT", 1), ("vT", 2)]):
                    for n in range(SC):
                        def qkv_one(m=m, n=n, sl=sl, dstk=dstk, bcol=bcol, hp=hp, wtk=wtk):
                            bk = 3
                            dstb = dstmap[dstk][sl]
                            P.mm_group([
                                (lambda t, c=c, n=n, sl=sl, bk=bk, m=m:
                                 t.matmul(pb[bk][:], WT[sl][:, m * 1024 + c * 128: m * 1024 + (c + 1) * 128],
                                          Hs(c, n * 512, 512), start=(c == 0), stop=(c == 7)))
                                for c in range(8)],
                                reads=[wtk] + [f"H{c}" for c in range(8)], writes=[f"pb{bk}"])
                            P.op("scalar", lambda s, n=n, bk=bk, dstb=dstb, bcol=bcol, hp=hp:
                                 s.activation(dstb[:, n * 512:(n + 1) * 512], pb[bk][:],
                                              AF.Identity, bias=biases[:, bcol * 8 + hp: bcol * 8 + hp + 1],
                                              scale=1.0),
                                 reads=[f"pb{bk}", "biases"], writes=[(dstk + str(sl)) if dstk != "vT" else "vT0"])
                        items.append(qkv_one)
                for g8 in range((SB + 7) // 8):
                    def vtok_g(g8=g8, sl=sl):
                        lo = g8 * 8
                        hi = min(lo + 8, SB)
                        for st in range(lo, hi):
                            P.op("tensor", lambda t, st=st, lo=lo, sl=sl: t.transpose(
                                pbb[1][:, (st - lo) * 128:(st - lo + 1) * 128],
                                vT[sl][:, st * 128:(st + 1) * 128], identb[:]),
                                reads=["vT0", "identb"], writes=["pbb1"])
                        P.op("scalar", lambda s, lo=lo, hi=hi, sl=sl: s.copy(
                            vtok[sl][:, lo * 128: hi * 128], pbb[1][:, 0:(hi - lo) * 128]),
                            reads=["pbb1"], writes=[f"vtok{sl}"])
                    items.append(vtok_g)
                return items

            def wo_items(hp):
                items = []
                sl = hp % 2
                wtk = f"WT{sl}"
                for oc in range(8):
                    def wo_oc(oc=oc, sl=sl, wtk=wtk, hp=hp):
                        for n in range(SC):
                            wb = 4 + (oc * SC + n) % 2
                            mm(pb[wb][:], WT[sl][:, 3072 + oc * 128: 3072 + (oc + 1) * 128],
                               opair[sl][:, n * 512:(n + 1) * 512],
                               reads=[wtk, f"opair{sl}"], writes=[f"pb{wb}"])
                            if hp == 0:
                                P.op("vector", lambda v, oc=oc, n=n, wb=wb: v.scalar_tensor_tensor(
                                    Xs(oc, n * 512, 512), pb[wb][:], biases[:, 24 + oc: 24 + oc + 1],
                                    Xs(oc, n * 512, 512), ALU.add, ALU.add),
                                    reads=[f"pb{wb}", "biases", f"X{oc}"], writes=[f"X{oc}"])
                            else:
                                P.op("vector", lambda v, oc=oc, n=n, wb=wb: v.tensor_tensor(
                                    Xs(oc, n * 512, 512), Xs(oc, n * 512, 512), pb[wb][:], ALU.add),
                                    reads=[f"pb{wb}", f"X{oc}"], writes=[f"X{oc}"])
                    items.append(wo_oc)
                return items

            def emit_scores(hp, qt):
                sl = hp % 2
                kts = [0] if qt == 0 else [0, qt - 1, qt]
                for h in range(2):
                    base = 64 * h
                    for j, kt in enumerate(kts):
                        mm(psc[:, h * 512 + j * 128: h * 512 + (j + 1) * 128],
                           qT[sl][base:base + 64, qt * 128:(qt + 1) * 128],
                           kT[sl][base:base + 64, kt * 128:(kt + 1) * 128],
                           reads=[f"qT{sl}", f"kT{sl}"], writes=[f"pb{h}"])

            def emit_softmax(hp, qt):
                W = 128 if qt == 0 else 384
                msk, mskk = (maskA, "maskA") if qt == 0 else (maskB, "maskB")
                par = qt % 2
                psc2 = psc[:].rearrange("p (h f) -> p h f", f=512)[:, :, 0:W]
                P.op("vector", lambda v, W=W, psc2=psc2: v.tensor_reduce(
                    nmx2[:], psc2, AX.X, ALU.max, negate=True),
                    reads=["pb0", "pb1"], writes=["nmx2"])
                P.op("vector", lambda v: v.tensor_scalar(
                    nm82[:], nmx2[:], 0.125, None, ALU.mult, ALU.bypass),
                    reads=["nmx2"], writes=["nm82"])
                mskw = msk[:].rearrange("p (h f) -> p h f", h=2)[:, :, 0:W]
                P.op("vector", lambda v, W=W, mskw=mskw, psc2=psc2: v.scalar_tensor_tensor(
                    s2b2[:].rearrange("p (h f) -> p h f", f=384)[:, :, 0:W], psc2, 0.125,
                    mskw, ALU.mult, ALU.add),
                    reads=["pb0", "pb1", mskk], writes=["s2b2"])
                for h in range(2):
                    P.op("scalar", lambda s, h=h, W=W, par=par: s.activation(
                        Pb[h][par][:, 0:W], s2b2[:, h * 384: h * 384 + W], AF.Exp,
                        bias=nm82[:, h: h + 1], scale=1.0, accum_out=zz[h][:]),
                        reads=["s2b2", "nm82"], writes=[f"Pb{h}_{par}", f"zz{h}"])
                P.op("vector", lambda v: v.reciprocal(rz2[:, 0:1], zz[0][:]),
                     reads=["zz0"], writes=["rz2"])
                P.op("vector", lambda v: v.reciprocal(rz2[:, 1:2], zz[1][:]),
                     reads=["zz1"], writes=["rz2"])
                for h in range(2):
                    P.op("vector", lambda v, h=h, W=W, par=par: v.tensor_scalar(
                        Pb[h][par][:, 0:W], Pb[h][par][:, 0:W], rz2[:, h: h + 1], None,
                        ALU.mult, ALU.bypass),
                        reads=[f"Pb{h}_{par}", "rz2"], writes=[f"Pb{h}_{par}"])

            def emit_ptpv(hp, qt):
                sl = hp % 2
                kts = [0] if qt == 0 else [0, qt - 1, qt]
                nk = len(kts)
                W = 128 * nk
                par = qt % 2
                for h in range(2):
                    base = 64 * h
                    for j in range(nk):
                        P.op("tensor", lambda t, h=h, j=j, par=par: t.transpose(
                            pbb[0][:, (h * 4 + j) * 128:(h * 4 + j + 1) * 128],
                            Pb[h][par][:, j * 128:(j + 1) * 128], identb[:]),
                            reads=[f"Pb{h}_{par}", "identb"], writes=["pbb0"])
                    P.op("scalar", lambda s, h=h, W=W: s.copy(PT[h][:, 0:W],
                                                              pbb[0][:, h * 512: h * 512 + W]),
                         reads=["pbb0"], writes=[f"PT{h}"])
                    P.mm_group([
                        (lambda t, h=h, j=j, kt=kt, qt=qt, base=base, nk=nk, sl=sl:
                         t.matmul(pb[2][base:base + 64, (qt % 4) * 128:(qt % 4 + 1) * 128],
                                  vtok[sl][:, kt * 128 + base: kt * 128 + base + 64],
                                  PT[h][:, j * 128:(j + 1) * 128],
                                  start=(j == 0), stop=(j == nk - 1)))
                        for j, kt in enumerate(kts)],
                        reads=[f"vtok{sl}", f"PT{h}"], writes=[f"pb2_{h}_{qt % 4}"])
                if qt % 4 == 3 or qt == SB - 1:
                    qlo = (qt // 4) * 4
                    nq = qt - qlo + 1
                    P.op("scalar", lambda s, qlo=qlo, nq=nq, sl=sl, qt=qt: s.copy(
                        opair[sl][:, qlo * 128:(qlo + nq) * 128], pb[2][:, 0:nq * 128]),
                        reads=[f"pb2_{h}_{q % 4}" for h in range(2) for q in range(qlo, qt + 1)],
                        writes=[f"opair{sl}"])

            emit_wdma(0)
            for it in filler_items(0):
                it()
            for hp in range(NP):
                if hp + 1 < NP:
                    emit_wdma(hp + 1)
                queue = []
                if hp >= 1:
                    queue += wo_items(hp - 1)
                if hp + 1 < NP:
                    queue += filler_items(hp + 1)
                prev = None
                popped = 0
                for qt in range(SB):
                    emit_scores(hp, qt)
                    emit_softmax(hp, qt)
                    if prev is not None:
                        emit_ptpv(hp, prev)
                    want = (qt + 1) * len(queue) // SB
                    while popped < want:
                        queue[popped]()
                        popped += 1
                    prev = qt
                emit_ptpv(hp, SB - 1)
                while popped < len(queue):
                    queue[popped]()
                    popped += 1
            for it in wo_items(NP - 1):
                it()
            if cfg.stage == "attn":
                emit_transposed_out(Xs, out_d, [f"X{c}" for c in range(8)])
                emit_transposed_out(Hs, dbg["h"], [f"H{c}" for c in range(8)], bf16=True)
                _finish(P, block)
                return nc

            ln_pass(False, "ln2")

            for sb in range(SB):
                P.mm_group([
                    (lambda t, c=c, sb=sb: t.matmul(
                        pb[2][:, 0:7], Hs(c, sb * 128, 128), wrT[:, c * 7:(c + 1) * 7],
                        start=(c == 0), stop=(c == 7)))
                    for c in range(8)],
                    reads=[f"H{c}" for c in range(8)] + ["wrT"], writes=["pb2"])
                P.op("vector", lambda v: v.tensor_tensor(lbuf[:], pb[2][:, 0:7], rbbc[:], ALU.add),
                     reads=["pb2", "rbbc"], writes=["lbuf"])
                P.op("vector", lambda v: v.tensor_reduce(r1[:], lbuf[:], AX.X, ALU.max, negate=True),
                     reads=["lbuf"], writes=["r1"])
                P.op("vector", lambda v: v.tensor_scalar(tbuf[:], lbuf[:], r1[:], None, ALU.add, ALU.bypass),
                     reads=["lbuf", "r1"], writes=["tbuf"])
                P.op("vector", lambda v: v.tensor_scalar(mbuf[:], tbuf[:], 0.0, None, ALU.is_ge, ALU.bypass),
                     reads=["tbuf"], writes=["mbuf"])
                P.op("vector", lambda v: v.scalar_tensor_tensor(l2buf[:], mbuf[:], NEG, tbuf[:],
                                                                ALU.mult, ALU.add),
                     reads=["mbuf", "tbuf"], writes=["l2buf"])
                P.op("vector", lambda v: v.tensor_reduce(r2[:], l2buf[:], AX.X, ALU.max, negate=True),
                     reads=["l2buf"], writes=["r2"])
                P.op("vector", lambda v: v.tensor_scalar(mbuf[:], tbuf[:], r2[:], 0.0, ALU.add, ALU.is_ge),
                     reads=["tbuf", "r2"], writes=["mbuf"])
                P.op("scalar", lambda s: s.activation(ebuf[:], tbuf[:], AF.Exp, bias=0.0, scale=1.0),
                     reads=["tbuf"], writes=["ebuf"])
                P.op("vector", lambda v: v.tensor_tensor(ebuf[:], ebuf[:], mbuf[:], ALU.mult),
                     reads=["ebuf", "mbuf"], writes=["ebuf"])
                P.op("vector", lambda v: v.tensor_reduce(r1[:], ebuf[:], AX.X, ALU.add),
                     reads=["ebuf"], writes=["r1"])
                P.op("vector", lambda v: v.reciprocal(r2[:], r1[:]), reads=["r1"], writes=["r2"])
                P.op("vector", lambda v, sb=sb: v.tensor_scalar(
                    wtok[:, sb * 7:(sb + 1) * 7], ebuf[:], r2[:], None, ALU.mult, ALU.bypass),
                    reads=["ebuf", "r2"], writes=["wtok"])
            for sb in range(SB):
                bk = (sb // 4) % 2
                P.op("tensor", lambda t, sb=sb, bk=bk: t.transpose(
                    pb[bk][0:7, (sb % 4) * 128:(sb % 4 + 1) * 128],
                    wtok[:, sb * 7:(sb + 1) * 7], identf[:]),
                    reads=["wtok", "identf"], writes=[f"pb{bk}"])
                if sb % 4 == 3 or sb == SB - 1:
                    lo = (sb // 4) * 4
                    nqq = (sb - lo + 1) * 128
                    P.op("vector", lambda v, lo=lo, nqq=nqq, bk=bk: v.tensor_copy(
                        wrows[0:7, lo * 128: lo * 128 + nqq], pb[bk][0:7, 0:nqq]),
                        reads=[f"pb{bk}"], writes=["wrows"])

            if cfg.stage == "router":
                P.dma("scalar", "dout", lambda s: s.dma_start(dbg["w"], wtok[:]), reads=["wtok"])
                emit_transposed_out(Hs, dbg["h"], [f"H{c}" for c in range(8)], bf16=True)
                emit_transposed_out(Xs, out_d, [f"X{c}" for c in range(8)])
                _finish(P, block)
                return nc

            for hc in range(8):
                P.dma("scalar", "dxw", lambda s, hc=hc: s.dma_start(
                    x1_d[hc], Xs(hc, 0, S)),
                    reads=[f"X{hc}"], writes=[f"x1_{hc}_{n}" for n in range(SC)])
        P.fence()

        with ExitStack() as es_f:
            Tf = lambda nm, sh, dt=F32: es_f.enter_context(nc.sbuf_tensor(nm, sh, dt))
            tbig = Tf("tbig", [128, NI * S], BF16)
            es_f1 = es_f.enter_context(ExitStack())
            Tf1 = lambda nm, sh, dt=F32: es_f1.enter_context(nc.sbuf_tensor(nm, sh, dt))
            sgstage = [Tf1(f"sgstage{s}", [128, 2048], BF16) for s in range(2)]
            wTf = [Tf1(f"wTf{s}", [128, 2048], BF16) for s in range(2)]
            sdstage = [Tf1(f"sdstage{s}", [128, 2816], BF16) for s in range(2)]
            sdT = [Tf1(f"sdT{s}", [128, 2816], BF16) for s in range(2)]
            sil = Tf1("sil", [128, 512], F32)
            wbc = Tf1("wbc", [128, S], F32)
            rin = [Tf1(f"rin{s}", [128, 512], F32) for s in range(2)]
            rout = [Tf1(f"rout{s}", [128, 512], F32) for s in range(2)]

            n_exp = 1 + cfg.NRE
            for e in range(n_exp):
                if e == 0:
                    g_src, u_src, d_src = sg_d, su_d, sd_d
                else:
                    g_src, u_src, d_src = rg_d[e - 1], ru_d[e - 1], rd_d[e - 1]
                    P.dma("gpsimd", "dwb", lambda g, e=e: g.dma_start(
                        wbc[0:1, :], wrows[e - 1:e, :]),
                        reads=["wrows"], writes=["wbc"])
                    for n in range(SC):
                        mm(pb[5][:], onesrow[:], wbc[0:1, n * 512:(n + 1) * 512],
                           reads=["wbc", "onesrow"], writes=["pb5"])
                        P.op("vector", lambda v, n=n: v.tensor_copy(
                            wbc[:, n * 512:(n + 1) * 512], pb[5][:]),
                            reads=["pb5"], writes=["wbc", "wbc"])
                def emit_sg_dma(i, g_src=g_src, u_src=u_src):
                    sl = i % 2
                    sgk = f"sgstage{sl}"
                    P.dma("gpsimd", f"dsg{sl}", lambda g, i=i, sl=sl, g_src=g_src: g.dma_start(
                        sgstage[sl][:, 0:1024], g_src[i * 128:(i + 1) * 128, :]), writes=[sgk])
                    P.dma("gpsimd", f"dsg{sl}", lambda g, i=i, sl=sl, u_src=u_src: g.dma_start(
                        sgstage[sl][:, 1024:2048], u_src[i * 128:(i + 1) * 128, :]), writes=[sgk])

                def emit_sg_transpose(i):
                    sl = i % 2
                    sgk = f"sgstage{sl}"
                    wtk = f"wTf{sl}"
                    for m in range(2):
                        for c in range(8):
                            P.op("tensor", lambda t, m=m, c=c, sl=sl: t.transpose(
                                pbb[m][:, c * 128:(c + 1) * 128],
                                sgstage[sl][:, m * 1024 + c * 128: m * 1024 + (c + 1) * 128],
                                identb[:]),
                                reads=[sgk, "identb"], writes=[f"pbb{m}"])
                        P.op("scalar", lambda s, m=m, sl=sl: s.copy(
                            wTf[sl][:, m * 1024:(m + 1) * 1024], pbb[m][:]),
                            reads=[f"pbb{m}"], writes=[wtk])

                def emit_gateup(i):
                    sl = i % 2
                    wtk = f"wTf{sl}"
                    for n in range(SC):
                        gb, ub = n % 2, 2 + n % 2
                        P.mm_group([
                            (lambda t, c=c, n=n, sl=sl, gb=gb: t.matmul(
                                pb[gb][:], wTf[sl][:, c * 128:(c + 1) * 128],
                                Hs(c, n * 512, 512), start=(c == 0), stop=(c == 7)))
                            for c in range(8)],
                            reads=[wtk] + [f"H{c}" for c in range(8)], writes=[f"pb{gb}"])
                        P.mm_group([
                            (lambda t, c=c, n=n, sl=sl, ub=ub: t.matmul(
                                pb[ub][:], wTf[sl][:, 1024 + c * 128: 1024 + (c + 1) * 128],
                                Hs(c, n * 512, 512), start=(c == 0), stop=(c == 7)))
                            for c in range(8)],
                            reads=[wtk] + [f"H{c}" for c in range(8)], writes=[f"pb{ub}"])
                        P.op("scalar", lambda s, gb=gb: s.activation(sil[:], pb[gb][:], AF.Silu,
                                                                     bias=0.0, scale=1.0),
                             reads=[f"pb{gb}"], writes=["sil"])
                        P.op("vector", lambda v, i=i, n=n, ub=ub: v.tensor_tensor(
                            tbig[:, i * S + n * 512: i * S + (n + 1) * 512], sil[:], pb[ub][:],
                            ALU.mult),
                            reads=["sil", f"pb{ub}"], writes=[f"t{i}"])

                emit_sg_dma(0)
                emit_sg_dma(1)
                for i in range(NI):
                    emit_sg_transpose(i)
                    if i + 2 < NI:
                        emit_sg_dma(i + 2)
                    if i >= 1:
                        emit_gateup(i - 1)
                emit_gateup(NI - 1)
                def emit_sd_dma(hc, d_src=d_src):
                    sl = hc % 2
                    P.dma("gpsimd", f"dsd{sl}", lambda g, hc=hc, sl=sl, d_src=d_src: g.dma_start(
                        sdstage[sl][:], d_src[hc * 128:(hc + 1) * 128, :]),
                        writes=[f"sdstage{sl}"])

                def emit_sd_transpose(hc):
                    sl = hc % 2
                    sdk, sdtk = f"sdstage{sl}", f"sdT{sl}"
                    for grp in range(3):
                        lo = grp * 8
                        hi = min(lo + 8, NI)
                        bkk = grp % 2
                        for i in range(lo, hi):
                            P.op("tensor", lambda t, i=i, lo=lo, sl=sl, bkk=bkk: t.transpose(
                                pbb[bkk][:, (i - lo) * 128:(i - lo + 1) * 128],
                                sdstage[sl][:, i * 128:(i + 1) * 128], identb[:]),
                                reads=[sdk, "identb"], writes=[f"pbb{bkk}"])
                        P.op("scalar", lambda s, lo=lo, hi=hi, sl=sl, bkk=bkk: s.copy(
                            sdT[sl][:, lo * 128: hi * 128], pbb[bkk][:, 0:(hi - lo) * 128]),
                            reads=[f"pbb{bkk}"], writes=[sdtk])

                def emit_down(hc):
                    sl = hc % 2
                    sdtk = f"sdT{sl}"
                    for n in range(SC):
                        dbk = [4, 5, 2, 3][n % 4]
                        rsl = (hc * SC + n) % 2
                        P.dma("sync", f"dri{rsl}", lambda s, hc=hc, n=n, rsl=rsl: s.dma_start(
                            rin[rsl][:], x1_d[hc][:, n * 512:(n + 1) * 512]),
                            reads=[f"x1_{hc}_{n}"], writes=[f"rin{rsl}"])
                        P.mm_group([
                            (lambda t, i=i, n=n, sl=sl, dbk=dbk: t.matmul(
                                pb[dbk][:], sdT[sl][:, i * 128:(i + 1) * 128],
                                tbig[:, i * S + n * 512: i * S + (n + 1) * 512],
                                start=(i == 0), stop=(i == NI - 1)))
                            for i in range(NI)],
                            reads=[sdtk] + [f"t{i}" for i in range(NI)], writes=[f"pb{dbk}"])
                        if e == 0:
                            P.op("vector", lambda v, n=n, dbk=dbk, rsl=rsl: v.tensor_tensor(
                                rout[rsl][:], rin[rsl][:], pb[dbk][:], ALU.add),
                                reads=[f"rin{rsl}", f"pb{dbk}"], writes=[f"rout{rsl}"])
                        else:
                            P.op("vector", lambda v, n=n, dbk=dbk, rsl=rsl: v.tensor_tensor(
                                rout[rsl][:], pb[dbk][:], wbc[:, n * 512:(n + 1) * 512], ALU.mult),
                                reads=[f"pb{dbk}", "wbc"], writes=[f"rout{rsl}"])
                            P.op("vector", lambda v, rsl=rsl: v.tensor_tensor(
                                rout[rsl][:], rout[rsl][:], rin[rsl][:], ALU.add),
                                reads=[f"rout{rsl}", f"rin{rsl}"], writes=[f"rout{rsl}"])
                        P.dma("scalar", f"dro{rsl}", lambda s, hc=hc, n=n, rsl=rsl: s.dma_start(
                            x1_d[hc][:, n * 512:(n + 1) * 512], rout[rsl][:]),
                            reads=[f"rout{rsl}"], writes=[f"x1_{hc}_{n}"])

                emit_sd_dma(0)
                emit_sd_dma(1)
                for hc in range(8):
                    emit_sd_transpose(hc)
                    if hc + 2 < 8:
                        emit_sd_dma(hc + 2)
                    if hc >= 1:
                        emit_down(hc - 1)
                emit_down(7)

            es_f1.close()
            P.fence()
            rinbig = [Tf(f"rinbig{s}", [128, 1024], F32) for s in range(2)]
            routbig = [Tf(f"routbig{s}", [128, 1024], F32) for s in range(2)]
            for sb in range(SB):
                sl = sb % 2
                src_fin = x1_d.rearrange("c p f -> p c f")[:, :, sb * 128:(sb + 1) * 128]
                P.dma("sync", f"dfi{sl}", lambda s, sl=sl, src_fin=src_fin: s.dma_start(
                    rinbig[sl][:].rearrange("p (c f) -> p c f", f=128), src_fin),
                    reads=[f"x1_{hc}_{n}" for hc in range(8) for n in range(SC)],
                    writes=[f"rinbig{sl}"])
                for hc in range(8):
                    P.op("tensor", lambda t, hc=hc, sl=sl: t.transpose(
                        pb[hc // 4][:, (hc % 4) * 128:(hc % 4 + 1) * 128],
                        rinbig[sl][:, hc * 128:(hc + 1) * 128], identf[:]),
                        reads=[f"rinbig{sl}", "identf"], writes=[f"pb{hc // 4}"])
                for g in range(2):
                    P.op("vector", lambda v, g=g, sl=sl: v.tensor_copy(
                        routbig[sl][:, g * 512:(g + 1) * 512], pb[g][:]),
                        reads=[f"pb{g}"], writes=[f"routbig{sl}"])
                P.dma("scalar", "dout", lambda s, sb=sb, sl=sl: s.dma_start(
                    out_d[sb * 128:(sb + 1) * 128, :], routbig[sl][:]),
                    reads=[f"routbig{sl}"])

            _finish(P, block)
    return nc


def _finish(P, block):
    n_out = P.dma_count.get("dout", 0)
    waits = {"dout": n_out}
    for k in ("dro0", "dro1"):
        if P.dma_count.get(k):
            waits[k] = P.dma_count[k]
    P.ops["scalar"].append((dict(waits), lambda s: s.nop(), None, 0))
    P.finalize(block)


import numpy as np
from concourse.bass_utils import run_bass_kernel_spmd

_NC = 8
_S = 2048


def _consts():
    i = np.arange(128)
    maskA = np.where(((i[None, :] <= i[:, None]) & (i[:, None] - i[None, :] <= 64))
                     | (i[None, :] == 0), 0.0, -1e30).astype(np.float32)
    mB = np.full((128, 384), -1e30, np.float32)
    mB[:, 0] = 0.0
    jj = np.arange(128)
    mB[:, 128:256] = np.where(jj[None, :] >= i[:, None] + 64, 0.0, -1e30)
    mB[:, 256:384] = np.where((jj[None, :] <= i[:, None]) & (i[:, None] - jj[None, :] <= 64),
                              0.0, -1e30)
    return {
        "identf": np.eye(128, dtype=np.float32),
        "identb": np.eye(128, dtype=np.float32),
        "maskA": np.concatenate([maskA, maskA], axis=1),
        "maskB": np.concatenate([mB, mB], axis=1),
        "onesrow": np.ones((1, 128), np.float32),
    }


_nc_cache = {}


def _get_nc():
    if "nc" not in _nc_cache:
        _nc_cache["nc"] = build(Cfg(S=_S, NRE=7, stage="full", NP=8))
    return _nc_cache["nc"]


def kernel(L0_attn_hidden_states, L0_attn_Wq, L0_attn_bq, L0_attn_Wk, L0_attn_bk,
           L0_attn_Wv, L0_attn_bv, L0_attn_Wo, L0_attn_bo, L0_ffn_sg, L0_ffn_su,
           L0_ffn_sd, L0_ffn_rg, L0_ffn_ru, L0_ffn_rd, L0_ffn_wr, L0_ffn_rb):
    x = np.ascontiguousarray(np.asarray(L0_attn_hidden_states, dtype=np.float32))
    c = lambda a: np.ascontiguousarray(np.asarray(a, dtype=np.float32))
    shared = {
        "wq": c(L0_attn_Wq), "wk": c(L0_attn_Wk), "wv": c(L0_attn_Wv), "wo": c(L0_attn_Wo),
        "bq": c(L0_attn_bq), "bk": c(L0_attn_bk), "bv": c(L0_attn_bv), "bo": c(L0_attn_bo),
        "sg": c(L0_ffn_sg), "su": c(L0_ffn_su), "sd": c(L0_ffn_sd),
        "rg": c(L0_ffn_rg), "ru": c(L0_ffn_ru), "rd": c(L0_ffn_rd),
        "wr": c(L0_ffn_wr), "rb": c(L0_ffn_rb),
    }
    shared.update(_consts())
    nc = _get_nc()
    in_maps = [dict(shared, x=np.ascontiguousarray(x[b])) for b in range(_NC)]
    res = run_bass_kernel_spmd(nc, in_maps, core_ids=list(range(_NC)))
    return np.stack([res.results[b]["out"] for b in range(_NC)], axis=0)


# version 07
# speedup vs baseline: 1.3945x; 1.3945x over previous
from contextlib import ExitStack
from collections import defaultdict
import concourse.bass as bass
import concourse.mybir as mybir

from collections import defaultdict

ENGINES = ("tensor", "vector", "scalar", "gpsimd", "sync")


class Prog:
    def __init__(self, nc):
        self.nc = nc
        self.ops = {e: [] for e in ENGINES}
        self.eng_count = {e: 0 for e in ENGINES}
        self.dma_count = defaultdict(int)
        self.res = {}
        self.high_water = {e: defaultdict(int) for e in ENGINES}
        self.sem_handles = {}
        self._es = None

    def init_sems(self, es, dma_keys):
        for e in ENGINES:
            self.sem_handles["@" + e] = es.enter_context(self.nc.semaphore("sem_" + e))
        for k in dma_keys:
            self.sem_handles[k] = es.enter_context(self.nc.semaphore("sem_" + k))

    def _want(self, waits, token):
        if token is None:
            return
        sem, val = token
        if waits.get(sem, 0) < val:
            waits[sem] = val

    def _emit(self, engine, fn, reads, writes, inc_sem, inc_amt, accum_key=None,
              extra_waits=()):
        waits = {}
        pf = getattr(self, "_pending_fence", None)
        if pf and engine in pf:
            self._want(waits, pf.pop(engine))
        for tok in extra_waits:
            self._want(waits, tok)
        for r in reads:
            e = self.res.get(r)
            if e:
                self._want(waits, e["w"])
        for w in writes:
            e = self.res.get(w)
            if e:
                if accum_key is None or e.get("acc") != accum_key:
                    self._want(waits, e["w"])
                    for t in e["r"]:
                        self._want(waits, t)
        hw = self.high_water[engine]
        final_waits = {}
        for sem, val in waits.items():
            if sem == "@" + engine:
                pass
            if hw[sem] < val:
                final_waits[sem] = val
                hw[sem] = val
        if inc_sem is None:
            inc_sem = "@" + engine
        if inc_sem.startswith("@"):
            self.eng_count[engine] += inc_amt
            token = (inc_sem, self.eng_count[engine])
        else:
            self.dma_count[inc_sem] += inc_amt
            token = (inc_sem, self.dma_count[inc_sem])
        for r in reads:
            self.res.setdefault(r, {"w": None, "r": [], "acc": None})["r"].append(token)
        for w in writes:
            e = self.res.setdefault(w, {"w": None, "r": [], "acc": None})
            e["w"] = token
            e["r"] = []
            e["acc"] = accum_key
        self.ops[engine].append((final_waits, fn, inc_sem if not inc_sem.startswith("@") else "@" + engine, inc_amt))
        return token

    def op(self, engine, fn, reads=(), writes=()):
        return self._emit(engine, fn, reads, writes, None, 1)

    def mm_group(self, fns, reads=(), writes=(), key=None):
        key = key or object()
        tok = None
        for i, fn in enumerate(fns):
            last = i == len(fns) - 1
            if last:
                tok = self._emit("tensor", fn, reads, writes, None, 1, accum_key=key)
            else:
                waits = {}
                pf = getattr(self, "_pending_fence", None)
                if pf and "tensor" in pf:
                    self._want(waits, pf.pop("tensor"))
                for r in reads:
                    e = self.res.get(r)
                    if e:
                        self._want(waits, e["w"])
                for w in writes:
                    e = self.res.get(w)
                    if e and e.get("acc") != key:
                        self._want(waits, e["w"])
                        for t in e["r"]:
                            self._want(waits, t)
                        e["acc"] = key
                hw = self.high_water["tensor"]
                fw = {}
                for sem, val in waits.items():
                    if hw[sem] < val:
                        fw[sem] = val
                        hw[sem] = val
                for w in writes:
                    e = self.res.setdefault(w, {"w": None, "r": [], "acc": key})
                    e["acc"] = key
                self.ops["tensor"].append((fw, fn, None, 0))
        return tok

    def dma(self, engine, sem_key, fn, reads=(), writes=()):
        prev = self.dma_count.get(sem_key, 0)
        extra = ((sem_key, prev),) if prev else ()
        return self._emit(engine, fn, reads, writes, sem_key, 16,
                          extra_waits=extra)

    def read_barrier(self, engine, resources):
        raise NotImplementedError

    def barrier_op(self, engine, fn, resources):
        return self.op(engine, fn, reads=tuple(resources), writes=())

    def fence(self):
        all_waits = {}
        for e in ENGINES:
            if self.eng_count[e]:
                all_waits["@" + e] = self.eng_count[e]
        for k, v in self.dma_count.items():
            if v:
                all_waits[k] = v
        self._pending_fence = {}
        for e in ENGINES:
            waits = {s: v for s, v in all_waits.items() if s != "@" + e}
            self.eng_count[e] += 1
            tok = ("@" + e, self.eng_count[e])
            self.ops[e].append((waits, lambda eng: eng.drain(), "@" + e, 1))
            self._pending_fence[e] = tok
            hw = self.high_water[e]
            for s, v in waits.items():
                if hw[s] < v:
                    hw[s] = v
        self.res = {}

    def finalize(self, block):
        sems = self.sem_handles
        for engine in ENGINES:
            stream = self.ops[engine]
            if not stream:
                continue

            def make_body(stream=stream, engine=engine):
                def body(eng):
                    for waits, fn, inc_sem, inc_amt in stream:
                        for sem, val in waits.items():
                            eng.wait_ge(sems[sem], val)
                        inst = fn(eng)
                        if inc_amt and inst is not None:
                            inst.then_inc(sems[inc_sem], inc_amt)
                return body

            getattr(block, engine)(make_body())

    def pending_counts(self):
        return dict(self.eng_count), dict(self.dma_count)


F32 = mybir.dt.float32
BF16 = mybir.dt.bfloat16
AF = mybir.ActivationFunctionType
ALU = mybir.AluOpType
AX = mybir.AxisListType

NEG = -1e30
EPS = 1e-5


class Cfg:
    def __init__(self, S=2048, NRE=7, stage="full", NP=8):
        self.S = S
        self.NRE = NRE
        self.stage = stage
        self.NP = NP


def build(cfg: Cfg):
    S = cfg.S
    SB = S // 128
    SC = S // 512
    NI = 22
    nc = bass.Bass("TRN2", target_bir_lowering=False)

    x_d = nc.dram_tensor("x", [S, 1024], F32, kind="ExternalInput").ap()
    wq_d = nc.dram_tensor("wq", [1024, 1024], F32, kind="ExternalInput").ap()
    wk_d = nc.dram_tensor("wk", [1024, 1024], F32, kind="ExternalInput").ap()
    wv_d = nc.dram_tensor("wv", [1024, 1024], F32, kind="ExternalInput").ap()
    wo_d = nc.dram_tensor("wo", [1024, 1024], F32, kind="ExternalInput").ap()
    bq_d = nc.dram_tensor("bq", [1024], F32, kind="ExternalInput").ap()
    bk_d = nc.dram_tensor("bk", [1024], F32, kind="ExternalInput").ap()
    bv_d = nc.dram_tensor("bv", [1024], F32, kind="ExternalInput").ap()
    bo_d = nc.dram_tensor("bo", [1024], F32, kind="ExternalInput").ap()
    sg_d = nc.dram_tensor("sg", [2816, 1024], F32, kind="ExternalInput").ap()
    su_d = nc.dram_tensor("su", [2816, 1024], F32, kind="ExternalInput").ap()
    sd_d = nc.dram_tensor("sd", [1024, 2816], F32, kind="ExternalInput").ap()
    rg_d = nc.dram_tensor("rg", [7, 2816, 1024], F32, kind="ExternalInput").ap()
    ru_d = nc.dram_tensor("ru", [7, 2816, 1024], F32, kind="ExternalInput").ap()
    rd_d = nc.dram_tensor("rd", [7, 1024, 2816], F32, kind="ExternalInput").ap()
    wr_d = nc.dram_tensor("wr", [7, 1024], F32, kind="ExternalInput").ap()
    rb_d = nc.dram_tensor("rb", [7], F32, kind="ExternalInput").ap()
    identf_d = nc.dram_tensor("identf", [128, 128], F32, kind="ExternalInput").ap()
    identb_d = nc.dram_tensor("identb", [128, 128], F32, kind="ExternalInput").ap()
    maskA_d = nc.dram_tensor("maskA", [128, 256], F32, kind="ExternalInput").ap()
    maskB_d = nc.dram_tensor("maskB", [128, 768], F32, kind="ExternalInput").ap()
    onesrow_d = nc.dram_tensor("onesrow", [1, 128], F32, kind="ExternalInput").ap()

    out_d = nc.dram_tensor("out", [S, 1024], F32, kind="ExternalOutput").ap()
    dbg = {}
    if cfg.stage in ("ln1", "attn", "router"):
        dbg["h"] = nc.dram_tensor("hdbg", [S, 1024], F32, kind="ExternalOutput").ap()
    if cfg.stage == "router":
        dbg["w"] = nc.dram_tensor("wdbg", [128, 7 * SB], F32, kind="ExternalOutput").ap()
    x1_d = nc.dram_tensor("x1d", [8, 128, S], F32, kind="Internal").ap()

    with ExitStack() as es:
        Tz = lambda nm, sh, dt=F32: es.enter_context(nc.sbuf_tensor(nm, sh, dt))
        H = Tz("H", [128, 8 * S], BF16)
        identf = Tz("identf_s", [128, 128], F32)
        identb = Tz("identb_s", [128, 128], BF16)
        onesrow = Tz("onesrow_s", [1, 128], F32)
        biases = Tz("biases", [128, 32], F32)
        brow = Tz("brow", [1, 1024], F32)
        wrstage = Tz("wrstage", [7, 1024], BF16)
        wrT = Tz("wrT", [128, 56], BF16)
        rbrow = Tz("rbrow", [1, 7], F32)
        rbbc = Tz("rbbc", [128, 7], F32)
        wtok = Tz("wtok", [128, 7 * SB], F32)
        wrows = Tz("wrows", [7, S], F32)
        lbuf = Tz("lbuf", [128, 7], F32)
        tbuf = Tz("tbuf", [128, 7], F32)
        mbuf = Tz("mbuf", [128, 7], F32)
        l2buf = Tz("l2buf", [128, 7], F32)
        ebuf = Tz("ebuf", [128, 7], F32)
        r1 = Tz("r1", [128, 1], F32)
        eps_col = Tz("eps_col", [128, 1], F32)
        r2 = Tz("r2", [128, 1], F32)
        xin = [Tz(f"xin{s}", [128, 1024], F32) for s in range(2)]
        st6 = [Tz(f"st6{s}", [128, 12], F32) for s in range(2)]
        mv = [Tz(f"mv{s}", [128, 2], F32) for s in range(2)]
        rv = [Tz(f"rv{s}", [128, 1], F32) for s in range(2)]
        htok = [Tz(f"htok{s}", [128, 1024], BF16) for s in range(2)]

        psc = es.enter_context(nc.psum_tensor("psc", [128, 1024], F32))
        pb = [psc[:, 0:512], psc[:, 512:1024]] + [
            es.enter_context(nc.psum_tensor(f"pb{i}", [128, 512], F32)) for i in range(2, 6)]
        pbb = [es.enter_context(nc.psum_tensor(f"pbb{i}", [128, 1024], BF16)) for i in range(2)]

        prog = Prog(nc)
        dma_keys = ["dx0", "dx1", "dwt0", "dwt1", "dsg0", "dsg1", "dsd0", "dsd1",
                    "dcst", "dwb", "dri0", "dri1", "dro0", "dro1", "dxw", "dout",
                    "dfi0", "dfi1"]
        prog.init_sems(es, dma_keys)
        block = es.enter_context(nc.Block())

        P = prog

        def mm(psum_ap, lhsT_ap, rhs_ap, reads, writes, start=True, stop=True):
            return P.op("tensor", lambda t: t.matmul(psum_ap, lhsT_ap, rhs_ap,
                                                     start=start, stop=stop),
                        reads=reads, writes=writes)

        def Hs(hc, lo, n):
            return H[:, hc * S + lo: hc * S + lo + n]

        P.op("vector", lambda v: v.memset(eps_col[:], EPS), writes=["eps_col"])
        P.dma("sync", "dcst", lambda s: s.dma_start(identf[:], identf_d), writes=["identf"])
        P.dma("sync", "dcst", lambda s: s.dma_start(onesrow[:], onesrow_d), writes=["onesrow"])
        P.dma("sync", "dcst", lambda s: s.dma_start(rbrow[:], rb_d[None, :]), writes=["rbrow"])
        P.dma("gpsimd", "dwb", lambda g: g.dma_start(identb[:], identb_d), writes=["identb"])
        P.dma("gpsimd", "dwb", lambda g: g.dma_start(wrstage[:], wr_d), writes=["wrstage"])
        for bi, b_d in enumerate([bq_d, bk_d, bv_d, bo_d]):
            P.dma("sync", "dcst", lambda s, b_d=b_d: s.dma_start(brow[:], b_d[None, :]),
                  reads=["biasesp"], writes=["brow"])
            for c in range(8):
                mm(pb[5][:, bi * 8 + c: bi * 8 + c + 1], brow[0:1, c * 128:(c + 1) * 128],
                   onesrow[0:1, 0:1], reads=["brow", "onesrow"], writes=["pb5"])
            P.op("vector", lambda v, bi=bi: v.tensor_copy(biases[:, bi * 8:(bi + 1) * 8],
                                                          pb[5][:, bi * 8:(bi + 1) * 8]),
                 reads=["pb5"], writes=["biases", "biasesp"])
        mm(pb[5][:, 40:47], onesrow[:], rbrow[0:1, 0:7], reads=["onesrow", "rbrow"], writes=["pb5"])
        P.op("vector", lambda v: v.tensor_copy(rbbc[:], pb[5][:, 40:47]), reads=["pb5"], writes=["rbbc"])
        for c in range(8):
            mm(pb[5][:, 64 + c * 8: 64 + c * 8 + 7], wrstage[0:7, c * 128:(c + 1) * 128],
               identb[0:7, 0:7], reads=["wrstage", "identb"], writes=["pb5"])
        P.op("vector", lambda v: v.tensor_copy(
            wrT[:].rearrange("p (c f) -> p c f", f=7),
            pb[5][:, 64:128].rearrange("p (c f) -> p c f", f=8)[:, :, 0:7]),
            reads=["pb5"], writes=["wrT"])

        with ExitStack() as es_a:
            Ta = lambda nm, sh, dt=F32: es_a.enter_context(nc.sbuf_tensor(nm, sh, dt))
            X = Ta("X", [128, 8 * S], F32)
            maskA = Ta("maskA_s", [128, 256], F32)
            maskB = Ta("maskB_s", [128, 768], F32)
            P.dma("sync", "dcst", lambda s: s.dma_start(maskA[:], maskA_d), writes=["maskA"])
            P.dma("sync", "dcst", lambda s: s.dma_start(maskB[:], maskB_d), writes=["maskB"])
            wstage = [Ta(f"wstage{s}", [128, 4096], BF16) for s in range(2)]
            WT = [Ta(f"WT{s}", [128, 4096], BF16) for s in range(2)]
            qT = [Ta(f"qT{s}", [128, S], BF16) for s in range(2)]
            kT = [Ta(f"kT{s}", [128, S], BF16) for s in range(2)]
            vT = [Ta("vT0", [128, S], BF16)] * 2
            vtok = [Ta(f"vtok{s}", [128, S], BF16) for s in range(2)]
            opair = [Ta(f"opair{s}", [128, S], BF16) for s in range(2)]
            s2b2 = Ta("s2b2", [128, 768], F32)
            nmx2 = Ta("nmx2", [128, 2], F32)
            nm82 = Ta("nm82", [128, 2], F32)
            rz2 = Ta("rz2", [128, 2], F32)
            Pb = [[Ta(f"Pb{h}_{s}", [128, 384], BF16) for s in range(2)] for h in range(2)]
            PT = [Ta(f"PT{h}", [128, 384], BF16) for h in range(2)]
            zz = [Ta(f"zz{h}", [128, 1], F32) for h in range(2)]

            def Xs(hc, lo, n):
                return X[:, hc * S + lo: hc * S + lo + n]

            def ln_pass(src_is_dram, dest_prefix):
                for sb in range(SB):
                    sl = sb % 2
                    xk = f"xin{sl}"
                    if src_is_dram:
                        P.dma("sync", f"dx{sl}",
                              lambda s, sb=sb, sl=sl: s.dma_start(
                                  xin[sl][:], x_d[sb * 128:(sb + 1) * 128, :]),
                              writes=[xk])
                    else:
                        for g in range(2):
                            P.mm_group([
                                (lambda t, hc=hc, sb=sb: t.transpose(
                                    pb[hc // 4][:, (hc % 4) * 128:(hc % 4 + 1) * 128],
                                    Xs(hc, sb * 128, 128), identf[:]))
                                for hc in range(4 * g, 4 * g + 4)],
                                reads=[f"X{hc}" for hc in range(4 * g, 4 * g + 4)] + ["identf"],
                                writes=[f"pb{g}"])
                        for g in range(2):
                            P.op("vector", lambda v, g=g, sl=sl: v.tensor_copy(
                                xin[sl][:, g * 512:(g + 1) * 512], pb[g][:]),
                                reads=[f"pb{g}"], writes=[xk])
                    P.op("vector", lambda v, sl=sl: v.bn_stats(st6[sl][:, 0:6], xin[sl][:, 0:512]),
                         reads=[xk], writes=[f"st6{sl}"])
                    P.op("vector", lambda v, sl=sl: v.bn_stats(st6[sl][:, 6:12], xin[sl][:, 512:1024]),
                         reads=[xk], writes=[f"st6{sl}"])
                    P.op("vector", lambda v, sl=sl: v.bn_aggr(mv[sl][:], st6[sl][:]),
                         reads=[f"st6{sl}"], writes=[f"mv{sl}"])
                    P.op("scalar", lambda s, sl=sl: s.activation(rv[sl][:], mv[sl][:, 1:2],
                                                                 AF.Ln, bias=eps_col[:], scale=1.0),
                         reads=[f"mv{sl}", "eps_col"], writes=[f"rv{sl}"])
                    P.op("scalar", lambda s, sl=sl: s.activation(rv[sl][:], rv[sl][:],
                                                                 AF.Exp, bias=0.0, scale=-0.5),
                         reads=[f"rv{sl}"], writes=[f"rv{sl}"])
                    P.op("vector", lambda v, sl=sl: v.tensor_scalar(
                        htok[sl][:], xin[sl][:], mv[sl][:, 0:1], rv[sl][:],
                        ALU.subtract, ALU.mult),
                        reads=[xk, f"mv{sl}", f"rv{sl}"], writes=[f"htok{sl}"])
                    if src_is_dram:
                        for g in range(2):
                            P.mm_group([
                                (lambda t, hc=hc, sl=sl: t.transpose(
                                    pb[hc // 4][:, (hc % 4) * 128:(hc % 4 + 1) * 128],
                                    xin[sl][:, hc * 128:(hc + 1) * 128], identf[:]))
                                for hc in range(4 * g, 4 * g + 4)],
                                reads=[xk, "identf"], writes=[f"pb{g}"])
                        for g in range(2):
                            dst = X[:, 0:4 * S].rearrange("p (c f) -> p c f", f=S)[
                                :, :, sb * 128:(sb + 1) * 128] if g == 0 else \
                                X[:, 4 * S:8 * S].rearrange("p (c f) -> p c f", f=S)[
                                :, :, sb * 128:(sb + 1) * 128]
                            P.op("vector", lambda v, g=g, dst=dst: v.tensor_copy(
                                dst, pb[g][:].rearrange("p (c f) -> p c f", f=128)),
                                reads=[f"pb{g}"],
                                writes=[f"X{c}" for c in range(4 * g, 4 * g + 4)])
                    P.mm_group([
                        (lambda t, hc=hc, sl=sl: t.transpose(
                            pbb[sl][:, hc * 128:(hc + 1) * 128],
                            htok[sl][:, hc * 128:(hc + 1) * 128], identb[:]))
                        for hc in range(8)],
                        reads=[f"htok{sl}", "identb"], writes=[f"pbb{sl}"])
                    dstH = H[:].rearrange("p (c f) -> p c f", f=S)[:, :, sb * 128:(sb + 1) * 128]
                    P.op("vector", lambda v, sl=sl, dstH=dstH: v.tensor_copy(
                        dstH, pbb[sl][:].rearrange("p (c f) -> p c f", f=128)),
                        reads=[f"pbb{sl}"], writes=[f"H{c}" for c in range(8)])

            ln_pass(True, "ln1")

            def emit_transposed_out(src_slicer, dram_ap, src_keys, bf16=False):
                for sb in range(SB):
                    sl = sb % 2
                    if not bf16:
                        for g in range(2):
                            P.mm_group([
                                (lambda t, hc=hc, sb=sb: t.transpose(
                                    pb[hc // 4][:, (hc % 4) * 128:(hc % 4 + 1) * 128],
                                    src_slicer(hc, sb * 128, 128), identf[:]))
                                for hc in range(4 * g, 4 * g + 4)],
                                reads=[src_keys[hc] for hc in range(4 * g, 4 * g + 4)] + ["identf"],
                                writes=[f"pb{g}"])
                        for g in range(2):
                            P.op("vector", lambda v, g=g, sl=sl: v.tensor_copy(
                                xin[sl][:, g * 512:(g + 1) * 512], pb[g][:]),
                                reads=[f"pb{g}"], writes=[f"xin{sl}"])
                    else:
                        P.mm_group([
                            (lambda t, hc=hc, sb=sb, sl=sl: t.transpose(
                                pbb[sl][:, hc * 128:(hc + 1) * 128],
                                src_slicer(hc, sb * 128, 128), identb[:]))
                            for hc in range(8)],
                            reads=list(src_keys) + ["identb"], writes=[f"pbb{sl}"])
                        P.op("vector", lambda v, sl=sl: v.tensor_copy(
                            xin[sl][:], pbb[sl][:]),
                            reads=[f"pbb{sl}"], writes=[f"xin{sl}"])
                    P.dma("scalar", "dout", lambda s, sb=sb, sl=sl: s.dma_start(
                        dram_ap[sb * 128:(sb + 1) * 128, :], xin[sl][:]),
                        reads=[f"xin{sl}"])

            if cfg.stage == "ln1":
                emit_transposed_out(Xs, out_d, [f"X{c}" for c in range(8)])
                emit_transposed_out(Hs, dbg["h"], [f"H{c}" for c in range(8)], bf16=True)
                _finish(P, block)
                return nc

            NP = cfg.NP

            def emit_wdma(hp):
                sl = hp % 2
                wsk = f"wstage{sl}"
                for m, w_d in enumerate([wq_d, wk_d, wv_d]):
                    P.dma("gpsimd", f"dwt{sl}",
                          lambda g, m=m, w_d=w_d, sl=sl, hp=hp: g.dma_start(
                              wstage[sl][:, m * 1024:(m + 1) * 1024],
                              w_d[hp * 128:(hp + 1) * 128, :]),
                          writes=[wsk])
                src_wo = wo_d.rearrange("(c p) f -> p c f", p=128)[:, :, hp * 128:(hp + 1) * 128]
                P.dma("gpsimd", f"dwt{sl}",
                      lambda g, sl=sl, src_wo=src_wo: g.dma_start(
                          wstage[sl][:, 3072:4096].rearrange("p (c f) -> p c f", f=128), src_wo),
                      writes=[wsk])

            def filler_items(hp):
                items = []
                sl = hp % 2
                wsk, wtk = f"wstage{sl}", f"WT{sl}"
                for seg in range(4):
                    def wt_seg(seg=seg, sl=sl, wsk=wsk, wtk=wtk):
                        P.mm_group([
                            (lambda t, seg=seg, c=c, sl=sl: t.transpose(
                                pbb[1][:, c * 128:(c + 1) * 128],
                                wstage[sl][:, seg * 1024 + c * 128: seg * 1024 + (c + 1) * 128],
                                identb[:]))
                            for c in range(8)],
                            reads=[wsk, "identb"], writes=["pbb1"])
                        P.op("scalar", lambda s, seg=seg, sl=sl: s.copy(
                            WT[sl][:, seg * 1024:(seg + 1) * 1024], pbb[1][:]),
                            reads=["pbb1"], writes=[wtk])
                    items.append(wt_seg)
                dstmap = {"qT": qT, "kT": kT, "vT": vT}
                for m, (dstk, bcol) in enumerate([("qT", 0), ("kT", 1), ("vT", 2)]):
                    for n in range(SC):
                        def qkv_one(m=m, n=n, sl=sl, dstk=dstk, bcol=bcol, hp=hp, wtk=wtk):
                            bk = 3
                            dstb = dstmap[dstk][sl]
                            P.mm_group([
                                (lambda t, c=c, n=n, sl=sl, bk=bk, m=m:
                                 t.matmul(pb[bk][:], WT[sl][:, m * 1024 + c * 128: m * 1024 + (c + 1) * 128],
                                          Hs(c, n * 512, 512), start=(c == 0), stop=(c == 7)))
                                for c in range(8)],
                                reads=[wtk] + [f"H{c}" for c in range(8)], writes=[f"pb{bk}"])
                            P.op("scalar", lambda s, n=n, bk=bk, dstb=dstb, bcol=bcol, hp=hp:
                                 s.activation(dstb[:, n * 512:(n + 1) * 512], pb[bk][:],
                                              AF.Identity, bias=biases[:, bcol * 8 + hp: bcol * 8 + hp + 1],
                                              scale=1.0),
                                 reads=[f"pb{bk}", "biases"], writes=[(dstk + str(sl)) if dstk != "vT" else "vT0"])
                        items.append(qkv_one)
                for g8 in range((SB + 7) // 8):
                    def vtok_g(g8=g8, sl=sl):
                        lo = g8 * 8
                        hi = min(lo + 8, SB)
                        P.mm_group([
                            (lambda t, st=st, lo=lo, sl=sl: t.transpose(
                                pbb[1][:, (st - lo) * 128:(st - lo + 1) * 128],
                                vT[sl][:, st * 128:(st + 1) * 128], identb[:]))
                            for st in range(lo, hi)],
                            reads=["vT0", "identb"], writes=["pbb1"])
                        P.op("scalar", lambda s, lo=lo, hi=hi, sl=sl: s.copy(
                            vtok[sl][:, lo * 128: hi * 128], pbb[1][:, 0:(hi - lo) * 128]),
                            reads=["pbb1"], writes=[f"vtok{sl}"])
                    items.append(vtok_g)
                return items

            def wo_items(hp):
                items = []
                sl = hp % 2
                wtk = f"WT{sl}"
                for oc in range(8):
                    def wo_oc(oc=oc, sl=sl, wtk=wtk, hp=hp):
                        for n in range(SC):
                            wb = 4 + (oc * SC + n) % 2
                            mm(pb[wb][:], WT[sl][:, 3072 + oc * 128: 3072 + (oc + 1) * 128],
                               opair[sl][:, n * 512:(n + 1) * 512],
                               reads=[wtk, f"opair{sl}"], writes=[f"pb{wb}"])
                            if hp == 0:
                                P.op("vector", lambda v, oc=oc, n=n, wb=wb: v.scalar_tensor_tensor(
                                    Xs(oc, n * 512, 512), pb[wb][:], biases[:, 24 + oc: 24 + oc + 1],
                                    Xs(oc, n * 512, 512), ALU.add, ALU.add),
                                    reads=[f"pb{wb}", "biases", f"X{oc}"], writes=[f"X{oc}"])
                            else:
                                P.op("vector", lambda v, oc=oc, n=n, wb=wb: v.tensor_tensor(
                                    Xs(oc, n * 512, 512), Xs(oc, n * 512, 512), pb[wb][:], ALU.add),
                                    reads=[f"pb{wb}", f"X{oc}"], writes=[f"X{oc}"])
                    items.append(wo_oc)
                return items

            def emit_scores(hp, qt):
                sl = hp % 2
                kts = [0] if qt == 0 else [0, qt - 1, qt]
                fns = []
                for h in range(2):
                    base = 64 * h
                    for j, kt in enumerate(kts):
                        fns.append(lambda t, h=h, base=base, j=j, kt=kt, qt=qt, sl=sl:
                                   t.matmul(psc[:, h * 512 + j * 128: h * 512 + (j + 1) * 128],
                                            qT[sl][base:base + 64, qt * 128:(qt + 1) * 128],
                                            kT[sl][base:base + 64, kt * 128:(kt + 1) * 128],
                                            start=True, stop=True))
                P.mm_group(fns, reads=[f"qT{sl}", f"kT{sl}"], writes=["pb0", "pb1"])

            def emit_softmax(hp, qt):
                W = 128 if qt == 0 else 384
                msk, mskk = (maskA, "maskA") if qt == 0 else (maskB, "maskB")
                par = qt % 2
                psc2 = psc[:].rearrange("p (h f) -> p h f", f=512)[:, :, 0:W]
                P.op("vector", lambda v, W=W, psc2=psc2: v.tensor_reduce(
                    nmx2[:], psc2, AX.X, ALU.max, negate=True),
                    reads=["pb0", "pb1"], writes=["nmx2"])
                P.op("vector", lambda v: v.tensor_scalar(
                    nm82[:], nmx2[:], 0.125, None, ALU.mult, ALU.bypass),
                    reads=["nmx2"], writes=["nm82"])
                mskw = msk[:].rearrange("p (h f) -> p h f", h=2)[:, :, 0:W]
                P.op("vector", lambda v, W=W, mskw=mskw, psc2=psc2: v.scalar_tensor_tensor(
                    s2b2[:].rearrange("p (h f) -> p h f", f=384)[:, :, 0:W], psc2, 0.125,
                    mskw, ALU.mult, ALU.add),
                    reads=["pb0", "pb1", mskk], writes=["s2b2"])
                for h in range(2):
                    P.op("scalar", lambda s, h=h, W=W, par=par: s.activation(
                        Pb[h][par][:, 0:W], s2b2[:, h * 384: h * 384 + W], AF.Exp,
                        bias=nm82[:, h: h + 1], scale=1.0, accum_out=zz[h][:]),
                        reads=["s2b2", "nm82"], writes=[f"Pb{h}_{par}", f"zz{h}"])
                P.op("vector", lambda v: v.reciprocal(rz2[:, 0:1], zz[0][:]),
                     reads=["zz0"], writes=["rz2"])
                P.op("vector", lambda v: v.reciprocal(rz2[:, 1:2], zz[1][:]),
                     reads=["zz1"], writes=["rz2"])
                for h in range(2):
                    P.op("vector", lambda v, h=h, W=W, par=par: v.tensor_scalar(
                        Pb[h][par][:, 0:W], Pb[h][par][:, 0:W], rz2[:, h: h + 1], None,
                        ALU.mult, ALU.bypass),
                        reads=[f"Pb{h}_{par}", "rz2"], writes=[f"Pb{h}_{par}"])

            def emit_ptpv(hp, qt):
                sl = hp % 2
                kts = [0] if qt == 0 else [0, qt - 1, qt]
                nk = len(kts)
                W = 128 * nk
                par = qt % 2
                P.mm_group([
                    (lambda t, h=h, j=j, par=par: t.transpose(
                        pbb[0][:, (h * 4 + j) * 128:(h * 4 + j + 1) * 128],
                        Pb[h][par][:, j * 128:(j + 1) * 128], identb[:]))
                    for h in range(2) for j in range(nk)],
                    reads=[f"Pb0_{par}", f"Pb1_{par}", "identb"], writes=["pbb0"])
                for h in range(2):
                    base = 64 * h
                    P.op("scalar", lambda s, h=h, W=W: s.copy(PT[h][:, 0:W],
                                                              pbb[0][:, h * 512: h * 512 + W]),
                         reads=["pbb0"], writes=[f"PT{h}"])
                for h in range(2):
                    base = 64 * h
                    P.mm_group([
                        (lambda t, h=h, j=j, kt=kt, qt=qt, base=base, nk=nk, sl=sl:
                         t.matmul(pb[2][base:base + 64, (qt % 4) * 128:(qt % 4 + 1) * 128],
                                  vtok[sl][:, kt * 128 + base: kt * 128 + base + 64],
                                  PT[h][:, j * 128:(j + 1) * 128],
                                  start=(j == 0), stop=(j == nk - 1)))
                        for j, kt in enumerate(kts)],
                        reads=[f"vtok{sl}", f"PT{h}"], writes=[f"pb2_{h}_{qt % 4}"])
                if qt % 4 == 3 or qt == SB - 1:
                    qlo = (qt // 4) * 4
                    nq = qt - qlo + 1
                    P.op("scalar", lambda s, qlo=qlo, nq=nq, sl=sl, qt=qt: s.copy(
                        opair[sl][:, qlo * 128:(qlo + nq) * 128], pb[2][:, 0:nq * 128]),
                        reads=[f"pb2_{h}_{q % 4}" for h in range(2) for q in range(qlo, qt + 1)],
                        writes=[f"opair{sl}"])

            emit_wdma(0)
            for it in filler_items(0):
                it()
            for hp in range(NP):
                if hp + 1 < NP:
                    emit_wdma(hp + 1)
                queue = []
                if hp >= 1:
                    queue += wo_items(hp - 1)
                if hp + 1 < NP:
                    queue += filler_items(hp + 1)
                prev = None
                popped = 0
                for qt in range(SB):
                    emit_scores(hp, qt)
                    emit_softmax(hp, qt)
                    if prev is not None:
                        emit_ptpv(hp, prev)
                    want = (qt + 1) * len(queue) // SB
                    while popped < want:
                        queue[popped]()
                        popped += 1
                    prev = qt
                emit_ptpv(hp, SB - 1)
                while popped < len(queue):
                    queue[popped]()
                    popped += 1
            for it in wo_items(NP - 1):
                it()
            if cfg.stage == "attn":
                emit_transposed_out(Xs, out_d, [f"X{c}" for c in range(8)])
                emit_transposed_out(Hs, dbg["h"], [f"H{c}" for c in range(8)], bf16=True)
                _finish(P, block)
                return nc

            ln_pass(False, "ln2")

            for sb in range(SB):
                P.mm_group([
                    (lambda t, c=c, sb=sb: t.matmul(
                        pb[2][:, 0:7], Hs(c, sb * 128, 128), wrT[:, c * 7:(c + 1) * 7],
                        start=(c == 0), stop=(c == 7)))
                    for c in range(8)],
                    reads=[f"H{c}" for c in range(8)] + ["wrT"], writes=["pb2"])
                P.op("vector", lambda v: v.tensor_tensor(lbuf[:], pb[2][:, 0:7], rbbc[:], ALU.add),
                     reads=["pb2", "rbbc"], writes=["lbuf"])
                P.op("vector", lambda v: v.tensor_reduce(r1[:], lbuf[:], AX.X, ALU.max, negate=True),
                     reads=["lbuf"], writes=["r1"])
                P.op("vector", lambda v: v.tensor_scalar(tbuf[:], lbuf[:], r1[:], None, ALU.add, ALU.bypass),
                     reads=["lbuf", "r1"], writes=["tbuf"])
                P.op("vector", lambda v: v.tensor_scalar(mbuf[:], tbuf[:], 0.0, None, ALU.is_ge, ALU.bypass),
                     reads=["tbuf"], writes=["mbuf"])
                P.op("vector", lambda v: v.scalar_tensor_tensor(l2buf[:], mbuf[:], NEG, tbuf[:],
                                                                ALU.mult, ALU.add),
                     reads=["mbuf", "tbuf"], writes=["l2buf"])
                P.op("vector", lambda v: v.tensor_reduce(r2[:], l2buf[:], AX.X, ALU.max, negate=True),
                     reads=["l2buf"], writes=["r2"])
                P.op("vector", lambda v: v.tensor_scalar(mbuf[:], tbuf[:], r2[:], 0.0, ALU.add, ALU.is_ge),
                     reads=["tbuf", "r2"], writes=["mbuf"])
                P.op("scalar", lambda s: s.activation(ebuf[:], tbuf[:], AF.Exp, bias=0.0, scale=1.0),
                     reads=["tbuf"], writes=["ebuf"])
                P.op("vector", lambda v: v.tensor_tensor(ebuf[:], ebuf[:], mbuf[:], ALU.mult),
                     reads=["ebuf", "mbuf"], writes=["ebuf"])
                P.op("vector", lambda v: v.tensor_reduce(r1[:], ebuf[:], AX.X, ALU.add),
                     reads=["ebuf"], writes=["r1"])
                P.op("vector", lambda v: v.reciprocal(r2[:], r1[:]), reads=["r1"], writes=["r2"])
                P.op("vector", lambda v, sb=sb: v.tensor_scalar(
                    wtok[:, sb * 7:(sb + 1) * 7], ebuf[:], r2[:], None, ALU.mult, ALU.bypass),
                    reads=["ebuf", "r2"], writes=["wtok"])
            for sb in range(SB):
                bk = (sb // 4) % 2
                P.mm_group([lambda t, sb=sb, bk=bk: t.transpose(
                    pb[bk][0:7, (sb % 4) * 128:(sb % 4 + 1) * 128],
                    wtok[:, sb * 7:(sb + 1) * 7], identf[:])],
                    reads=["wtok", "identf"], writes=[f"pb{bk}"])
                if sb % 4 == 3 or sb == SB - 1:
                    lo = (sb // 4) * 4
                    nqq = (sb - lo + 1) * 128
                    P.op("vector", lambda v, lo=lo, nqq=nqq, bk=bk: v.tensor_copy(
                        wrows[0:7, lo * 128: lo * 128 + nqq], pb[bk][0:7, 0:nqq]),
                        reads=[f"pb{bk}"], writes=["wrows"])

            if cfg.stage == "router":
                P.dma("scalar", "dout", lambda s: s.dma_start(dbg["w"], wtok[:]), reads=["wtok"])
                emit_transposed_out(Hs, dbg["h"], [f"H{c}" for c in range(8)], bf16=True)
                emit_transposed_out(Xs, out_d, [f"X{c}" for c in range(8)])
                _finish(P, block)
                return nc

            for hc in range(8):
                P.dma("scalar", "dxw", lambda s, hc=hc: s.dma_start(
                    x1_d[hc], Xs(hc, 0, S)),
                    reads=[f"X{hc}"], writes=[f"x1_{hc}_{n}" for n in range(SC)])
        P.fence()

        with ExitStack() as es_f:
            Tf = lambda nm, sh, dt=F32: es_f.enter_context(nc.sbuf_tensor(nm, sh, dt))
            tbig = Tf("tbig", [128, NI * S], BF16)
            es_f1 = es_f.enter_context(ExitStack())
            Tf1 = lambda nm, sh, dt=F32: es_f1.enter_context(nc.sbuf_tensor(nm, sh, dt))
            sgstage = [Tf1(f"sgstage{s}", [128, 2048], BF16) for s in range(2)]
            wTf = [Tf1(f"wTf{s}", [128, 2048], BF16) for s in range(2)]
            sdstage = [Tf1(f"sdstage{s}", [128, 2816], BF16) for s in range(2)]
            sdT = [Tf1(f"sdT{s}", [128, 2816], BF16) for s in range(2)]
            sil = Tf1("sil", [128, 512], F32)
            wbc = Tf1("wbc", [128, S], F32)
            rin = [Tf1(f"rin{s}", [128, 512], F32) for s in range(2)]
            rout = [Tf1(f"rout{s}", [128, 512], F32) for s in range(2)]

            n_exp = 1 + cfg.NRE
            for e in range(n_exp):
                if e == 0:
                    g_src, u_src, d_src = sg_d, su_d, sd_d
                else:
                    g_src, u_src, d_src = rg_d[e - 1], ru_d[e - 1], rd_d[e - 1]
                    P.dma("gpsimd", "dwb", lambda g, e=e: g.dma_start(
                        wbc[0:1, :], wrows[e - 1:e, :]),
                        reads=["wrows"], writes=["wbc"])
                    for n in range(SC):
                        mm(pb[5][:], onesrow[:], wbc[0:1, n * 512:(n + 1) * 512],
                           reads=["wbc", "onesrow"], writes=["pb5"])
                        P.op("vector", lambda v, n=n: v.tensor_copy(
                            wbc[:, n * 512:(n + 1) * 512], pb[5][:]),
                            reads=["pb5"], writes=["wbc", "wbc"])
                def emit_sg_dma(i, g_src=g_src, u_src=u_src):
                    sl = i % 2
                    sgk = f"sgstage{sl}"
                    P.dma("gpsimd", f"dsg{sl}", lambda g, i=i, sl=sl, g_src=g_src: g.dma_start(
                        sgstage[sl][:, 0:1024], g_src[i * 128:(i + 1) * 128, :]), writes=[sgk])
                    P.dma("gpsimd", f"dsg{sl}", lambda g, i=i, sl=sl, u_src=u_src: g.dma_start(
                        sgstage[sl][:, 1024:2048], u_src[i * 128:(i + 1) * 128, :]), writes=[sgk])

                def emit_sg_transpose(i):
                    sl = i % 2
                    sgk = f"sgstage{sl}"
                    wtk = f"wTf{sl}"
                    for m in range(2):
                        P.mm_group([
                            (lambda t, m=m, c=c, sl=sl: t.transpose(
                                pbb[m][:, c * 128:(c + 1) * 128],
                                sgstage[sl][:, m * 1024 + c * 128: m * 1024 + (c + 1) * 128],
                                identb[:]))
                            for c in range(8)],
                            reads=[sgk, "identb"], writes=[f"pbb{m}"])
                        P.op("scalar", lambda s, m=m, sl=sl: s.copy(
                            wTf[sl][:, m * 1024:(m + 1) * 1024], pbb[m][:]),
                            reads=[f"pbb{m}"], writes=[wtk])

                def emit_gateup(i):
                    sl = i % 2
                    wtk = f"wTf{sl}"
                    for n in range(SC):
                        gb, ub = n % 2, 2 + n % 2
                        P.mm_group([
                            (lambda t, c=c, n=n, sl=sl, gb=gb: t.matmul(
                                pb[gb][:], wTf[sl][:, c * 128:(c + 1) * 128],
                                Hs(c, n * 512, 512), start=(c == 0), stop=(c == 7)))
                            for c in range(8)],
                            reads=[wtk] + [f"H{c}" for c in range(8)], writes=[f"pb{gb}"])
                        P.mm_group([
                            (lambda t, c=c, n=n, sl=sl, ub=ub: t.matmul(
                                pb[ub][:], wTf[sl][:, 1024 + c * 128: 1024 + (c + 1) * 128],
                                Hs(c, n * 512, 512), start=(c == 0), stop=(c == 7)))
                            for c in range(8)],
                            reads=[wtk] + [f"H{c}" for c in range(8)], writes=[f"pb{ub}"])
                        P.op("scalar", lambda s, gb=gb: s.activation(sil[:], pb[gb][:], AF.Silu,
                                                                     bias=0.0, scale=1.0),
                             reads=[f"pb{gb}"], writes=["sil"])
                        P.op("vector", lambda v, i=i, n=n, ub=ub: v.tensor_tensor(
                            tbig[:, i * S + n * 512: i * S + (n + 1) * 512], sil[:], pb[ub][:],
                            ALU.mult),
                            reads=["sil", f"pb{ub}"], writes=[f"t{i}"])

                emit_sg_dma(0)
                emit_sg_dma(1)
                for i in range(NI):
                    emit_sg_transpose(i)
                    if i + 2 < NI:
                        emit_sg_dma(i + 2)
                    if i >= 1:
                        emit_gateup(i - 1)
                emit_gateup(NI - 1)
                def emit_sd_dma(hc, d_src=d_src):
                    sl = hc % 2
                    P.dma("gpsimd", f"dsd{sl}", lambda g, hc=hc, sl=sl, d_src=d_src: g.dma_start(
                        sdstage[sl][:], d_src[hc * 128:(hc + 1) * 128, :]),
                        writes=[f"sdstage{sl}"])

                def emit_sd_transpose(hc):
                    sl = hc % 2
                    sdk, sdtk = f"sdstage{sl}", f"sdT{sl}"
                    for grp in range(3):
                        lo = grp * 8
                        hi = min(lo + 8, NI)
                        bkk = grp % 2
                        P.mm_group([
                            (lambda t, i=i, lo=lo, sl=sl, bkk=bkk: t.transpose(
                                pbb[bkk][:, (i - lo) * 128:(i - lo + 1) * 128],
                                sdstage[sl][:, i * 128:(i + 1) * 128], identb[:]))
                            for i in range(lo, hi)],
                            reads=[sdk, "identb"], writes=[f"pbb{bkk}"])
                        P.op("scalar", lambda s, lo=lo, hi=hi, sl=sl, bkk=bkk: s.copy(
                            sdT[sl][:, lo * 128: hi * 128], pbb[bkk][:, 0:(hi - lo) * 128]),
                            reads=[f"pbb{bkk}"], writes=[sdtk])

                def emit_down(hc):
                    sl = hc % 2
                    sdtk = f"sdT{sl}"
                    for n in range(SC):
                        dbk = [4, 5, 2, 3][n % 4]
                        rsl = (hc * SC + n) % 2
                        P.dma("sync", f"dri{rsl}", lambda s, hc=hc, n=n, rsl=rsl: s.dma_start(
                            rin[rsl][:], x1_d[hc][:, n * 512:(n + 1) * 512]),
                            reads=[f"x1_{hc}_{n}"], writes=[f"rin{rsl}"])
                        P.mm_group([
                            (lambda t, i=i, n=n, sl=sl, dbk=dbk: t.matmul(
                                pb[dbk][:], sdT[sl][:, i * 128:(i + 1) * 128],
                                tbig[:, i * S + n * 512: i * S + (n + 1) * 512],
                                start=(i == 0), stop=(i == NI - 1)))
                            for i in range(NI)],
                            reads=[sdtk] + [f"t{i}" for i in range(NI)], writes=[f"pb{dbk}"])
                        if e == 0:
                            P.op("vector", lambda v, n=n, dbk=dbk, rsl=rsl: v.tensor_tensor(
                                rout[rsl][:], rin[rsl][:], pb[dbk][:], ALU.add),
                                reads=[f"rin{rsl}", f"pb{dbk}"], writes=[f"rout{rsl}"])
                        else:
                            P.op("vector", lambda v, n=n, dbk=dbk, rsl=rsl: v.tensor_tensor(
                                rout[rsl][:], pb[dbk][:], wbc[:, n * 512:(n + 1) * 512], ALU.mult),
                                reads=[f"pb{dbk}", "wbc"], writes=[f"rout{rsl}"])
                            P.op("vector", lambda v, rsl=rsl: v.tensor_tensor(
                                rout[rsl][:], rout[rsl][:], rin[rsl][:], ALU.add),
                                reads=[f"rout{rsl}", f"rin{rsl}"], writes=[f"rout{rsl}"])
                        P.dma("scalar", f"dro{rsl}", lambda s, hc=hc, n=n, rsl=rsl: s.dma_start(
                            x1_d[hc][:, n * 512:(n + 1) * 512], rout[rsl][:]),
                            reads=[f"rout{rsl}"], writes=[f"x1_{hc}_{n}"])

                emit_sd_dma(0)
                emit_sd_dma(1)
                for hc in range(8):
                    emit_sd_transpose(hc)
                    if hc + 2 < 8:
                        emit_sd_dma(hc + 2)
                    if hc >= 1:
                        emit_down(hc - 1)
                emit_down(7)

            es_f1.close()
            P.fence()
            rinbig = [Tf(f"rinbig{s}", [128, 1024], F32) for s in range(2)]
            routbig = [Tf(f"routbig{s}", [128, 1024], F32) for s in range(2)]
            for sb in range(SB):
                sl = sb % 2
                src_fin = x1_d.rearrange("c p f -> p c f")[:, :, sb * 128:(sb + 1) * 128]
                P.dma("sync", f"dfi{sl}", lambda s, sl=sl, src_fin=src_fin: s.dma_start(
                    rinbig[sl][:].rearrange("p (c f) -> p c f", f=128), src_fin),
                    reads=[f"x1_{hc}_{n}" for hc in range(8) for n in range(SC)],
                    writes=[f"rinbig{sl}"])
                for g in range(2):
                    P.mm_group([
                        (lambda t, hc=hc, sl=sl: t.transpose(
                            pb[hc // 4][:, (hc % 4) * 128:(hc % 4 + 1) * 128],
                            rinbig[sl][:, hc * 128:(hc + 1) * 128], identf[:]))
                        for hc in range(4 * g, 4 * g + 4)],
                        reads=[f"rinbig{sl}", "identf"], writes=[f"pb{g}"])
                for g in range(2):
                    P.op("vector", lambda v, g=g, sl=sl: v.tensor_copy(
                        routbig[sl][:, g * 512:(g + 1) * 512], pb[g][:]),
                        reads=[f"pb{g}"], writes=[f"routbig{sl}"])
                P.dma("scalar", "dout", lambda s, sb=sb, sl=sl: s.dma_start(
                    out_d[sb * 128:(sb + 1) * 128, :], routbig[sl][:]),
                    reads=[f"routbig{sl}"])

            _finish(P, block)
    return nc


def _finish(P, block):
    n_out = P.dma_count.get("dout", 0)
    waits = {"dout": n_out}
    for k in ("dro0", "dro1"):
        if P.dma_count.get(k):
            waits[k] = P.dma_count[k]
    P.ops["scalar"].append((dict(waits), lambda s: s.nop(), None, 0))
    P.finalize(block)


import numpy as np
from concourse.bass_utils import run_bass_kernel_spmd

_NC = 8
_S = 2048


def _consts():
    i = np.arange(128)
    maskA = np.where(((i[None, :] <= i[:, None]) & (i[:, None] - i[None, :] <= 64))
                     | (i[None, :] == 0), 0.0, -1e30).astype(np.float32)
    mB = np.full((128, 384), -1e30, np.float32)
    mB[:, 0] = 0.0
    jj = np.arange(128)
    mB[:, 128:256] = np.where(jj[None, :] >= i[:, None] + 64, 0.0, -1e30)
    mB[:, 256:384] = np.where((jj[None, :] <= i[:, None]) & (i[:, None] - jj[None, :] <= 64),
                              0.0, -1e30)
    return {
        "identf": np.eye(128, dtype=np.float32),
        "identb": np.eye(128, dtype=np.float32),
        "maskA": np.concatenate([maskA, maskA], axis=1),
        "maskB": np.concatenate([mB, mB], axis=1),
        "onesrow": np.ones((1, 128), np.float32),
    }


_nc_cache = {}


def _get_nc():
    if "nc" not in _nc_cache:
        _nc_cache["nc"] = build(Cfg(S=_S, NRE=7, stage="full", NP=8))
    return _nc_cache["nc"]


def kernel(L0_attn_hidden_states, L0_attn_Wq, L0_attn_bq, L0_attn_Wk, L0_attn_bk,
           L0_attn_Wv, L0_attn_bv, L0_attn_Wo, L0_attn_bo, L0_ffn_sg, L0_ffn_su,
           L0_ffn_sd, L0_ffn_rg, L0_ffn_ru, L0_ffn_rd, L0_ffn_wr, L0_ffn_rb):
    x = np.ascontiguousarray(np.asarray(L0_attn_hidden_states, dtype=np.float32))
    c = lambda a: np.ascontiguousarray(np.asarray(a, dtype=np.float32))
    shared = {
        "wq": c(L0_attn_Wq), "wk": c(L0_attn_Wk), "wv": c(L0_attn_Wv), "wo": c(L0_attn_Wo),
        "bq": c(L0_attn_bq), "bk": c(L0_attn_bk), "bv": c(L0_attn_bv), "bo": c(L0_attn_bo),
        "sg": c(L0_ffn_sg), "su": c(L0_ffn_su), "sd": c(L0_ffn_sd),
        "rg": c(L0_ffn_rg), "ru": c(L0_ffn_ru), "rd": c(L0_ffn_rd),
        "wr": c(L0_ffn_wr), "rb": c(L0_ffn_rb),
    }
    shared.update(_consts())
    nc = _get_nc()
    in_maps = [dict(shared, x=np.ascontiguousarray(x[b])) for b in range(_NC)]
    res = run_bass_kernel_spmd(nc, in_maps, core_ids=list(range(_NC)))
    return np.stack([res.results[b]["out"] for b in range(_NC)], axis=0)


# version 08
# speedup vs baseline: 1.8255x; 1.3091x over previous
from contextlib import ExitStack
from collections import defaultdict
import concourse.bass as bass
import concourse.mybir as mybir

ENGINES = ("tensor", "vector", "scalar", "gpsimd", "sync")


class Prog:
    def __init__(self, nc):
        self.nc = nc
        self.ops = {e: [] for e in ENGINES}
        self.eng_count = {e: 0 for e in ENGINES}
        self.dma_count = defaultdict(int)
        self.res = {}
        self.high_water = {e: defaultdict(int) for e in ENGINES}
        self.sem_handles = {}
        self._es = None

    def init_sems(self, es, dma_keys):
        for e in ENGINES:
            self.sem_handles["@" + e] = es.enter_context(self.nc.semaphore("sem_" + e))
        for k in dma_keys:
            self.sem_handles[k] = es.enter_context(self.nc.semaphore("sem_" + k))

    def _want(self, waits, token):
        if token is None:
            return
        sem, val = token
        if waits.get(sem, 0) < val:
            waits[sem] = val

    def _emit(self, engine, fn, reads, writes, inc_sem, inc_amt, accum_key=None,
              extra_waits=()):
        waits = {}
        pf = getattr(self, "_pending_fence", None)
        if pf and engine in pf:
            self._want(waits, pf.pop(engine))
        for tok in extra_waits:
            self._want(waits, tok)
        for r in reads:
            e = self.res.get(r)
            if e:
                self._want(waits, e["w"])
        for w in writes:
            e = self.res.get(w)
            if e:
                if accum_key is None or e.get("acc") != accum_key:
                    self._want(waits, e["w"])
                    for t in e["r"]:
                        self._want(waits, t)
        hw = self.high_water[engine]
        final_waits = {}
        for sem, val in waits.items():
            if sem == "@" + engine:
                pass
            if hw[sem] < val:
                final_waits[sem] = val
                hw[sem] = val
        if inc_sem is None:
            inc_sem = "@" + engine
        if inc_sem.startswith("@"):
            self.eng_count[engine] += inc_amt
            token = (inc_sem, self.eng_count[engine])
        else:
            self.dma_count[inc_sem] += inc_amt
            token = (inc_sem, self.dma_count[inc_sem])
        for r in reads:
            self.res.setdefault(r, {"w": None, "r": [], "acc": None})["r"].append(token)
        for w in writes:
            e = self.res.setdefault(w, {"w": None, "r": [], "acc": None})
            e["w"] = token
            e["r"] = []
            e["acc"] = accum_key
        self.ops[engine].append((final_waits, fn, inc_sem if not inc_sem.startswith("@") else "@" + engine, inc_amt))
        return token

    def op(self, engine, fn, reads=(), writes=()):
        return self._emit(engine, fn, reads, writes, None, 1)

    def mm_group(self, fns, reads=(), writes=(), key=None):
        key = key or object()
        tok = None
        for i, fn in enumerate(fns):
            last = i == len(fns) - 1
            if last:
                tok = self._emit("tensor", fn, reads, writes, None, 1, accum_key=key)
            else:
                waits = {}
                pf = getattr(self, "_pending_fence", None)
                if pf and "tensor" in pf:
                    self._want(waits, pf.pop("tensor"))
                for r in reads:
                    e = self.res.get(r)
                    if e:
                        self._want(waits, e["w"])
                for w in writes:
                    e = self.res.get(w)
                    if e and e.get("acc") != key:
                        self._want(waits, e["w"])
                        for t in e["r"]:
                            self._want(waits, t)
                        e["acc"] = key
                hw = self.high_water["tensor"]
                fw = {}
                for sem, val in waits.items():
                    if hw[sem] < val:
                        fw[sem] = val
                        hw[sem] = val
                for w in writes:
                    e = self.res.setdefault(w, {"w": None, "r": [], "acc": key})
                    e["acc"] = key
                self.ops["tensor"].append((fw, fn, None, 0))
        return tok

    def dma(self, engine, sem_key, fn, reads=(), writes=()):
        prev = self.dma_count.get(sem_key, 0)
        extra = ((sem_key, prev),) if prev else ()
        return self._emit(engine, fn, reads, writes, sem_key, 16,
                          extra_waits=extra)

    def read_barrier(self, engine, resources):
        raise NotImplementedError

    def barrier_op(self, engine, fn, resources):
        return self.op(engine, fn, reads=tuple(resources), writes=())

    def fence(self):
        all_waits = {}
        for e in ENGINES:
            if self.eng_count[e]:
                all_waits["@" + e] = self.eng_count[e]
        for k, v in self.dma_count.items():
            if v:
                all_waits[k] = v
        self._pending_fence = {}
        for e in ENGINES:
            waits = {s: v for s, v in all_waits.items() if s != "@" + e}
            self.eng_count[e] += 1
            tok = ("@" + e, self.eng_count[e])
            self.ops[e].append((waits, lambda eng: eng.drain(), "@" + e, 1))
            self._pending_fence[e] = tok
            hw = self.high_water[e]
            for s, v in waits.items():
                if hw[s] < v:
                    hw[s] = v
        self.res = {}

    def finalize(self, block):
        sems = self.sem_handles
        for engine in ENGINES:
            stream = self.ops[engine]
            if not stream:
                continue

            def make_body(stream=stream, engine=engine):
                def body(eng):
                    for waits, fn, inc_sem, inc_amt in stream:
                        for sem, val in waits.items():
                            eng.wait_ge(sems[sem], val)
                        inst = fn(eng)
                        if inc_amt and inst is not None:
                            inst.then_inc(sems[inc_sem], inc_amt)
                return body

            getattr(block, engine)(make_body())

    def pending_counts(self):
        return dict(self.eng_count), dict(self.dma_count)


F32 = mybir.dt.float32
BF16 = mybir.dt.bfloat16
AF = mybir.ActivationFunctionType
ALU = mybir.AluOpType
AX = mybir.AxisListType

NEG = -1e30
EPS = 1e-5


class Cfg:
    def __init__(self, S=2048, NRE=7, stage="full", NP=8, C=1024, sparse=True,
                 prep_cut=99):
        self.S = S
        self.NRE = NRE
        self.stage = stage
        self.NP = NP
        self.C = C
        self.sparse = sparse
        self.prep_cut = prep_cut
        self.skip_attn = False


def build(cfg: Cfg):
    nc = _build(cfg)
    mybir.codegen_inst_isa_subclasses(nc)
    return nc


def _build(cfg: Cfg):
    S = cfg.S
    SB = S // 128
    SC = S // 512
    NI = 22
    nc = bass.Bass("TRN2", target_bir_lowering=False)

    x_d = nc.dram_tensor("x", [S, 1024], F32, kind="ExternalInput").ap()
    wq_d = nc.dram_tensor("wq", [1024, 1024], F32, kind="ExternalInput").ap()
    wk_d = nc.dram_tensor("wk", [1024, 1024], F32, kind="ExternalInput").ap()
    wv_d = nc.dram_tensor("wv", [1024, 1024], F32, kind="ExternalInput").ap()
    wo_d = nc.dram_tensor("wo", [1024, 1024], F32, kind="ExternalInput").ap()
    bq_d = nc.dram_tensor("bq", [1024], F32, kind="ExternalInput").ap()
    bk_d = nc.dram_tensor("bk", [1024], F32, kind="ExternalInput").ap()
    bv_d = nc.dram_tensor("bv", [1024], F32, kind="ExternalInput").ap()
    bo_d = nc.dram_tensor("bo", [1024], F32, kind="ExternalInput").ap()
    sg_d = nc.dram_tensor("sg", [2816, 1024], F32, kind="ExternalInput").ap()
    su_d = nc.dram_tensor("su", [2816, 1024], F32, kind="ExternalInput").ap()
    sd_d = nc.dram_tensor("sd", [1024, 2816], F32, kind="ExternalInput").ap()
    rg_d = nc.dram_tensor("rg", [7, 2816, 1024], F32, kind="ExternalInput").ap()
    ru_d = nc.dram_tensor("ru", [7, 2816, 1024], F32, kind="ExternalInput").ap()
    rd_d = nc.dram_tensor("rd", [7, 1024, 2816], F32, kind="ExternalInput").ap()
    wr_d = nc.dram_tensor("wr", [7, 1024], F32, kind="ExternalInput").ap()
    rb_d = nc.dram_tensor("rb", [7], F32, kind="ExternalInput").ap()
    identf_d = nc.dram_tensor("identf", [128, 128], F32, kind="ExternalInput").ap()
    identb_d = nc.dram_tensor("identb", [128, 128], F32, kind="ExternalInput").ap()
    maskA_d = nc.dram_tensor("maskA", [128, 256], F32, kind="ExternalInput").ap()
    maskB_d = nc.dram_tensor("maskB", [128, 768], F32, kind="ExternalInput").ap()
    onesrow_d = nc.dram_tensor("onesrow", [1, 128], F32, kind="ExternalInput").ap()

    out_d = nc.dram_tensor("out", [S, 1024], F32, kind="ExternalOutput").ap()
    dbg = {}
    if cfg.stage in ("ln1", "attn", "router"):
        dbg["h"] = nc.dram_tensor("hdbg", [S, 1024], F32, kind="ExternalOutput").ap()
    if cfg.stage == "router":
        dbg["w"] = nc.dram_tensor("wdbg", [128, 7 * SB], F32, kind="ExternalOutput").ap()
    x1_d = nc.dram_tensor("x1d", [8, 128, S], F32, kind="Internal").ap()
    C = cfg.C
    yc_d = nc.dram_tensor("ycd", [7 * C + 128, 1024], BF16, kind="Internal").ap()
    h2t_d = nc.dram_tensor("h2td", [S, 1024], BF16, kind="Internal").ap()
    tokid1_d = nc.dram_tensor("tokid1", [16, 128], F32, kind="ExternalInput").ap()
    iotac2_d = nc.dram_tensor("iotac2", [128, 7], F32, kind="ExternalInput").ap()

    with ExitStack() as es:
        Tz = lambda nm, sh, dt=F32: es.enter_context(nc.sbuf_tensor(nm, sh, dt))
        H = Tz("H", [128, 8 * S], BF16)
        identf = Tz("identf_s", [128, 128], F32)
        identb = Tz("identb_s", [128, 128], BF16)
        onesrow = Tz("onesrow_s", [1, 128], F32)
        biases = Tz("biases", [128, 32], F32)
        wrT = Tz("wrT", [128, 56], BF16)
        rbrow = Tz("rbrow", [1, 7], F32)
        rbbc = Tz("rbbc", [128, 7], F32)
        wtok = Tz("wtok", [128, 7 * SB], F32)
        ismaxB = Tz("ismaxB", [128, 7 * SB], F32)
        top2B = Tz("top2B", [128, 7 * SB], F32)
        W0B = Tz("W0B", [128, SB], F32)
        W1B = Tz("W1B", [128, SB], F32)
        selm = Tz("selm", [7, S], BF16)
        iotac2 = Tz("iotac2_s", [128, 7], F32)
        lbuf = Tz("lbuf", [128, 7], F32)
        tbuf = Tz("tbuf", [128, 7], F32)
        mbuf = Tz("mbuf", [128, 7], F32)
        l2buf = Tz("l2buf", [128, 7], F32)
        ebuf = Tz("ebuf", [128, 7], F32)
        r1 = Tz("r1", [128, 1], F32)
        eps_col = Tz("eps_col", [128, 1], F32)
        r2 = Tz("r2", [128, 1], F32)

        psc = es.enter_context(nc.psum_tensor("psc", [128, 1024], F32))
        pb = [psc[:, 0:512], psc[:, 512:1024]] + [
            es.enter_context(nc.psum_tensor(f"pb{i}", [128, 512], F32)) for i in range(2, 6)]
        pbb = [es.enter_context(nc.psum_tensor(f"pbb{i}", [128, 1024], BF16)) for i in range(2)]

        prog = Prog(nc)
        dma_keys = ["dx0", "dx1", "dwt0", "dwt1", "dsg0", "dsg1", "dsd0", "dsd1",
                    "dcst", "dwb", "dri0", "dri1", "dro0", "dro1", "dxw", "dout",
                    "dfi0", "dfi1", "dga0", "dga1", "dhc0", "dhc1", "dht0", "dht1"]
        prog.init_sems(es, dma_keys)
        block = es.enter_context(nc.Block())

        P = prog

        def mm(psum_ap, lhsT_ap, rhs_ap, reads, writes, start=True, stop=True):
            return P.op("tensor", lambda t: t.matmul(psum_ap, lhsT_ap, rhs_ap,
                                                     start=start, stop=stop),
                        reads=reads, writes=writes)

        def Hs(hc, lo, n):
            return H[:, hc * S + lo: hc * S + lo + n]

        P.op("vector", lambda v: v.memset(eps_col[:], EPS), writes=["eps_col"])
        P.dma("sync", "dcst", lambda s: s.dma_start(identf[:], identf_d), writes=["identf"])
        P.dma("sync", "dcst", lambda s: s.dma_start(onesrow[:], onesrow_d), writes=["onesrow"])
        P.dma("sync", "dcst", lambda s: s.dma_start(iotac2[:], iotac2_d), writes=["iotac2"])
        P.dma("sync", "dcst", lambda s: s.dma_start(rbrow[:], rb_d[None, :]), writes=["rbrow"])
        P.dma("gpsimd", "dwb", lambda g: g.dma_start(identb[:], identb_d), writes=["identb"])

        with ExitStack() as es_a:
            Ta = lambda nm, sh, dt=F32: es_a.enter_context(nc.sbuf_tensor(nm, sh, dt))
            X = Ta("X", [128, 8 * S], F32)
            brow = Ta("brow", [1, 1024], F32)
            wrstage = Ta("wrstage", [7, 1024], BF16)
            P.dma("gpsimd", "dwb", lambda g: g.dma_start(wrstage[:], wr_d), writes=["wrstage"])
            for bi, b_d in enumerate([bq_d, bk_d, bv_d, bo_d]):
                P.dma("sync", "dcst", lambda s, b_d=b_d: s.dma_start(brow[:], b_d[None, :]),
                      reads=["biasesp"], writes=["brow"])
                for c in range(8):
                    mm(pb[5][:, bi * 8 + c: bi * 8 + c + 1], brow[0:1, c * 128:(c + 1) * 128],
                       onesrow[0:1, 0:1], reads=["brow", "onesrow"], writes=["pb5"])
                P.op("vector", lambda v, bi=bi: v.tensor_copy(biases[:, bi * 8:(bi + 1) * 8],
                                                              pb[5][:, bi * 8:(bi + 1) * 8]),
                     reads=["pb5"], writes=["biases", "biasesp"])
            mm(pb[5][:, 40:47], onesrow[:], rbrow[0:1, 0:7], reads=["onesrow", "rbrow"], writes=["pb5"])
            P.op("vector", lambda v: v.tensor_copy(rbbc[:], pb[5][:, 40:47]), reads=["pb5"], writes=["rbbc"])
            for c in range(8):
                mm(pb[5][:, 64 + c * 8: 64 + c * 8 + 7], wrstage[0:7, c * 128:(c + 1) * 128],
                   identb[0:7, 0:7], reads=["wrstage", "identb"], writes=["pb5"])
            P.op("vector", lambda v: v.tensor_copy(
                wrT[:].rearrange("p (c f) -> p c f", f=7),
                pb[5][:, 64:128].rearrange("p (c f) -> p c f", f=8)[:, :, 0:7]),
                reads=["pb5"], writes=["wrT"])

            xin = [Ta(f"xin{s}", [128, 1024], F32) for s in range(2)]

            st6 = [Ta(f"st6{s}", [128, 12], F32) for s in range(2)]
            mv = [Ta(f"mv{s}", [128, 2], F32) for s in range(2)]
            rv = [Ta(f"rv{s}", [128, 1], F32) for s in range(2)]
            htok = [Ta(f"htok{s}", [128, 1024], BF16) for s in range(2)]
            wrows = Ta("wrows", [7, S], F32)
            maskA = Ta("maskA_s", [128, 256], BF16)
            maskB = Ta("maskB_s", [128, 768], BF16)
            P.dma("gpsimd", "dwb", lambda g: g.dma_start(maskA[:], maskA_d), writes=["maskA"])
            P.dma("gpsimd", "dwb", lambda g: g.dma_start(maskB[:], maskB_d), writes=["maskB"])
            wstage = [Ta(f"wstage{s}", [128, 4096], BF16) for s in range(2)]
            WT = [Ta(f"WT{s}", [128, 4096], BF16) for s in range(2)]
            qT = [Ta(f"qT{s}", [128, S], BF16) for s in range(2)]
            kT = [Ta(f"kT{s}", [128, S], BF16) for s in range(2)]
            vT = [Ta("vT0", [128, S], BF16)] * 2
            vtok = [Ta(f"vtok{s}", [128, S], BF16) for s in range(2)]
            opair = [Ta(f"opair{s}", [128, S], BF16) for s in range(2)]
            s2b2 = Ta("s2b2", [128, 768], F32)
            nmx2 = Ta("nmx2", [128, 2], F32)
            nm82 = Ta("nm82", [128, 2], F32)
            rz2 = Ta("rz2", [128, 2], F32)
            Pb = [[Ta(f"Pb{h}_{s}", [128, 384], BF16) for s in range(2)] for h in range(2)]
            PT = [Ta(f"PT{h}", [128, 384], BF16) for h in range(2)]
            zz = [Ta(f"zz{h}", [128, 1], F32) for h in range(2)]

            def Xs(hc, lo, n):
                return X[:, hc * S + lo: hc * S + lo + n]

            def ln_pass(src_is_dram, dest_prefix):
                for sb in range(SB):
                    sl = sb % 2
                    xk = f"xin{sl}"
                    if src_is_dram:
                        P.dma("sync", f"dx{sl}",
                              lambda s, sb=sb, sl=sl: s.dma_start(
                                  xin[sl][:], x_d[sb * 128:(sb + 1) * 128, :]),
                              writes=[xk])
                    else:
                        for g in range(2):
                            P.mm_group([
                                (lambda t, hc=hc, sb=sb: t.transpose(
                                    pb[hc // 4][:, (hc % 4) * 128:(hc % 4 + 1) * 128],
                                    Xs(hc, sb * 128, 128), identf[:]))
                                for hc in range(4 * g, 4 * g + 4)],
                                reads=[f"X{hc}" for hc in range(4 * g, 4 * g + 4)] + ["identf"],
                                writes=[f"pb{g}"])
                        for g in range(2):
                            P.op("vector", lambda v, g=g, sl=sl: v.tensor_copy(
                                xin[sl][:, g * 512:(g + 1) * 512], pb[g][:]),
                                reads=[f"pb{g}"], writes=[xk])
                    P.op("vector", lambda v, sl=sl: v.bn_stats(st6[sl][:, 0:6], xin[sl][:, 0:512]),
                         reads=[xk], writes=[f"st6{sl}"])
                    P.op("vector", lambda v, sl=sl: v.bn_stats(st6[sl][:, 6:12], xin[sl][:, 512:1024]),
                         reads=[xk], writes=[f"st6{sl}"])
                    P.op("vector", lambda v, sl=sl: v.bn_aggr(mv[sl][:], st6[sl][:]),
                         reads=[f"st6{sl}"], writes=[f"mv{sl}"])
                    P.op("scalar", lambda s, sl=sl: s.activation(rv[sl][:], mv[sl][:, 1:2],
                                                                 AF.Ln, bias=eps_col[:], scale=1.0),
                         reads=[f"mv{sl}", "eps_col"], writes=[f"rv{sl}"])
                    P.op("scalar", lambda s, sl=sl: s.activation(rv[sl][:], rv[sl][:],
                                                                 AF.Exp, bias=0.0, scale=-0.5),
                         reads=[f"rv{sl}"], writes=[f"rv{sl}"])
                    P.op("vector", lambda v, sl=sl: v.tensor_scalar(
                        htok[sl][:], xin[sl][:], mv[sl][:, 0:1], rv[sl][:],
                        ALU.subtract, ALU.mult),
                        reads=[xk, f"mv{sl}", f"rv{sl}"], writes=[f"htok{sl}"])
                    if src_is_dram:
                        for g in range(2):
                            P.mm_group([
                                (lambda t, hc=hc, sl=sl: t.transpose(
                                    pb[hc // 4][:, (hc % 4) * 128:(hc % 4 + 1) * 128],
                                    xin[sl][:, hc * 128:(hc + 1) * 128], identf[:]))
                                for hc in range(4 * g, 4 * g + 4)],
                                reads=[xk, "identf"], writes=[f"pb{g}"])
                        for g in range(2):
                            dst = X[:, 0:4 * S].rearrange("p (c f) -> p c f", f=S)[
                                :, :, sb * 128:(sb + 1) * 128] if g == 0 else \
                                X[:, 4 * S:8 * S].rearrange("p (c f) -> p c f", f=S)[
                                :, :, sb * 128:(sb + 1) * 128]
                            P.op("vector", lambda v, g=g, dst=dst: v.tensor_copy(
                                dst, pb[g][:].rearrange("p (c f) -> p c f", f=128)),
                                reads=[f"pb{g}"],
                                writes=[f"X{c}" for c in range(4 * g, 4 * g + 4)])
                    if (not src_is_dram) and cfg.sparse and cfg.NRE > 0:
                        P.dma("gpsimd", f"dht{sl}", lambda g, sb=sb, sl=sl: g.dma_start(
                            h2t_d[sb * 128:(sb + 1) * 128, :], htok[sl][:]),
                            reads=[f"htok{sl}"], writes=["h2t"])
                    P.mm_group([
                        (lambda t, hc=hc, sl=sl: t.transpose(
                            pbb[sl][:, hc * 128:(hc + 1) * 128],
                            htok[sl][:, hc * 128:(hc + 1) * 128], identb[:]))
                        for hc in range(8)],
                        reads=[f"htok{sl}", "identb"], writes=[f"pbb{sl}"])
                    dstH = H[:].rearrange("p (c f) -> p c f", f=S)[:, :, sb * 128:(sb + 1) * 128]
                    P.op("vector", lambda v, sl=sl, dstH=dstH: v.tensor_copy(
                        dstH, pbb[sl][:].rearrange("p (c f) -> p c f", f=128)),
                        reads=[f"pbb{sl}"], writes=[f"H{c}" for c in range(8)])

            ln_pass(True, "ln1")

            def emit_transposed_out(src_slicer, dram_ap, src_keys, bf16=False):
                for sb in range(SB):
                    sl = sb % 2
                    if not bf16:
                        for g in range(2):
                            P.mm_group([
                                (lambda t, hc=hc, sb=sb: t.transpose(
                                    pb[hc // 4][:, (hc % 4) * 128:(hc % 4 + 1) * 128],
                                    src_slicer(hc, sb * 128, 128), identf[:]))
                                for hc in range(4 * g, 4 * g + 4)],
                                reads=[src_keys[hc] for hc in range(4 * g, 4 * g + 4)] + ["identf"],
                                writes=[f"pb{g}"])
                        for g in range(2):
                            P.op("vector", lambda v, g=g, sl=sl: v.tensor_copy(
                                xin[sl][:, g * 512:(g + 1) * 512], pb[g][:]),
                                reads=[f"pb{g}"], writes=[f"xin{sl}"])
                    else:
                        P.mm_group([
                            (lambda t, hc=hc, sb=sb, sl=sl: t.transpose(
                                pbb[sl][:, hc * 128:(hc + 1) * 128],
                                src_slicer(hc, sb * 128, 128), identb[:]))
                            for hc in range(8)],
                            reads=list(src_keys) + ["identb"], writes=[f"pbb{sl}"])
                        P.op("vector", lambda v, sl=sl: v.tensor_copy(
                            xin[sl][:], pbb[sl][:]),
                            reads=[f"pbb{sl}"], writes=[f"xin{sl}"])
                    P.dma("scalar", "dout", lambda s, sb=sb, sl=sl: s.dma_start(
                        dram_ap[sb * 128:(sb + 1) * 128, :], xin[sl][:]),
                        reads=[f"xin{sl}"])

            if cfg.stage == "ln1":
                emit_transposed_out(Xs, out_d, [f"X{c}" for c in range(8)])
                emit_transposed_out(Hs, dbg["h"], [f"H{c}" for c in range(8)], bf16=True)
                _finish(P, block)
                return nc

            NP = cfg.NP

            def emit_wdma(hp):
                sl = hp % 2
                wsk = f"wstage{sl}"
                for m, w_d in enumerate([wq_d, wk_d, wv_d]):
                    P.dma("gpsimd", f"dwt{sl}",
                          lambda g, m=m, w_d=w_d, sl=sl, hp=hp: g.dma_start(
                              wstage[sl][:, m * 1024:(m + 1) * 1024],
                              w_d[hp * 128:(hp + 1) * 128, :]),
                          writes=[wsk])
                src_wo = wo_d.rearrange("(c p) f -> p c f", p=128)[:, :, hp * 128:(hp + 1) * 128]
                P.dma("gpsimd", f"dwt{sl}",
                      lambda g, sl=sl, src_wo=src_wo: g.dma_start(
                          wstage[sl][:, 3072:4096].rearrange("p (c f) -> p c f", f=128), src_wo),
                      writes=[wsk])

            def filler_items(hp):
                items = []
                sl = hp % 2
                wsk, wtk = f"wstage{sl}", f"WT{sl}"
                for seg in range(4):
                    def wt_seg(seg=seg, sl=sl, wsk=wsk, wtk=wtk):
                        P.mm_group([
                            (lambda t, seg=seg, c=c, sl=sl: t.transpose(
                                pbb[1][:, c * 128:(c + 1) * 128],
                                wstage[sl][:, seg * 1024 + c * 128: seg * 1024 + (c + 1) * 128],
                                identb[:]))
                            for c in range(8)],
                            reads=[wsk, "identb"], writes=["pbb1"])
                        P.op("scalar", lambda s, seg=seg, sl=sl: s.copy(
                            WT[sl][:, seg * 1024:(seg + 1) * 1024], pbb[1][:]),
                            reads=["pbb1"], writes=[wtk])
                    items.append(wt_seg)
                dstmap = {"qT": qT, "kT": kT, "vT": vT}
                for m, (dstk, bcol) in enumerate([("qT", 0), ("kT", 1), ("vT", 2)]):
                    for n in range(SC):
                        def qkv_one(m=m, n=n, sl=sl, dstk=dstk, bcol=bcol, hp=hp, wtk=wtk):
                            bk = 3 + (m * SC + n) % 2
                            dstb = dstmap[dstk][sl]
                            P.mm_group([
                                (lambda t, c=c, n=n, sl=sl, bk=bk, m=m:
                                 t.matmul(pb[bk][:], WT[sl][:, m * 1024 + c * 128: m * 1024 + (c + 1) * 128],
                                          Hs(c, n * 512, 512), start=(c == 0), stop=(c == 7)))
                                for c in range(8)],
                                reads=[wtk] + [f"H{c}" for c in range(8)], writes=[f"pb{bk}"])
                            P.op("scalar", lambda s, n=n, bk=bk, dstb=dstb, bcol=bcol, hp=hp:
                                 s.activation(dstb[:, n * 512:(n + 1) * 512], pb[bk][:],
                                              AF.Identity, bias=biases[:, bcol * 8 + hp: bcol * 8 + hp + 1],
                                              scale=1.0),
                                 reads=[f"pb{bk}", "biases"], writes=[(dstk + str(sl)) if dstk != "vT" else "vT0"])
                        items.append(qkv_one)
                for g8 in range((SB + 7) // 8):
                    def vtok_g(g8=g8, sl=sl):
                        lo = g8 * 8
                        hi = min(lo + 8, SB)
                        P.mm_group([
                            (lambda t, st=st, lo=lo, sl=sl: t.transpose(
                                pbb[1][:, (st - lo) * 128:(st - lo + 1) * 128],
                                vT[sl][:, st * 128:(st + 1) * 128], identb[:]))
                            for st in range(lo, hi)],
                            reads=["vT0", "identb"], writes=["pbb1"])
                        P.op("scalar", lambda s, lo=lo, hi=hi, sl=sl: s.copy(
                            vtok[sl][:, lo * 128: hi * 128], pbb[1][:, 0:(hi - lo) * 128]),
                            reads=["pbb1"], writes=[f"vtok{sl}"])
                    items.append(vtok_g)
                return items

            def wo_items(hp):
                items = []
                sl = hp % 2
                wtk = f"WT{sl}"
                for oc in range(8):
                    def wo_oc(oc=oc, sl=sl, wtk=wtk, hp=hp):
                        for n in range(SC):
                            wb = 4 + (oc * SC + n) % 2
                            mm(pb[wb][:], WT[sl][:, 3072 + oc * 128: 3072 + (oc + 1) * 128],
                               opair[sl][:, n * 512:(n + 1) * 512],
                               reads=[wtk, f"opair{sl}"], writes=[f"pb{wb}"])
                            if hp == 0:
                                P.op("vector", lambda v, oc=oc, n=n, wb=wb: v.scalar_tensor_tensor(
                                    Xs(oc, n * 512, 512), pb[wb][:], biases[:, 24 + oc: 24 + oc + 1],
                                    Xs(oc, n * 512, 512), ALU.add, ALU.add),
                                    reads=[f"pb{wb}", "biases", f"X{oc}"], writes=[f"X{oc}"])
                            else:
                                P.op("vector", lambda v, oc=oc, n=n, wb=wb: v.tensor_tensor(
                                    Xs(oc, n * 512, 512), Xs(oc, n * 512, 512), pb[wb][:], ALU.add),
                                    reads=[f"pb{wb}", f"X{oc}"], writes=[f"X{oc}"])
                    items.append(wo_oc)
                return items

            def emit_scores(hp, qt):
                sl = hp % 2
                kts = [0] if qt == 0 else [0, qt - 1, qt]
                fns = []
                for h in range(2):
                    base = 64 * h
                    for j, kt in enumerate(kts):
                        fns.append(lambda t, h=h, base=base, j=j, kt=kt, qt=qt, sl=sl:
                                   t.matmul(psc[:, h * 512 + j * 128: h * 512 + (j + 1) * 128],
                                            qT[sl][base:base + 64, qt * 128:(qt + 1) * 128],
                                            kT[sl][base:base + 64, kt * 128:(kt + 1) * 128],
                                            start=True, stop=True))
                P.mm_group(fns, reads=[f"qT{sl}", f"kT{sl}"], writes=["pb0", "pb1"])

            def emit_softmax(hp, qt):
                W = 128 if qt == 0 else 384
                msk, mskk = (maskA, "maskA") if qt == 0 else (maskB, "maskB")
                par = qt % 2
                psc2 = psc[:].rearrange("p (h f) -> p h f", f=512)[:, :, 0:W]
                P.op("vector", lambda v, W=W, psc2=psc2: v.tensor_reduce(
                    nmx2[:], psc2, AX.X, ALU.max, negate=True),
                    reads=["pb0", "pb1"], writes=["nmx2"])
                P.op("vector", lambda v: v.tensor_scalar(
                    nm82[:], nmx2[:], 0.125, None, ALU.mult, ALU.bypass),
                    reads=["nmx2"], writes=["nm82"])
                mskw = msk[:].rearrange("p (h f) -> p h f", h=2)[:, :, 0:W]
                P.op("vector", lambda v, W=W, mskw=mskw, psc2=psc2: v.scalar_tensor_tensor(
                    s2b2[:].rearrange("p (h f) -> p h f", f=384)[:, :, 0:W], psc2, 0.125,
                    mskw, ALU.mult, ALU.add),
                    reads=["pb0", "pb1", mskk], writes=["s2b2"])
                for h in range(2):
                    P.op("scalar", lambda s, h=h, W=W, par=par: s.activation(
                        Pb[h][par][:, 0:W], s2b2[:, h * 384: h * 384 + W], AF.Exp,
                        bias=nm82[:, h: h + 1], scale=1.0, accum_out=zz[h][:]),
                        reads=["s2b2", "nm82"], writes=[f"Pb{h}_{par}", f"zz{h}"])
                P.op("vector", lambda v: v.reciprocal(rz2[:, 0:1], zz[0][:]),
                     reads=["zz0"], writes=["rz2"])
                P.op("vector", lambda v: v.reciprocal(rz2[:, 1:2], zz[1][:]),
                     reads=["zz1"], writes=["rz2"])
                for h in range(2):
                    P.op("vector", lambda v, h=h, W=W, par=par: v.tensor_scalar(
                        Pb[h][par][:, 0:W], Pb[h][par][:, 0:W], rz2[:, h: h + 1], None,
                        ALU.mult, ALU.bypass),
                        reads=[f"Pb{h}_{par}", "rz2"], writes=[f"Pb{h}_{par}"])

            def emit_ptpv(hp, qt):
                sl = hp % 2
                kts = [0] if qt == 0 else [0, qt - 1, qt]
                nk = len(kts)
                W = 128 * nk
                par = qt % 2
                P.mm_group([
                    (lambda t, h=h, j=j, par=par: t.transpose(
                        pbb[0][:, (h * 4 + j) * 128:(h * 4 + j + 1) * 128],
                        Pb[h][par][:, j * 128:(j + 1) * 128], identb[:]))
                    for h in range(2) for j in range(nk)],
                    reads=[f"Pb0_{par}", f"Pb1_{par}", "identb"], writes=["pbb0"])
                for h in range(2):
                    base = 64 * h
                    P.op("scalar", lambda s, h=h, W=W: s.copy(PT[h][:, 0:W],
                                                              pbb[0][:, h * 512: h * 512 + W]),
                         reads=["pbb0"], writes=[f"PT{h}"])
                for h in range(2):
                    base = 64 * h
                    P.mm_group([
                        (lambda t, h=h, j=j, kt=kt, qt=qt, base=base, nk=nk, sl=sl:
                         t.matmul(pb[2][base:base + 64, (qt % 4) * 128:(qt % 4 + 1) * 128],
                                  vtok[sl][:, kt * 128 + base: kt * 128 + base + 64],
                                  PT[h][:, j * 128:(j + 1) * 128],
                                  start=(j == 0), stop=(j == nk - 1)))
                        for j, kt in enumerate(kts)],
                        reads=[f"vtok{sl}", f"PT{h}"], writes=[f"pb2_{h}_{qt % 4}"])
                if qt % 4 == 3 or qt == SB - 1:
                    qlo = (qt // 4) * 4
                    nq = qt - qlo + 1
                    P.op("scalar", lambda s, qlo=qlo, nq=nq, sl=sl, qt=qt: s.copy(
                        opair[sl][:, qlo * 128:(qlo + nq) * 128], pb[2][:, 0:nq * 128]),
                        reads=[f"pb2_{h}_{q % 4}" for h in range(2) for q in range(qlo, qt + 1)],
                        writes=[f"opair{sl}"])

            _attn_drivers = [] if getattr(cfg, "skip_attn", False) else [1]
            for _ in _attn_drivers:
                emit_wdma(0)
                for it in filler_items(0):
                    it()
                for hp in range(NP):
                    if hp + 1 < NP:
                        emit_wdma(hp + 1)
                    queue = []
                    if hp >= 1:
                        queue += wo_items(hp - 1)
                    if hp + 1 < NP:
                        queue += filler_items(hp + 1)
                    prev = None
                    popped = 0
                    for qt in range(SB):
                        emit_scores(hp, qt)
                        emit_softmax(hp, qt)
                        if prev is not None:
                            emit_ptpv(hp, prev)
                        want = (qt + 1) * len(queue) // SB
                        while popped < want:
                            queue[popped]()
                            popped += 1
                        prev = qt
                    emit_ptpv(hp, SB - 1)
                    while popped < len(queue):
                        queue[popped]()
                        popped += 1
                for it in wo_items(NP - 1):
                    it()
            if cfg.stage == "attn":
                emit_transposed_out(Xs, out_d, [f"X{c}" for c in range(8)])
                emit_transposed_out(Hs, dbg["h"], [f"H{c}" for c in range(8)], bf16=True)
                _finish(P, block)
                return nc

            ln_pass(False, "ln2")

            for sb in range(SB):
                P.mm_group([
                    (lambda t, c=c, sb=sb: t.matmul(
                        pb[2][:, 0:7], Hs(c, sb * 128, 128), wrT[:, c * 7:(c + 1) * 7],
                        start=(c == 0), stop=(c == 7)))
                    for c in range(8)],
                    reads=[f"H{c}" for c in range(8)] + ["wrT"], writes=["pb2"])
                P.op("vector", lambda v: v.tensor_tensor(lbuf[:], pb[2][:, 0:7], rbbc[:], ALU.add),
                     reads=["pb2", "rbbc"], writes=["lbuf"])
                P.op("vector", lambda v: v.tensor_reduce(r1[:], lbuf[:], AX.X, ALU.max, negate=True),
                     reads=["lbuf"], writes=["r1"])
                P.op("vector", lambda v: v.tensor_scalar(tbuf[:], lbuf[:], r1[:], None, ALU.add, ALU.bypass),
                     reads=["lbuf", "r1"], writes=["tbuf"])
                P.op("vector", lambda v, sb=sb: v.tensor_scalar(
                    ismaxB[:, sb * 7:(sb + 1) * 7], tbuf[:], 0.0, None, ALU.is_ge, ALU.bypass),
                    reads=["tbuf"], writes=["ismaxB"])
                P.op("vector", lambda v, sb=sb: v.scalar_tensor_tensor(
                    l2buf[:], ismaxB[:, sb * 7:(sb + 1) * 7], NEG, tbuf[:], ALU.mult, ALU.add),
                    reads=["ismaxB", "tbuf"], writes=["l2buf"])
                P.op("vector", lambda v: v.tensor_reduce(r2[:], l2buf[:], AX.X, ALU.max, negate=True),
                     reads=["l2buf"], writes=["r2"])
                P.op("vector", lambda v, sb=sb: v.tensor_scalar(
                    top2B[:, sb * 7:(sb + 1) * 7], tbuf[:], r2[:], 0.0, ALU.add, ALU.is_ge),
                    reads=["tbuf", "r2"], writes=["top2B"])
                P.op("scalar", lambda s: s.activation(ebuf[:], tbuf[:], AF.Exp, bias=0.0, scale=1.0),
                     reads=["tbuf"], writes=["ebuf"])
                P.op("vector", lambda v, sb=sb: v.tensor_tensor(
                    ebuf[:], ebuf[:], top2B[:, sb * 7:(sb + 1) * 7], ALU.mult),
                    reads=["ebuf", "top2B"], writes=["ebuf"])
                P.op("vector", lambda v: v.tensor_reduce(r1[:], ebuf[:], AX.X, ALU.add),
                     reads=["ebuf"], writes=["r1"])
                P.op("vector", lambda v: v.reciprocal(r2[:], r1[:]), reads=["r1"], writes=["r2"])
                P.op("vector", lambda v, sb=sb: v.tensor_scalar(
                    wtok[:, sb * 7:(sb + 1) * 7], ebuf[:], r2[:], None, ALU.mult, ALU.bypass),
                    reads=["ebuf", "r2"], writes=["wtok"])
                P.op("vector", lambda v, sb=sb: v.scalar_tensor_tensor(
                    mbuf[:], wtok[:, sb * 7:(sb + 1) * 7], 1.0, ismaxB[:, sb * 7:(sb + 1) * 7],
                    ALU.mult, ALU.mult, accum_out=W0B[:, sb:sb + 1]),
                    reads=["wtok", "ismaxB"], writes=["mbuf", "W0B"])
                P.op("vector", lambda v, sb=sb: v.tensor_scalar(
                    W1B[:, sb:sb + 1], W0B[:, sb:sb + 1], -1.0, 1.0, ALU.mult, ALU.add),
                    reads=["W0B"], writes=["W1B"])
            for sb in range(SB):
                bk = (sb // 4) % 2
                P.mm_group([lambda t, sb=sb, bk=bk: t.transpose(
                    pb[bk][0:7, (sb % 4) * 128:(sb % 4 + 1) * 128],
                    wtok[:, sb * 7:(sb + 1) * 7], identf[:])],
                    reads=["wtok", "identf"], writes=[f"pb{bk}"])
                if sb % 4 == 3 or sb == SB - 1:
                    lo = (sb // 4) * 4
                    nqq = (sb - lo + 1) * 128
                    P.op("vector", lambda v, lo=lo, nqq=nqq, bk=bk: v.tensor_copy(
                        wrows[0:7, lo * 128: lo * 128 + nqq], pb[bk][0:7, 0:nqq]),
                        reads=[f"pb{bk}"], writes=["wrows"])

            P.op("vector", lambda v: v.tensor_scalar(selm[0:7, :], wrows[0:7, :], 0.0, None,
                                                     ALU.is_gt, ALU.bypass),
                 reads=["wrows"], writes=["selm"])
            if cfg.stage == "router":
                P.dma("scalar", "dout", lambda s: s.dma_start(dbg["w"], wtok[:]), reads=["wtok"])
                emit_transposed_out(Hs, dbg["h"], [f"H{c}" for c in range(8)], bf16=True)
                emit_transposed_out(Xs, out_d, [f"X{c}" for c in range(8)])
                _finish(P, block)
                return nc

            for hc in range(8):
                P.dma("scalar", "dxw", lambda s, hc=hc: s.dma_start(
                    x1_d[hc], Xs(hc, 0, S)),
                    reads=[f"X{hc}"], writes=[f"x1_{hc}_{n}" for n in range(SC)])
        P.fence()

        with ExitStack() as es_f:
            Tf = lambda nm, sh, dt=F32: es_f.enter_context(nc.sbuf_tensor(nm, sh, dt))
            tbig = Tf("tbig", [128, NI * S], BF16)
            es_f1 = es_f.enter_context(ExitStack())
            Tf1 = lambda nm, sh, dt=F32: es_f1.enter_context(nc.sbuf_tensor(nm, sh, dt))
            sgstage = [Tf1(f"sgstage{s}", [128, 2048], BF16) for s in range(2)]
            wTf = [Tf1(f"wTf{s}", [128, 2048], BF16) for s in range(2)]
            sdstage = [Tf1(f"sdstage{s}", [128, 2816], BF16) for s in range(2)]
            sdT = [Tf1(f"sdT{s}", [128, 2816], BF16) for s in range(2)]
            sil = Tf1("sil", [128, 512], F32)
            rin = [Tf1("rin0", [128, 512], F32)] * 2
            rout = [Tf1("rout0", [128, 512], F32)] * 2
            CW = C // 16
            Hcmp = Tf1("Hcmp", [128, 8 * C], BF16)
            arrb = Tf1("arrb", [16, S // 16 + CW], F32)
            tokid1 = Tf1("tokid1_s", [16, 128], F32)
            postens = Tf1("postens", [7, S], F32)
            P.op("vector", lambda v: v.tensor_tensor_scan(
                postens[0:7, :], selm[0:7, :], selm[0:7, :], 0.0, ALU.add, ALU.bypass),
                reads=["selm"], writes=["postens"])
            P.dma("sync", "dcst", lambda s: s.dma_start(tokid1[:], tokid1_d), writes=["tokid1"])
            idxf = Tf1("idxf", [16, S // 16 + CW], F32)
            nfb = es_f1.enter_context(nc.sbuf_tensor("nfb", [1, 1], mybir.dt.uint32))
            idxall = es_f1.enter_context(nc.sbuf_tensor("idxall", [128, 7 * CW], mybir.dt.int16))
            ptok = Tf1("ptok", [128, 7], F32)
            ubuf = Tf1("ubuf", [128, 7], F32)
            m2b = Tf1("m2b", [128, 7], F32)
            jnk = Tf1("jnk", [128, 7], F32)
            F0 = Tf1("F0", [128, SB], F32)
            F1 = Tf1("F1", [128, SB], F32)
            T1s = Tf1("T1s", [16, 128], F32)
            g16 = Tf1("g16", [16, 256], F32)
            idxg = es_f1.enter_context(nc.sbuf_tensor("idxg", [128, 256], mybir.dt.int16))
            ycst = [Tf1(f"ycst{s}", [128, 512], BF16) for s in range(2)]
            ydst = [Tf1(f"ydst{s}", [128, 512], BF16) for s in range(2)]

            from concourse import library_config as _lc

            if cfg.sparse and cfg.NRE > 0:
                P.op("vector", lambda v: v.memset(ycst[0][:], 0.0), writes=["ycst0"])
                for half in range(2):
                    P.dma("gpsimd", "dwb", lambda g, half=half: g.dma_start(
                        yc_d[7 * C:7 * C + 1, half * 512:(half + 1) * 512],
                        ycst[0][0:1, 0:512]),
                        reads=["ycst0"], writes=["yczero"])
                P.op("vector", lambda v: v.memset(arrb[0:16, S // 16: S // 16 + CW], 0.0),
                     writes=["arrtail"])
                P.op("vector", lambda v: v.memset(idxf[:], 0.0), writes=["idxf"])
                _pc = cfg.prep_cut
                _lvl = _pc if _pc < 20 else (2 if _pc <= 29 else 4)
                _n_comp = 0 if _lvl < 2 else (_pc - 20 if 20 < _pc < 28 else cfg.NRE)
                _do_sg = _pc not in (28, 29, 58)
                if _lvl >= 2:
                    _lib2 = _lc.attnmlp if _pc == 29 else _lc.sparse_gather
                    P.op("gpsimd", lambda g, _lib2=_lib2: (g.load_library(_lib2),
                                                           g.nop())[1],
                         reads=["yczero", "h2t"], writes=["gplib"])
                if _pc == 29:
                    ctrlg = Tf1("ctrlg", [128, 1024], BF16)
                    idxc = es_f1.enter_context(
                        nc.sbuf_tensor("idxc", [128, 8], mybir.dt.int16))
                    P.op("vector", lambda v: v.memset(idxc[:], 0), writes=["idxc"])
                    P.dma("gpsimd", "dhc0", lambda g: g.dma_gather(
                        ctrlg[:].rearrange("p (b f) -> p b f", b=1),
                        h2t_d, idxc[:], 128, 128, 1024),
                        reads=["h2t", "idxc", "gplib"], writes=["ctrlg"])
                for e in range(_n_comp):
                    selap = top2B[:].rearrange("p (s e) -> p e s", e=7)[:, e, :]
                    P.mm_group([lambda t, selap=selap: t.transpose(
                        pb[2][0:SB, 0:128], selap, identf[:])],
                        reads=["top2B", "identf"], writes=["pb2"])
                    P.op("vector", lambda v: v.tensor_copy(T1s[0:SB, :], pb[2][0:SB, 0:128]),
                         reads=["pb2"], writes=["T1s"])
                    for pd in range(8):
                        P.mm_group([lambda t, pd=pd: t.transpose(
                            pb[3][0:16, 0:SB], T1s[0:SB, pd * 16:(pd + 1) * 16],
                            identf[0:SB, 0:SB])],
                            reads=["T1s", "identf"], writes=["pb3"])
                        dstw = arrb[0:16, 0:S // 16].rearrange("q (s d) -> q s d", d=8)[:, :, pd:pd + 1]
                        P.op("vector", lambda v, pd=pd, dstw=dstw: v.tensor_copy(
                            dstw, pb[3][0:16, 0:SB].rearrange("q (s d) -> q s d", d=1)),
                            reads=["pb3"], writes=["arrb"])
                    P.op("vector", lambda v: v.tensor_tensor(
                        arrb[0:16, 0:S // 16], arrb[0:16, 0:S // 16],
                        tokid1[0:16, 0:S // 16], ALU.mult),
                        reads=["arrb", "tokid1"], writes=["arrb"])
                    P.op("vector", lambda v: v.tensor_scalar(
                        arrb[0:16, 0:S // 16], arrb[0:16, 0:S // 16], -1.0, None, ALU.add, ALU.bypass),
                        reads=["arrb"], writes=["arrb"])
                    if _do_sg:
                        P.op("gpsimd", lambda g: g.sparse_gather(
                            idxf[0:16, :], arrb[0:16, :], num_found=nfb[:]),
                            reads=["arrb", "arrtail", "gplib"], writes=["idxf", "nfb"])
                    P.op("vector", lambda v, e=e: v.tensor_copy(
                        idxall[0:16, e * CW:(e + 1) * CW], idxf[0:16, 0:CW]),
                        reads=["idxf"], writes=["idxall"])
                for gpart in range(1, 8 if _lvl >= 3 else 1):
                    P.dma("sync", "dcst", lambda s, gpart=gpart: s.dma_start(
                        idxall[gpart * 16:(gpart + 1) * 16, 0:cfg.NRE * CW],
                        idxall[0:16, 0:cfg.NRE * CW]),
                        reads=["idxall"], writes=["idxall"])
                for sb in range(SB if _lvl >= 4 else 0):
                    P.mm_group([lambda t, sb=sb: t.transpose(
                        pb[3][0:128, 0:7], postens[0:7, sb * 128:(sb + 1) * 128], identf[0:7, 0:7])],
                        reads=["postens", "identf"], writes=["pb3"])
                    P.op("vector", lambda v: v.tensor_tensor(
                        ptok[:], pb[3][0:128, 0:7], iotac2[:], ALU.add),
                        reads=["pb3", "iotac2"], writes=["ptok"])
                    P.op("vector", lambda v, sb=sb: v.tensor_tensor(
                        m2b[:], top2B[:, sb * 7:(sb + 1) * 7], ismaxB[:, sb * 7:(sb + 1) * 7],
                        ALU.subtract),
                        reads=["top2B", "ismaxB"], writes=["m2b"])
                    P.op("vector", lambda v, sb=sb: v.scalar_tensor_tensor(
                        jnk[:], ptok[:], 1.0, ismaxB[:, sb * 7:(sb + 1) * 7], ALU.mult, ALU.mult,
                        accum_out=F0[:, sb:sb + 1]),
                        reads=["ptok", "ismaxB"], writes=["jnk", "F0"])
                    P.op("vector", lambda v, sb=sb: v.scalar_tensor_tensor(
                        jnk[:], ptok[:], 1.0, m2b[:], ALU.mult, ALU.mult,
                        accum_out=F1[:, sb:sb + 1]),
                        reads=["ptok", "m2b"], writes=["jnk", "F1"])
                    P.op("vector", lambda v, sb=sb: v.scalar_tensor_tensor(
                        jnk[:], pb[3][0:128, 0:7], 1.0, ismaxB[:, sb * 7:(sb + 1) * 7],
                        ALU.mult, ALU.mult, accum_out=r1[:]),
                        reads=["pb3", "ismaxB"], writes=["jnk", "r1"])
                    P.op("vector", lambda v: v.tensor_scalar(r1[:], r1[:], C + 0.5, None,
                                                             ALU.is_le, ALU.bypass),
                         reads=["r1"], writes=["r1"])
                    P.op("vector", lambda v, sb=sb: v.tensor_scalar(
                        F0[:, sb:sb + 1], F0[:, sb:sb + 1], -7.0 * C, r1[:], ALU.add, ALU.mult),
                        reads=["F0", "r1"], writes=["F0"])
                    P.op("vector", lambda v, sb=sb: v.tensor_scalar(
                        F0[:, sb:sb + 1], F0[:, sb:sb + 1], 7.0 * C, None, ALU.add, ALU.bypass),
                        reads=["F0"], writes=["F0"])
                    P.op("vector", lambda v, sb=sb: v.scalar_tensor_tensor(
                        jnk[:], pb[3][0:128, 0:7], 1.0, m2b[:], ALU.mult, ALU.mult,
                        accum_out=r2[:]),
                        reads=["pb3", "m2b"], writes=["jnk", "r2"])
                    P.op("vector", lambda v: v.tensor_scalar(r2[:], r2[:], C + 0.5, None,
                                                             ALU.is_le, ALU.bypass),
                         reads=["r2"], writes=["r2"])
                    P.op("vector", lambda v, sb=sb: v.tensor_scalar(
                        F1[:, sb:sb + 1], F1[:, sb:sb + 1], -7.0 * C, r2[:], ALU.add, ALU.mult),
                        reads=["F1", "r2"], writes=["F1"])
                    P.op("vector", lambda v, sb=sb: v.tensor_scalar(
                        F1[:, sb:sb + 1], F1[:, sb:sb + 1], 7.0 * C, None, ALU.add, ALU.bypass),
                        reads=["F1"], writes=["F1"])
                for slot, FB in enumerate([F0, F1] if _lvl >= 4 else []):
                    fk = "F0" if slot == 0 else "F1"
                    P.mm_group([lambda t, FB=FB: t.transpose(
                        pb[2][0:SB, 0:128], FB[:, 0:SB], identf[:])],
                        reads=[fk, "identf"], writes=["pb2"])
                    P.op("vector", lambda v: v.tensor_copy(T1s[0:SB, :], pb[2][0:SB, 0:128]),
                         reads=["pb2"], writes=["T1s"])
                    for pd in range(8):
                        P.mm_group([lambda t, pd=pd: t.transpose(
                            pb[3][0:16, 0:16], T1s[0:16, pd * 16:(pd + 1) * 16], identf[0:16, 0:16])],
                            reads=["T1s", "identf"], writes=["pb3"])
                        dst = g16[0:16, slot * 128: (slot + 1) * 128].rearrange(
                            "q (s d) -> q s d", d=8)[:, :, pd:pd + 1]
                        P.op("vector", lambda v, pd=pd, dst=dst: v.tensor_copy(
                            dst, pb[3][0:16, 0:16].rearrange("q (s d) -> q s d", d=1)),
                            reads=["pb3"], writes=["g16"])
                    P.op("vector", lambda v, slot=slot: v.tensor_copy(
                        idxg[0:16, slot * 128:(slot + 1) * 128], g16[0:16, slot * 128:(slot + 1) * 128]),
                        reads=["g16"], writes=["idxg"])
                for gpart in range(1, 8 if _lvl >= 4 else 1):
                    P.dma("sync", "dcst", lambda s, gpart=gpart: s.dma_start(
                        idxg[gpart * 16:(gpart + 1) * 16, :], idxg[0:16, :]),
                        reads=["idxg"], writes=["idxg"])
                if _lvl >= 4:
                    P.op("gpsimd", lambda g: (g.load_library(_lc.attnmlp),
                                              g.nop())[1],
                         reads=["yczero", "h2t"], writes=["gplib"])

            def emit_x1_dump():
                outflat = out_d.rearrange("s f -> (s f)")
                for hc in range(8):
                    dst = outflat[hc * 128 * S:(hc + 1) * 128 * S].rearrange(
                        "(p s) -> p s", p=128)
                    P.dma("scalar", "dout", lambda s, dst=dst, hc=hc: s.dma_start(
                        dst, x1_d[hc]),
                        reads=[f"x1_{hc}_{n}" for n in range(SC)])

            if cfg.stage == "sprep":
                emit_x1_dump()
                _finish(P, block, all_sems=True)
                return nc

            n_exp = 1 + cfg.NRE
            for e in range(n_exp):
                if e == 0:
                    g_src, u_src, d_src = sg_d, su_d, sd_d
                else:
                    g_src, u_src, d_src = rg_d[e - 1], ru_d[e - 1], rd_d[e - 1]
                    if cfg.sparse:
                        for h in range(2):
                            P.dma("gpsimd", f"dhc{h}", lambda g, e=e, h=h: g.dma_gather(
                                Hcmp[:, h * 4096:(h + 1) * 4096].rearrange("p (b j) -> p b j", j=512),
                                h2t_d, idxall[:, (e - 1) * CW + h * 32: (e - 1) * CW + h * 32 + 32],
                                512, 512, 1024, transpose=True),
                                reads=["h2t", "idxall", "gplib"], writes=[f"Hcmp{h}"])
                def emit_sg_dma(i, g_src=g_src, u_src=u_src):
                    sl = i % 2
                    sgk = f"sgstage{sl}"
                    P.dma("gpsimd", f"dsg{sl}", lambda g, i=i, sl=sl, g_src=g_src: g.dma_start(
                        sgstage[sl][:, 0:1024], g_src[i * 128:(i + 1) * 128, :]), writes=[sgk])
                    P.dma("gpsimd", f"dsg{sl}", lambda g, i=i, sl=sl, u_src=u_src: g.dma_start(
                        sgstage[sl][:, 1024:2048], u_src[i * 128:(i + 1) * 128, :]), writes=[sgk])

                def emit_sg_transpose(i):
                    sl = i % 2
                    sgk = f"sgstage{sl}"
                    wtk = f"wTf{sl}"
                    for m in range(2):
                        P.mm_group([
                            (lambda t, m=m, c=c, sl=sl: t.transpose(
                                pbb[m][:, c * 128:(c + 1) * 128],
                                sgstage[sl][:, m * 1024 + c * 128: m * 1024 + (c + 1) * 128],
                                identb[:]))
                            for c in range(8)],
                            reads=[sgk, "identb"], writes=[f"pbb{m}"])
                        P.op("scalar", lambda s, m=m, sl=sl: s.copy(
                            wTf[sl][:, m * 1024:(m + 1) * 1024], pbb[m][:]),
                            reads=[f"pbb{m}"], writes=[wtk])

                sparse_e = cfg.sparse and e > 0
                NTOK = C if sparse_e else S
                NSC = NTOK // 512
                if sparse_e:
                    def h_rhs(c, n):
                        return Hcmp[:, n * 4096 + c * 512: n * 4096 + (c + 1) * 512]

                    def h_keys(n):
                        return [f"Hcmp{n}"]
                else:
                    def h_rhs(c, n):
                        return Hs(c, n * 512, 512)

                    def h_keys(n):
                        return [f"H{c}" for c in range(8)]

                def emit_gateup(i, NSC=NSC, h_rhs=h_rhs, h_keys=h_keys):
                    sl = i % 2
                    wtk = f"wTf{sl}"
                    for n in range(NSC):
                        gb, ub = n % 2, 2 + n % 2
                        P.mm_group([
                            (lambda t, c=c, n=n, sl=sl, gb=gb: t.matmul(
                                pb[gb][:], wTf[sl][:, c * 128:(c + 1) * 128],
                                h_rhs(c, n), start=(c == 0), stop=(c == 7)))
                            for c in range(8)],
                            reads=[wtk] + h_keys(n), writes=[f"pb{gb}"])
                        P.mm_group([
                            (lambda t, c=c, n=n, sl=sl, ub=ub: t.matmul(
                                pb[ub][:], wTf[sl][:, 1024 + c * 128: 1024 + (c + 1) * 128],
                                h_rhs(c, n), start=(c == 0), stop=(c == 7)))
                            for c in range(8)],
                            reads=[wtk] + h_keys(n), writes=[f"pb{ub}"])
                        P.op("scalar", lambda s, gb=gb: s.activation(sil[:], pb[gb][:], AF.Silu,
                                                                     bias=0.0, scale=1.0),
                             reads=[f"pb{gb}"], writes=["sil"])
                        P.op("vector", lambda v, i=i, n=n, ub=ub: v.tensor_tensor(
                            tbig[:, i * S + n * 512: i * S + (n + 1) * 512], sil[:], pb[ub][:],
                            ALU.mult),
                            reads=["sil", f"pb{ub}"], writes=[f"t{i}"])

                emit_sg_dma(0)
                emit_sg_dma(1)
                for i in range(NI):
                    emit_sg_transpose(i)
                    if i + 2 < NI:
                        emit_sg_dma(i + 2)
                    if i >= 1:
                        emit_gateup(i - 1)
                emit_gateup(NI - 1)
                def emit_sd_dma(hc, d_src=d_src):
                    sl = hc % 2
                    P.dma("gpsimd", f"dsd{sl}", lambda g, hc=hc, sl=sl, d_src=d_src: g.dma_start(
                        sdstage[sl][:], d_src[hc * 128:(hc + 1) * 128, :]),
                        writes=[f"sdstage{sl}"])

                def emit_sd_transpose(hc):
                    sl = hc % 2
                    sdk, sdtk = f"sdstage{sl}", f"sdT{sl}"
                    for grp in range(3):
                        lo = grp * 8
                        hi = min(lo + 8, NI)
                        bkk = grp % 2
                        P.mm_group([
                            (lambda t, i=i, lo=lo, sl=sl, bkk=bkk: t.transpose(
                                pbb[bkk][:, (i - lo) * 128:(i - lo + 1) * 128],
                                sdstage[sl][:, i * 128:(i + 1) * 128], identb[:]))
                            for i in range(lo, hi)],
                            reads=[sdk, "identb"], writes=[f"pbb{bkk}"])
                        P.op("scalar", lambda s, lo=lo, hi=hi, sl=sl, bkk=bkk: s.copy(
                            sdT[sl][:, lo * 128: hi * 128], pbb[bkk][:, 0:(hi - lo) * 128]),
                            reads=[f"pbb{bkk}"], writes=[sdtk])

                def emit_down(hc, NSC=NSC, sparse_e=sparse_e, e=e):
                    sl = hc % 2
                    sdtk = f"sdT{sl}"
                    for n in range(NSC):
                        dbk = [4, 5, 2, 3][n % 4]
                        rsl = (hc * NSC + n) % 2
                        if not sparse_e:
                            P.dma("sync", "dri0", lambda s, hc=hc, n=n: s.dma_start(
                                rin[0][:], x1_d[hc][:, n * 512:(n + 1) * 512]),
                                reads=[f"x1_{hc}_{n}"], writes=["rin0"])
                        P.mm_group([
                            (lambda t, i=i, n=n, sl=sl, dbk=dbk: t.matmul(
                                pb[dbk][:], sdT[sl][:, i * 128:(i + 1) * 128],
                                tbig[:, i * S + n * 512: i * S + (n + 1) * 512],
                                start=(i == 0), stop=(i == NI - 1)))
                            for i in range(NI)],
                            reads=[sdtk] + [f"t{i}" for i in range(NI)], writes=[f"pb{dbk}"])
                        if not sparse_e:
                            P.op("vector", lambda v, n=n, dbk=dbk: v.tensor_tensor(
                                rout[0][:], rin[0][:], pb[dbk][:], ALU.add),
                                reads=["rin0", f"pb{dbk}"], writes=["rout0"])
                            P.dma("scalar", "dro0", lambda s, hc=hc, n=n: s.dma_start(
                                x1_d[hc][:, n * 512:(n + 1) * 512], rout[0][:]),
                                reads=["rout0"], writes=[f"x1_{hc}_{n}"])
                        else:
                            P.op("scalar", lambda s, n=n, dbk=dbk, rsl=rsl: s.copy(
                                ycst[rsl][:], pb[dbk][:]),
                                reads=[f"pb{dbk}"], writes=[f"ycst{rsl}"])
                            P.mm_group([
                                (lambda t, j=j, rsl=rsl: t.transpose(
                                    pbb[0][:, j * 128:(j + 1) * 128],
                                    ycst[rsl][:, j * 128:(j + 1) * 128], identb[:]))
                                for j in range(4)],
                                reads=[f"ycst{rsl}", "identb"], writes=["pbb0"])
                            P.op("scalar", lambda s, rsl=rsl: s.copy(
                                ydst[rsl][:], pbb[0][:, 0:512]),
                                reads=["pbb0"], writes=[f"ydst{rsl}"])
                            dst = yc_d[(e - 1) * C + n * 512:(e - 1) * C + (n + 1) * 512,
                                       hc * 128:(hc + 1) * 128].rearrange(
                                "(j p) f -> p j f", p=128)
                            P.dma("scalar", f"dro{rsl}", lambda s, dst=dst, rsl=rsl: s.dma_start(
                                dst, ydst[rsl][:].rearrange("p (j f) -> p j f", f=128)),
                                reads=[f"ydst{rsl}"], writes=[f"yc_{e}_{hc}_{n}"])

                emit_sd_dma(0)
                emit_sd_dma(1)
                for hc in range(8):
                    emit_sd_transpose(hc)
                    if hc + 2 < 8:
                        emit_sd_dma(hc + 2)
                    if hc >= 1:
                        emit_down(hc - 1)
                emit_down(7)

            if cfg.stage == "sexp":
                emit_x1_dump()
                _finish(P, block, all_sems=True)
                return nc

            all_yc = ([f"yc_{e}_{hc}_{n}" for e in range(1, n_exp) for hc in range(8)
                       for n in range(C // 512)] + ["yczero"]) if (cfg.sparse and cfg.NRE > 0) else []
            es_f1.close()
            P.fence()
            rinbig = [Tf(f"rinbig{s}", [128, 1024], F32) for s in range(2)]
            routbig = [Tf(f"routbig{s}", [128, 1024], F32) for s in range(2)]
            ygat2 = [[Tf(f"yg{sl}_{s}", [128, 1024], BF16) for s in range(2)] for sl in range(2)]
            sparse_fin = cfg.sparse and cfg.NRE > 0
            for sb in range(SB):
                sl = sb % 2
                if sparse_fin:
                    for slot in range(2):
                        P.dma("gpsimd", f"dga{slot}", lambda g, sb=sb, sl=sl, slot=slot: g.dma_gather(
                            ygat2[slot][sl][:].rearrange("p (b f) -> p b f", b=1),
                            yc_d, idxg[:, slot * 128 + sb * 8: slot * 128 + (sb + 1) * 8],
                            128, 128, 1024),
                            reads=["idxg"], writes=[f"yg{slot}{sl}"])
                src_fin = x1_d.rearrange("c p f -> p c f")[:, :, sb * 128:(sb + 1) * 128]
                P.dma("sync", f"dfi{sl}", lambda s, sl=sl, src_fin=src_fin: s.dma_start(
                    rinbig[sl][:].rearrange("p (c f) -> p c f", f=128), src_fin),
                    reads=[f"x1_{hc}_{n}" for hc in range(8) for n in range(SC)],
                    writes=[f"rinbig{sl}"])
                for g in range(2):
                    P.mm_group([
                        (lambda t, hc=hc, sl=sl: t.transpose(
                            pb[hc // 4][:, (hc % 4) * 128:(hc % 4 + 1) * 128],
                            rinbig[sl][:, hc * 128:(hc + 1) * 128], identf[:]))
                        for hc in range(4 * g, 4 * g + 4)],
                        reads=[f"rinbig{sl}", "identf"], writes=[f"pb{g}"])
                for g in range(2):
                    P.op("vector", lambda v, g=g, sl=sl: v.tensor_copy(
                        routbig[sl][:, g * 512:(g + 1) * 512], pb[g][:]),
                        reads=[f"pb{g}"], writes=[f"routbig{sl}"])
                if sparse_fin:
                    P.op("vector", lambda v, sb=sb, sl=sl: v.scalar_tensor_tensor(
                        routbig[sl][:], ygat2[0][sl][:], W0B[:, sb:sb + 1], routbig[sl][:],
                        ALU.mult, ALU.add),
                        reads=[f"yg0{sl}", "W0B", f"routbig{sl}"], writes=[f"routbig{sl}"])
                    P.op("vector", lambda v, sb=sb, sl=sl: v.scalar_tensor_tensor(
                        routbig[sl][:], ygat2[1][sl][:], W1B[:, sb:sb + 1], routbig[sl][:],
                        ALU.mult, ALU.add),
                        reads=[f"yg1{sl}", "W1B", f"routbig{sl}"], writes=[f"routbig{sl}"])
                P.dma("scalar", "dout", lambda s, sb=sb, sl=sl: s.dma_start(
                    out_d[sb * 128:(sb + 1) * 128, :], routbig[sl][:]),
                    reads=[f"routbig{sl}"])

            _finish(P, block)
    return nc


def _finish(P, block, all_sems=False):
    n_out = P.dma_count.get("dout", 0)
    waits = {"dout": n_out}
    keys = list(P.dma_count.keys()) if all_sems else ["dro0", "dro1"]
    for k in keys:
        if k != "dout" and P.dma_count.get(k):
            waits[k] = P.dma_count[k]
    P.ops["scalar"].append((dict(waits), lambda s: s.nop(), None, 0))
    P.finalize(block)


import numpy as np
from concourse.bass_utils import run_bass_kernel_spmd

_NC = 8
_S = 2048
_C = 1024


def _consts():
    i = np.arange(128)
    maskA = np.where(((i[None, :] <= i[:, None]) & (i[:, None] - i[None, :] <= 64))
                     | (i[None, :] == 0), 0.0, -1e30).astype(np.float32)
    mB = np.full((128, 384), -1e30, np.float32)
    mB[:, 0] = 0.0
    jj = np.arange(128)
    mB[:, 128:256] = np.where(jj[None, :] >= i[:, None] + 64, 0.0, -1e30)
    mB[:, 256:384] = np.where((jj[None, :] <= i[:, None]) & (i[:, None] - jj[None, :] <= 64),
                              0.0, -1e30)
    return {
        "identf": np.eye(128, dtype=np.float32),
        "identb": np.eye(128, dtype=np.float32),
        "maskA": np.concatenate([maskA, maskA], axis=1),
        "maskB": np.concatenate([mB, mB], axis=1),
        "onesrow": np.ones((1, 128), np.float32),
        "tokid1": (np.arange(128)[None, :] * 16 + np.arange(16)[:, None] + 1).astype(np.float32),
        "iotac2": np.broadcast_to((np.arange(7) * _C - 1).astype(np.float32), (128, 7)).copy(),
    }


_nc_cache = {}


def _get_nc():
    if "nc" not in _nc_cache:
        _nc_cache["nc"] = build(Cfg(S=_S, NRE=7, stage="full", NP=8, C=_C, sparse=True))
    return _nc_cache["nc"]


def kernel(L0_attn_hidden_states, L0_attn_Wq, L0_attn_bq, L0_attn_Wk, L0_attn_bk,
           L0_attn_Wv, L0_attn_bv, L0_attn_Wo, L0_attn_bo, L0_ffn_sg, L0_ffn_su,
           L0_ffn_sd, L0_ffn_rg, L0_ffn_ru, L0_ffn_rd, L0_ffn_wr, L0_ffn_rb):
    x = np.ascontiguousarray(np.asarray(L0_attn_hidden_states, dtype=np.float32))
    c = lambda a: np.ascontiguousarray(np.asarray(a, dtype=np.float32))
    shared = {
        "wq": c(L0_attn_Wq), "wk": c(L0_attn_Wk), "wv": c(L0_attn_Wv), "wo": c(L0_attn_Wo),
        "bq": c(L0_attn_bq), "bk": c(L0_attn_bk), "bv": c(L0_attn_bv), "bo": c(L0_attn_bo),
        "sg": c(L0_ffn_sg), "su": c(L0_ffn_su), "sd": c(L0_ffn_sd),
        "rg": c(L0_ffn_rg), "ru": c(L0_ffn_ru), "rd": c(L0_ffn_rd),
        "wr": c(L0_ffn_wr), "rb": c(L0_ffn_rb),
    }
    shared.update(_consts())
    nc = _get_nc()
    in_maps = [dict(shared, x=np.ascontiguousarray(x[b])) for b in range(_NC)]
    res = run_bass_kernel_spmd(nc, in_maps, core_ids=list(range(_NC)))
    return np.stack([res.results[b]["out"] for b in range(_NC)], axis=0)


# version 09
# speedup vs baseline: 1.9705x; 1.0795x over previous
from contextlib import ExitStack
from collections import defaultdict
import concourse.bass as bass
import concourse.mybir as mybir

ENGINES = ("tensor", "vector", "scalar", "gpsimd", "sync")


class Prog:
    def __init__(self, nc):
        self.nc = nc
        self.ops = {e: [] for e in ENGINES}
        self.eng_count = {e: 0 for e in ENGINES}
        self.dma_count = defaultdict(int)
        self.res = {}
        self.high_water = {e: defaultdict(int) for e in ENGINES}
        self.sem_handles = {}
        self._es = None

    def init_sems(self, es, dma_keys):
        for e in ENGINES:
            self.sem_handles["@" + e] = es.enter_context(self.nc.semaphore("sem_" + e))
        for k in dma_keys:
            self.sem_handles[k] = es.enter_context(self.nc.semaphore("sem_" + k))

    def _want(self, waits, token):
        if token is None:
            return
        sem, val = token
        if waits.get(sem, 0) < val:
            waits[sem] = val

    def _emit(self, engine, fn, reads, writes, inc_sem, inc_amt, accum_key=None,
              extra_waits=()):
        waits = {}
        pf = getattr(self, "_pending_fence", None)
        if pf and engine in pf:
            self._want(waits, pf.pop(engine))
        for tok in extra_waits:
            self._want(waits, tok)
        for r in reads:
            e = self.res.get(r)
            if e:
                self._want(waits, e["w"])
        for w in writes:
            e = self.res.get(w)
            if e:
                if accum_key is None or e.get("acc") != accum_key:
                    self._want(waits, e["w"])
                    for t in e["r"]:
                        self._want(waits, t)
        hw = self.high_water[engine]
        final_waits = {}
        for sem, val in waits.items():
            if sem == "@" + engine:
                pass
            if hw[sem] < val:
                final_waits[sem] = val
                hw[sem] = val
        if inc_sem is None:
            inc_sem = "@" + engine
        if inc_sem.startswith("@"):
            self.eng_count[engine] += inc_amt
            token = (inc_sem, self.eng_count[engine])
        else:
            self.dma_count[inc_sem] += inc_amt
            token = (inc_sem, self.dma_count[inc_sem])
        for r in reads:
            self.res.setdefault(r, {"w": None, "r": [], "acc": None})["r"].append(token)
        for w in writes:
            e = self.res.setdefault(w, {"w": None, "r": [], "acc": None})
            e["w"] = token
            e["r"] = []
            e["acc"] = accum_key
        self.ops[engine].append((final_waits, fn, inc_sem if not inc_sem.startswith("@") else "@" + engine, inc_amt))
        return token

    def op(self, engine, fn, reads=(), writes=()):
        return self._emit(engine, fn, reads, writes, None, 1)

    def mm_group(self, fns, reads=(), writes=(), key=None):
        key = key or object()
        tok = None
        for i, fn in enumerate(fns):
            last = i == len(fns) - 1
            if last:
                tok = self._emit("tensor", fn, reads, writes, None, 1, accum_key=key)
            else:
                waits = {}
                pf = getattr(self, "_pending_fence", None)
                if pf and "tensor" in pf:
                    self._want(waits, pf.pop("tensor"))
                for r in reads:
                    e = self.res.get(r)
                    if e:
                        self._want(waits, e["w"])
                for w in writes:
                    e = self.res.get(w)
                    if e and e.get("acc") != key:
                        self._want(waits, e["w"])
                        for t in e["r"]:
                            self._want(waits, t)
                        e["acc"] = key
                hw = self.high_water["tensor"]
                fw = {}
                for sem, val in waits.items():
                    if hw[sem] < val:
                        fw[sem] = val
                        hw[sem] = val
                for w in writes:
                    e = self.res.setdefault(w, {"w": None, "r": [], "acc": key})
                    e["acc"] = key
                self.ops["tensor"].append((fw, fn, None, 0))
        return tok

    def dma(self, engine, sem_key, fn, reads=(), writes=()):
        prev = self.dma_count.get(sem_key, 0)
        extra = ((sem_key, prev),) if prev else ()
        return self._emit(engine, fn, reads, writes, sem_key, 16,
                          extra_waits=extra)

    def read_barrier(self, engine, resources):
        raise NotImplementedError

    def barrier_op(self, engine, fn, resources):
        return self.op(engine, fn, reads=tuple(resources), writes=())

    def fence(self):
        all_waits = {}
        for e in ENGINES:
            if self.eng_count[e]:
                all_waits["@" + e] = self.eng_count[e]
        for k, v in self.dma_count.items():
            if v:
                all_waits[k] = v
        self._pending_fence = {}
        for e in ENGINES:
            waits = {s: v for s, v in all_waits.items() if s != "@" + e}
            self.eng_count[e] += 1
            tok = ("@" + e, self.eng_count[e])
            self.ops[e].append((waits, lambda eng: eng.drain(), "@" + e, 1))
            self._pending_fence[e] = tok
            hw = self.high_water[e]
            for s, v in waits.items():
                if hw[s] < v:
                    hw[s] = v
        self.res = {}

    def finalize(self, block):
        sems = self.sem_handles
        for engine in ENGINES:
            stream = self.ops[engine]
            if not stream:
                continue

            def make_body(stream=stream, engine=engine):
                def body(eng):
                    for waits, fn, inc_sem, inc_amt in stream:
                        for sem, val in waits.items():
                            eng.wait_ge(sems[sem], val)
                        inst = fn(eng)
                        if inc_amt and inst is not None:
                            inst.then_inc(sems[inc_sem], inc_amt)
                return body

            getattr(block, engine)(make_body())

    def pending_counts(self):
        return dict(self.eng_count), dict(self.dma_count)


F32 = mybir.dt.float32
BF16 = mybir.dt.bfloat16
AF = mybir.ActivationFunctionType
ALU = mybir.AluOpType
AX = mybir.AxisListType

NEG = -1e30
EPS = 1e-5


class Cfg:
    def __init__(self, S=2048, NRE=7, stage="full", NP=8, C=1024, sparse=True,
                 prep_cut=99):
        self.S = S
        self.NRE = NRE
        self.stage = stage
        self.NP = NP
        self.C = C
        self.sparse = sparse
        self.prep_cut = prep_cut
        self.skip_attn = False


def build(cfg: Cfg):
    nc = _build(cfg)
    mybir.codegen_inst_isa_subclasses(nc)
    return nc


def _build(cfg: Cfg):
    S = cfg.S
    SB = S // 128
    SC = S // 512
    NI = 22
    nc = bass.Bass("TRN2", target_bir_lowering=False)

    x_d = nc.dram_tensor("x", [S, 1024], F32, kind="ExternalInput").ap()
    wq_d = nc.dram_tensor("wq", [1024, 1024], F32, kind="ExternalInput").ap()
    wk_d = nc.dram_tensor("wk", [1024, 1024], F32, kind="ExternalInput").ap()
    wv_d = nc.dram_tensor("wv", [1024, 1024], F32, kind="ExternalInput").ap()
    wo_d = nc.dram_tensor("wo", [1024, 1024], F32, kind="ExternalInput").ap()
    bq_d = nc.dram_tensor("bq", [1024], F32, kind="ExternalInput").ap()
    bk_d = nc.dram_tensor("bk", [1024], F32, kind="ExternalInput").ap()
    bv_d = nc.dram_tensor("bv", [1024], F32, kind="ExternalInput").ap()
    bo_d = nc.dram_tensor("bo", [1024], F32, kind="ExternalInput").ap()
    sg_d = nc.dram_tensor("sg", [2816, 1024], F32, kind="ExternalInput").ap()
    su_d = nc.dram_tensor("su", [2816, 1024], F32, kind="ExternalInput").ap()
    sd_d = nc.dram_tensor("sd", [1024, 2816], F32, kind="ExternalInput").ap()
    rg_d = nc.dram_tensor("rg", [7, 2816, 1024], F32, kind="ExternalInput").ap()
    ru_d = nc.dram_tensor("ru", [7, 2816, 1024], F32, kind="ExternalInput").ap()
    rd_d = nc.dram_tensor("rd", [7, 1024, 2816], F32, kind="ExternalInput").ap()
    wr_d = nc.dram_tensor("wr", [7, 1024], F32, kind="ExternalInput").ap()
    rb_d = nc.dram_tensor("rb", [7], F32, kind="ExternalInput").ap()
    identf_d = nc.dram_tensor("identf", [128, 128], F32, kind="ExternalInput").ap()
    identb_d = nc.dram_tensor("identb", [128, 128], F32, kind="ExternalInput").ap()
    maskA_d = nc.dram_tensor("maskA", [128, 256], F32, kind="ExternalInput").ap()
    maskB_d = nc.dram_tensor("maskB", [128, 768], F32, kind="ExternalInput").ap()
    onesrow_d = nc.dram_tensor("onesrow", [1, 128], F32, kind="ExternalInput").ap()

    out_d = nc.dram_tensor("out", [S, 1024], F32, kind="ExternalOutput").ap()
    dbg = {}
    if cfg.stage in ("ln1", "attn", "router"):
        dbg["h"] = nc.dram_tensor("hdbg", [S, 1024], F32, kind="ExternalOutput").ap()
    if cfg.stage == "router":
        dbg["w"] = nc.dram_tensor("wdbg", [128, 7 * SB], F32, kind="ExternalOutput").ap()
    x1_d = nc.dram_tensor("x1d", [8, 128, S], F32, kind="Internal").ap()
    C = cfg.C
    yc_d = nc.dram_tensor("ycd", [7 * C + 128, 1024], BF16, kind="Internal").ap()
    h2t_d = nc.dram_tensor("h2td", [S, 1024], BF16, kind="Internal").ap()
    tokid1_d = nc.dram_tensor("tokid1", [16, 128], F32, kind="ExternalInput").ap()
    iotac2_d = nc.dram_tensor("iotac2", [128, 7], F32, kind="ExternalInput").ap()

    with ExitStack() as es:
        Tz = lambda nm, sh, dt=F32: es.enter_context(nc.sbuf_tensor(nm, sh, dt))
        H = Tz("H", [128, 8 * S], BF16)
        identf = Tz("identf_s", [128, 128], F32)
        identb = Tz("identb_s", [128, 128], BF16)
        onesrow = Tz("onesrow_s", [1, 128], F32)
        biases = Tz("biases", [128, 32], F32)
        wrT = Tz("wrT", [128, 56], BF16)
        rbrow = Tz("rbrow", [1, 7], F32)
        rbbc = Tz("rbbc", [128, 7], F32)
        wtok = Tz("wtok", [128, 7 * SB], F32)
        ismaxB = Tz("ismaxB", [128, 7 * SB], F32)
        top2B = Tz("top2B", [128, 7 * SB], F32)
        W0B = Tz("W0B", [128, SB], F32)
        W1B = Tz("W1B", [128, SB], F32)
        selm = Tz("selm", [7, S], BF16)
        iotac2 = Tz("iotac2_s", [128, 7], F32)
        lbuf = Tz("lbuf", [128, 7], F32)
        tbuf = Tz("tbuf", [128, 7], F32)
        mbuf = Tz("mbuf", [128, 7], F32)
        l2buf = Tz("l2buf", [128, 7], F32)
        ebuf = Tz("ebuf", [128, 7], F32)
        r1 = Tz("r1", [128, 1], F32)
        eps_col = Tz("eps_col", [128, 1], F32)
        r2 = Tz("r2", [128, 1], F32)

        psc = es.enter_context(nc.psum_tensor("psc", [128, 1024], F32))
        pb = [psc[:, 0:512], psc[:, 512:1024]] + [
            es.enter_context(nc.psum_tensor(f"pb{i}", [128, 512], F32)) for i in range(2, 6)]
        pbb = [es.enter_context(nc.psum_tensor(f"pbb{i}", [128, 1024], BF16)) for i in range(2)]

        prog = Prog(nc)
        dma_keys = ["dx0", "dx1", "dwt0", "dwt1", "dsg0", "dsg1", "dsd0", "dsd1",
                    "dcst", "dwb", "dri0", "dri1", "dro0", "dro1", "dxw", "dout",
                    "dfi0", "dfi1", "dga0", "dga1", "dhc0", "dhc1", "dht0", "dht1"]
        prog.init_sems(es, dma_keys)
        block = es.enter_context(nc.Block())

        P = prog

        def mm(psum_ap, lhsT_ap, rhs_ap, reads, writes, start=True, stop=True):
            return P.op("tensor", lambda t: t.matmul(psum_ap, lhsT_ap, rhs_ap,
                                                     start=start, stop=stop),
                        reads=reads, writes=writes)

        def Hs(hc, lo, n):
            return H[:, hc * S + lo: hc * S + lo + n]

        P.op("vector", lambda v: v.memset(eps_col[:], EPS), writes=["eps_col"])
        P.dma("sync", "dcst", lambda s: s.dma_start(identf[:], identf_d), writes=["identf"])
        P.dma("sync", "dcst", lambda s: s.dma_start(onesrow[:], onesrow_d), writes=["onesrow"])
        P.dma("sync", "dcst", lambda s: s.dma_start(iotac2[:], iotac2_d), writes=["iotac2"])
        P.dma("sync", "dcst", lambda s: s.dma_start(rbrow[:], rb_d[None, :]), writes=["rbrow"])
        P.dma("gpsimd", "dwb", lambda g: g.dma_start(identb[:], identb_d), writes=["identb"])

        with ExitStack() as es_a:
            Ta = lambda nm, sh, dt=F32: es_a.enter_context(nc.sbuf_tensor(nm, sh, dt))
            X = Ta("X", [128, 8 * S], F32)
            brow = Ta("brow", [1, 1024], F32)
            wrstage = Ta("wrstage", [7, 1024], BF16)
            P.dma("gpsimd", "dwb", lambda g: g.dma_start(wrstage[:], wr_d), writes=["wrstage"])
            for bi, b_d in enumerate([bq_d, bk_d, bv_d, bo_d]):
                P.dma("sync", "dcst", lambda s, b_d=b_d: s.dma_start(brow[:], b_d[None, :]),
                      reads=["biasesp"], writes=["brow"])
                for c in range(8):
                    mm(pb[5][:, bi * 8 + c: bi * 8 + c + 1], brow[0:1, c * 128:(c + 1) * 128],
                       onesrow[0:1, 0:1], reads=["brow", "onesrow"], writes=["pb5"])
                P.op("vector", lambda v, bi=bi: v.tensor_copy(biases[:, bi * 8:(bi + 1) * 8],
                                                              pb[5][:, bi * 8:(bi + 1) * 8]),
                     reads=["pb5"], writes=["biases", "biasesp"])
            mm(pb[5][:, 40:47], onesrow[:], rbrow[0:1, 0:7], reads=["onesrow", "rbrow"], writes=["pb5"])
            P.op("vector", lambda v: v.tensor_copy(rbbc[:], pb[5][:, 40:47]), reads=["pb5"], writes=["rbbc"])
            for c in range(8):
                mm(pb[5][:, 64 + c * 8: 64 + c * 8 + 7], wrstage[0:7, c * 128:(c + 1) * 128],
                   identb[0:7, 0:7], reads=["wrstage", "identb"], writes=["pb5"])
            P.op("vector", lambda v: v.tensor_copy(
                wrT[:].rearrange("p (c f) -> p c f", f=7),
                pb[5][:, 64:128].rearrange("p (c f) -> p c f", f=8)[:, :, 0:7]),
                reads=["pb5"], writes=["wrT"])

            xin = [Ta(f"xin{s}", [128, 1024], F32) for s in range(2)]

            st6 = [Ta(f"st6{s}", [128, 12], F32) for s in range(2)]
            mv = [Ta(f"mv{s}", [128, 2], F32) for s in range(2)]
            rv = [Ta(f"rv{s}", [128, 1], F32) for s in range(2)]
            htok = [Ta(f"htok{s}", [128, 1024], BF16) for s in range(2)]
            wrows = Ta("wrows", [7, S], F32)
            maskA = Ta("maskA_s", [128, 256], BF16)
            maskB = Ta("maskB_s", [128, 768], BF16)
            P.dma("gpsimd", "dwb", lambda g: g.dma_start(maskA[:], maskA_d), writes=["maskA"])
            P.dma("gpsimd", "dwb", lambda g: g.dma_start(maskB[:], maskB_d), writes=["maskB"])
            wstage = [Ta(f"wstage{s}", [128, 4096], BF16) for s in range(2)]
            WT = [Ta(f"WT{s}", [128, 4096], BF16) for s in range(2)]
            qT = [Ta(f"qT{s}", [128, S], BF16) for s in range(2)]
            kT = [Ta(f"kT{s}", [128, S], BF16) for s in range(2)]
            vT = [Ta("vT0", [128, S], BF16)] * 2
            vtok = [Ta(f"vtok{s}", [128, S], BF16) for s in range(2)]
            opair = [Ta(f"opair{s}", [128, S], BF16) for s in range(2)]
            s2b2 = Ta("s2b2", [128, 768], F32)
            nmx2 = Ta("nmx2", [128, 2], F32)
            nm82 = Ta("nm82", [128, 2], F32)
            rz2 = Ta("rz2", [128, 2], F32)
            Pb = [[Ta(f"Pb{h}_{s}", [128, 384], BF16) for s in range(2)] for h in range(2)]
            PT = [Ta(f"PT{h}", [128, 384], BF16) for h in range(2)]
            zz = [Ta(f"zz{h}", [128, 1], F32) for h in range(2)]

            def Xs(hc, lo, n):
                return X[:, hc * S + lo: hc * S + lo + n]

            def ln_pass(src_is_dram, dest_prefix):
                for sb in range(SB):
                    sl = sb % 2
                    xk = f"xin{sl}"
                    if src_is_dram:
                        P.dma("sync", f"dx{sl}",
                              lambda s, sb=sb, sl=sl: s.dma_start(
                                  xin[sl][:], x_d[sb * 128:(sb + 1) * 128, :]),
                              writes=[xk])
                    else:
                        for g in range(2):
                            P.mm_group([
                                (lambda t, hc=hc, sb=sb: t.transpose(
                                    pb[hc // 4][:, (hc % 4) * 128:(hc % 4 + 1) * 128],
                                    Xs(hc, sb * 128, 128), identf[:]))
                                for hc in range(4 * g, 4 * g + 4)],
                                reads=[f"X{hc}" for hc in range(4 * g, 4 * g + 4)] + ["identf"],
                                writes=[f"pb{g}"])
                        for g in range(2):
                            P.op("vector", lambda v, g=g, sl=sl: v.tensor_copy(
                                xin[sl][:, g * 512:(g + 1) * 512], pb[g][:]),
                                reads=[f"pb{g}"], writes=[xk])
                    P.op("vector", lambda v, sl=sl: v.bn_stats(st6[sl][:, 0:6], xin[sl][:, 0:512]),
                         reads=[xk], writes=[f"st6{sl}"])
                    P.op("vector", lambda v, sl=sl: v.bn_stats(st6[sl][:, 6:12], xin[sl][:, 512:1024]),
                         reads=[xk], writes=[f"st6{sl}"])
                    P.op("vector", lambda v, sl=sl: v.bn_aggr(mv[sl][:], st6[sl][:]),
                         reads=[f"st6{sl}"], writes=[f"mv{sl}"])
                    P.op("scalar", lambda s, sl=sl: s.activation(rv[sl][:], mv[sl][:, 1:2],
                                                                 AF.Ln, bias=eps_col[:], scale=1.0),
                         reads=[f"mv{sl}", "eps_col"], writes=[f"rv{sl}"])
                    P.op("scalar", lambda s, sl=sl: s.activation(rv[sl][:], rv[sl][:],
                                                                 AF.Exp, bias=0.0, scale=-0.5),
                         reads=[f"rv{sl}"], writes=[f"rv{sl}"])
                    P.op("vector", lambda v, sl=sl: v.tensor_scalar(
                        htok[sl][:], xin[sl][:], mv[sl][:, 0:1], rv[sl][:],
                        ALU.subtract, ALU.mult),
                        reads=[xk, f"mv{sl}", f"rv{sl}"], writes=[f"htok{sl}"])
                    if src_is_dram:
                        for g in range(2):
                            P.mm_group([
                                (lambda t, hc=hc, sl=sl: t.transpose(
                                    pb[hc // 4][:, (hc % 4) * 128:(hc % 4 + 1) * 128],
                                    xin[sl][:, hc * 128:(hc + 1) * 128], identf[:]))
                                for hc in range(4 * g, 4 * g + 4)],
                                reads=[xk, "identf"], writes=[f"pb{g}"])
                        for g in range(2):
                            dst = X[:, 0:4 * S].rearrange("p (c f) -> p c f", f=S)[
                                :, :, sb * 128:(sb + 1) * 128] if g == 0 else \
                                X[:, 4 * S:8 * S].rearrange("p (c f) -> p c f", f=S)[
                                :, :, sb * 128:(sb + 1) * 128]
                            P.op("vector", lambda v, g=g, dst=dst: v.tensor_copy(
                                dst, pb[g][:].rearrange("p (c f) -> p c f", f=128)),
                                reads=[f"pb{g}"],
                                writes=[f"X{c}" for c in range(4 * g, 4 * g + 4)])
                    if (not src_is_dram) and cfg.sparse and cfg.NRE > 0:
                        P.dma("gpsimd", f"dht{sl}", lambda g, sb=sb, sl=sl: g.dma_start(
                            h2t_d[sb * 128:(sb + 1) * 128, :], htok[sl][:]),
                            reads=[f"htok{sl}"], writes=["h2t"])
                    P.mm_group([
                        (lambda t, hc=hc, sl=sl: t.transpose(
                            pbb[sl][:, hc * 128:(hc + 1) * 128],
                            htok[sl][:, hc * 128:(hc + 1) * 128], identb[:]))
                        for hc in range(8)],
                        reads=[f"htok{sl}", "identb"], writes=[f"pbb{sl}"])
                    dstH = H[:].rearrange("p (c f) -> p c f", f=S)[:, :, sb * 128:(sb + 1) * 128]
                    P.op("vector", lambda v, sl=sl, dstH=dstH: v.tensor_copy(
                        dstH, pbb[sl][:].rearrange("p (c f) -> p c f", f=128)),
                        reads=[f"pbb{sl}"], writes=[f"H{c}" for c in range(8)])

            ln_pass(True, "ln1")

            def emit_transposed_out(src_slicer, dram_ap, src_keys, bf16=False):
                for sb in range(SB):
                    sl = sb % 2
                    if not bf16:
                        for g in range(2):
                            P.mm_group([
                                (lambda t, hc=hc, sb=sb: t.transpose(
                                    pb[hc // 4][:, (hc % 4) * 128:(hc % 4 + 1) * 128],
                                    src_slicer(hc, sb * 128, 128), identf[:]))
                                for hc in range(4 * g, 4 * g + 4)],
                                reads=[src_keys[hc] for hc in range(4 * g, 4 * g + 4)] + ["identf"],
                                writes=[f"pb{g}"])
                        for g in range(2):
                            P.op("vector", lambda v, g=g, sl=sl: v.tensor_copy(
                                xin[sl][:, g * 512:(g + 1) * 512], pb[g][:]),
                                reads=[f"pb{g}"], writes=[f"xin{sl}"])
                    else:
                        P.mm_group([
                            (lambda t, hc=hc, sb=sb, sl=sl: t.transpose(
                                pbb[sl][:, hc * 128:(hc + 1) * 128],
                                src_slicer(hc, sb * 128, 128), identb[:]))
                            for hc in range(8)],
                            reads=list(src_keys) + ["identb"], writes=[f"pbb{sl}"])
                        P.op("vector", lambda v, sl=sl: v.tensor_copy(
                            xin[sl][:], pbb[sl][:]),
                            reads=[f"pbb{sl}"], writes=[f"xin{sl}"])
                    P.dma("scalar", "dout", lambda s, sb=sb, sl=sl: s.dma_start(
                        dram_ap[sb * 128:(sb + 1) * 128, :], xin[sl][:]),
                        reads=[f"xin{sl}"])

            if cfg.stage == "ln1":
                emit_transposed_out(Xs, out_d, [f"X{c}" for c in range(8)])
                emit_transposed_out(Hs, dbg["h"], [f"H{c}" for c in range(8)], bf16=True)
                _finish(P, block)
                return nc

            NP = cfg.NP

            def emit_wdma(hp):
                sl = hp % 2
                wsk = f"wstage{sl}"
                for m, w_d in enumerate([wq_d, wk_d, wv_d]):
                    P.dma("gpsimd", f"dwt{sl}",
                          lambda g, m=m, w_d=w_d, sl=sl, hp=hp: g.dma_start(
                              wstage[sl][:, m * 1024:(m + 1) * 1024],
                              w_d[hp * 128:(hp + 1) * 128, :]),
                          writes=[wsk])
                src_wo = wo_d.rearrange("(c p) f -> p c f", p=128)[:, :, hp * 128:(hp + 1) * 128]
                P.dma("gpsimd", f"dwt{sl}",
                      lambda g, sl=sl, src_wo=src_wo: g.dma_start(
                          wstage[sl][:, 3072:4096].rearrange("p (c f) -> p c f", f=128), src_wo),
                      writes=[wsk])

            def filler_items(hp):
                items = []
                sl = hp % 2
                wsk, wtk = f"wstage{sl}", f"WT{sl}"
                for seg in range(4):
                    def wt_seg(seg=seg, sl=sl, wsk=wsk, wtk=wtk):
                        P.mm_group([
                            (lambda t, seg=seg, c=c, sl=sl: t.transpose(
                                pbb[1][:, c * 128:(c + 1) * 128],
                                wstage[sl][:, seg * 1024 + c * 128: seg * 1024 + (c + 1) * 128],
                                identb[:]))
                            for c in range(8)],
                            reads=[wsk, "identb"], writes=["pbb1"])
                        P.op("scalar", lambda s, seg=seg, sl=sl: s.copy(
                            WT[sl][:, seg * 1024:(seg + 1) * 1024], pbb[1][:]),
                            reads=["pbb1"], writes=[wtk])
                    items.append(wt_seg)
                dstmap = {"qT": qT, "kT": kT, "vT": vT}
                for m, (dstk, bcol) in enumerate([("qT", 0), ("kT", 1), ("vT", 2)]):
                    for n in range(SC):
                        def qkv_one(m=m, n=n, sl=sl, dstk=dstk, bcol=bcol, hp=hp, wtk=wtk):
                            bk = 3 + (m * SC + n) % 2
                            dstb = dstmap[dstk][sl]
                            P.mm_group([
                                (lambda t, c=c, n=n, sl=sl, bk=bk, m=m:
                                 t.matmul(pb[bk][:], WT[sl][:, m * 1024 + c * 128: m * 1024 + (c + 1) * 128],
                                          Hs(c, n * 512, 512), start=(c == 0), stop=(c == 7)))
                                for c in range(8)],
                                reads=[wtk] + [f"H{c}" for c in range(8)], writes=[f"pb{bk}"])
                            P.op("scalar", lambda s, n=n, bk=bk, dstb=dstb, bcol=bcol, hp=hp:
                                 s.activation(dstb[:, n * 512:(n + 1) * 512], pb[bk][:],
                                              AF.Identity, bias=biases[:, bcol * 8 + hp: bcol * 8 + hp + 1],
                                              scale=1.0),
                                 reads=[f"pb{bk}", "biases"], writes=[(dstk + str(sl)) if dstk != "vT" else "vT0"])
                        items.append(qkv_one)
                for g8 in range((SB + 7) // 8):
                    def vtok_g(g8=g8, sl=sl):
                        lo = g8 * 8
                        hi = min(lo + 8, SB)
                        P.mm_group([
                            (lambda t, st=st, lo=lo, sl=sl: t.transpose(
                                pbb[1][:, (st - lo) * 128:(st - lo + 1) * 128],
                                vT[sl][:, st * 128:(st + 1) * 128], identb[:]))
                            for st in range(lo, hi)],
                            reads=["vT0", "identb"], writes=["pbb1"])
                        P.op("scalar", lambda s, lo=lo, hi=hi, sl=sl: s.copy(
                            vtok[sl][:, lo * 128: hi * 128], pbb[1][:, 0:(hi - lo) * 128]),
                            reads=["pbb1"], writes=[f"vtok{sl}"])
                    items.append(vtok_g)
                return items

            def wo_items(hp):
                items = []
                sl = hp % 2
                wtk = f"WT{sl}"
                for oc in range(8):
                    def wo_oc(oc=oc, sl=sl, wtk=wtk, hp=hp):
                        for n in range(SC):
                            wb = 4 + (oc * SC + n) % 2
                            mm(pb[wb][:], WT[sl][:, 3072 + oc * 128: 3072 + (oc + 1) * 128],
                               opair[sl][:, n * 512:(n + 1) * 512],
                               reads=[wtk, f"opair{sl}"], writes=[f"pb{wb}"])
                            if hp == 0:
                                P.op("vector", lambda v, oc=oc, n=n, wb=wb: v.scalar_tensor_tensor(
                                    Xs(oc, n * 512, 512), pb[wb][:], biases[:, 24 + oc: 24 + oc + 1],
                                    Xs(oc, n * 512, 512), ALU.add, ALU.add),
                                    reads=[f"pb{wb}", "biases", f"X{oc}"], writes=[f"X{oc}"])
                            else:
                                P.op("vector", lambda v, oc=oc, n=n, wb=wb: v.tensor_tensor(
                                    Xs(oc, n * 512, 512), Xs(oc, n * 512, 512), pb[wb][:], ALU.add),
                                    reads=[f"pb{wb}", f"X{oc}"], writes=[f"X{oc}"])
                    items.append(wo_oc)
                return items

            def emit_scores(hp, qt):
                sl = hp % 2
                kts = [0] if qt == 0 else [0, qt - 1, qt]
                fns = []
                for h in range(2):
                    base = 64 * h
                    for j, kt in enumerate(kts):
                        fns.append(lambda t, h=h, base=base, j=j, kt=kt, qt=qt, sl=sl:
                                   t.matmul(psc[:, h * 512 + j * 128: h * 512 + (j + 1) * 128],
                                            qT[sl][base:base + 64, qt * 128:(qt + 1) * 128],
                                            kT[sl][base:base + 64, kt * 128:(kt + 1) * 128],
                                            start=True, stop=True))
                P.mm_group(fns, reads=[f"qT{sl}", f"kT{sl}"], writes=["pb0", "pb1"])

            def emit_softmax(hp, qt):
                W = 128 if qt == 0 else 384
                msk, mskk = (maskA, "maskA") if qt == 0 else (maskB, "maskB")
                par = qt % 2
                psc2 = psc[:].rearrange("p (h f) -> p h f", f=512)[:, :, 0:W]
                P.op("vector", lambda v, W=W, psc2=psc2: v.tensor_reduce(
                    nmx2[:], psc2, AX.X, ALU.max, negate=True),
                    reads=["pb0", "pb1"], writes=["nmx2"])
                P.op("vector", lambda v: v.tensor_scalar(
                    nm82[:], nmx2[:], 0.125, None, ALU.mult, ALU.bypass),
                    reads=["nmx2"], writes=["nm82"])
                mskw = msk[:].rearrange("p (h f) -> p h f", h=2)[:, :, 0:W]
                P.op("vector", lambda v, W=W, mskw=mskw, psc2=psc2: v.scalar_tensor_tensor(
                    s2b2[:].rearrange("p (h f) -> p h f", f=384)[:, :, 0:W], psc2, 0.125,
                    mskw, ALU.mult, ALU.add),
                    reads=["pb0", "pb1", mskk], writes=["s2b2"])
                for h in range(2):
                    P.op("scalar", lambda s, h=h, W=W, par=par: s.activation(
                        Pb[h][par][:, 0:W], s2b2[:, h * 384: h * 384 + W], AF.Exp,
                        bias=nm82[:, h: h + 1], scale=1.0, accum_out=zz[h][:]),
                        reads=["s2b2", "nm82"], writes=[f"Pb{h}_{par}", f"zz{h}"])
                P.op("vector", lambda v: v.reciprocal(rz2[:, 0:1], zz[0][:]),
                     reads=["zz0"], writes=["rz2"])
                P.op("vector", lambda v: v.reciprocal(rz2[:, 1:2], zz[1][:]),
                     reads=["zz1"], writes=["rz2"])
                for h in range(2):
                    P.op("vector", lambda v, h=h, W=W, par=par: v.tensor_scalar(
                        Pb[h][par][:, 0:W], Pb[h][par][:, 0:W], rz2[:, h: h + 1], None,
                        ALU.mult, ALU.bypass),
                        reads=[f"Pb{h}_{par}", "rz2"], writes=[f"Pb{h}_{par}"])

            def emit_ptpv(hp, qt):
                sl = hp % 2
                kts = [0] if qt == 0 else [0, qt - 1, qt]
                nk = len(kts)
                W = 128 * nk
                par = qt % 2
                P.mm_group([
                    (lambda t, h=h, j=j, par=par: t.transpose(
                        pbb[0][:, (h * 4 + j) * 128:(h * 4 + j + 1) * 128],
                        Pb[h][par][:, j * 128:(j + 1) * 128], identb[:]))
                    for h in range(2) for j in range(nk)],
                    reads=[f"Pb0_{par}", f"Pb1_{par}", "identb"], writes=["pbb0"])
                for h in range(2):
                    base = 64 * h
                    P.op("scalar", lambda s, h=h, W=W: s.copy(PT[h][:, 0:W],
                                                              pbb[0][:, h * 512: h * 512 + W]),
                         reads=["pbb0"], writes=[f"PT{h}"])
                for h in range(2):
                    base = 64 * h
                    P.mm_group([
                        (lambda t, h=h, j=j, kt=kt, qt=qt, base=base, nk=nk, sl=sl:
                         t.matmul(pb[2][base:base + 64, (qt % 4) * 128:(qt % 4 + 1) * 128],
                                  vtok[sl][:, kt * 128 + base: kt * 128 + base + 64],
                                  PT[h][:, j * 128:(j + 1) * 128],
                                  start=(j == 0), stop=(j == nk - 1)))
                        for j, kt in enumerate(kts)],
                        reads=[f"vtok{sl}", f"PT{h}"], writes=[f"pb2_{h}_{qt % 4}"])
                if qt % 4 == 3 or qt == SB - 1:
                    qlo = (qt // 4) * 4
                    nq = qt - qlo + 1
                    P.op("scalar", lambda s, qlo=qlo, nq=nq, sl=sl, qt=qt: s.copy(
                        opair[sl][:, qlo * 128:(qlo + nq) * 128], pb[2][:, 0:nq * 128]),
                        reads=[f"pb2_{h}_{q % 4}" for h in range(2) for q in range(qlo, qt + 1)],
                        writes=[f"opair{sl}"])

            _attn_drivers = [] if getattr(cfg, "skip_attn", False) else [1]
            for _ in _attn_drivers:
                emit_wdma(0)
                for it in filler_items(0):
                    it()
                for hp in range(NP):
                    if hp + 1 < NP:
                        emit_wdma(hp + 1)
                    queue = []
                    if hp >= 1:
                        queue += wo_items(hp - 1)
                    if hp + 1 < NP:
                        queue += filler_items(hp + 1)
                    prev = None
                    popped = 0
                    for qt in range(SB):
                        emit_scores(hp, qt)
                        if prev is not None:
                            emit_ptpv(hp, prev)
                        emit_softmax(hp, qt)
                        want = (qt + 1) * len(queue) // SB
                        while popped < want:
                            queue[popped]()
                            popped += 1
                        prev = qt
                    emit_ptpv(hp, SB - 1)
                    while popped < len(queue):
                        queue[popped]()
                        popped += 1
                for it in wo_items(NP - 1):
                    it()
            if cfg.stage == "attn":
                emit_transposed_out(Xs, out_d, [f"X{c}" for c in range(8)])
                emit_transposed_out(Hs, dbg["h"], [f"H{c}" for c in range(8)], bf16=True)
                _finish(P, block)
                return nc

            ln_pass(False, "ln2")

            for sb in range(SB):
                P.mm_group([
                    (lambda t, c=c, sb=sb: t.matmul(
                        pb[2][:, 0:7], Hs(c, sb * 128, 128), wrT[:, c * 7:(c + 1) * 7],
                        start=(c == 0), stop=(c == 7)))
                    for c in range(8)],
                    reads=[f"H{c}" for c in range(8)] + ["wrT"], writes=["pb2"])
                P.op("vector", lambda v: v.tensor_tensor(lbuf[:], pb[2][:, 0:7], rbbc[:], ALU.add),
                     reads=["pb2", "rbbc"], writes=["lbuf"])
                P.op("vector", lambda v: v.tensor_reduce(r1[:], lbuf[:], AX.X, ALU.max, negate=True),
                     reads=["lbuf"], writes=["r1"])
                P.op("vector", lambda v: v.tensor_scalar(tbuf[:], lbuf[:], r1[:], None, ALU.add, ALU.bypass),
                     reads=["lbuf", "r1"], writes=["tbuf"])
                P.op("vector", lambda v, sb=sb: v.tensor_scalar(
                    ismaxB[:, sb * 7:(sb + 1) * 7], tbuf[:], 0.0, None, ALU.is_ge, ALU.bypass),
                    reads=["tbuf"], writes=["ismaxB"])
                P.op("vector", lambda v, sb=sb: v.scalar_tensor_tensor(
                    l2buf[:], ismaxB[:, sb * 7:(sb + 1) * 7], NEG, tbuf[:], ALU.mult, ALU.add),
                    reads=["ismaxB", "tbuf"], writes=["l2buf"])
                P.op("vector", lambda v: v.tensor_reduce(r2[:], l2buf[:], AX.X, ALU.max, negate=True),
                     reads=["l2buf"], writes=["r2"])
                P.op("vector", lambda v, sb=sb: v.tensor_scalar(
                    top2B[:, sb * 7:(sb + 1) * 7], tbuf[:], r2[:], 0.0, ALU.add, ALU.is_ge),
                    reads=["tbuf", "r2"], writes=["top2B"])
                P.op("scalar", lambda s: s.activation(ebuf[:], tbuf[:], AF.Exp, bias=0.0, scale=1.0),
                     reads=["tbuf"], writes=["ebuf"])
                P.op("vector", lambda v, sb=sb: v.tensor_tensor(
                    ebuf[:], ebuf[:], top2B[:, sb * 7:(sb + 1) * 7], ALU.mult),
                    reads=["ebuf", "top2B"], writes=["ebuf"])
                P.op("vector", lambda v: v.tensor_reduce(r1[:], ebuf[:], AX.X, ALU.add),
                     reads=["ebuf"], writes=["r1"])
                P.op("vector", lambda v: v.reciprocal(r2[:], r1[:]), reads=["r1"], writes=["r2"])
                P.op("vector", lambda v, sb=sb: v.tensor_scalar(
                    wtok[:, sb * 7:(sb + 1) * 7], ebuf[:], r2[:], None, ALU.mult, ALU.bypass),
                    reads=["ebuf", "r2"], writes=["wtok"])
                P.op("vector", lambda v, sb=sb: v.scalar_tensor_tensor(
                    mbuf[:], wtok[:, sb * 7:(sb + 1) * 7], 1.0, ismaxB[:, sb * 7:(sb + 1) * 7],
                    ALU.mult, ALU.mult, accum_out=W0B[:, sb:sb + 1]),
                    reads=["wtok", "ismaxB"], writes=["mbuf", "W0B"])
                P.op("vector", lambda v, sb=sb: v.tensor_scalar(
                    W1B[:, sb:sb + 1], W0B[:, sb:sb + 1], -1.0, 1.0, ALU.mult, ALU.add),
                    reads=["W0B"], writes=["W1B"])
            for sb in range(SB):
                bk = (sb // 4) % 2
                P.mm_group([lambda t, sb=sb, bk=bk: t.transpose(
                    pb[bk][0:7, (sb % 4) * 128:(sb % 4 + 1) * 128],
                    wtok[:, sb * 7:(sb + 1) * 7], identf[:])],
                    reads=["wtok", "identf"], writes=[f"pb{bk}"])
                if sb % 4 == 3 or sb == SB - 1:
                    lo = (sb // 4) * 4
                    nqq = (sb - lo + 1) * 128
                    P.op("vector", lambda v, lo=lo, nqq=nqq, bk=bk: v.tensor_copy(
                        wrows[0:7, lo * 128: lo * 128 + nqq], pb[bk][0:7, 0:nqq]),
                        reads=[f"pb{bk}"], writes=["wrows"])

            P.op("vector", lambda v: v.tensor_scalar(selm[0:7, :], wrows[0:7, :], 0.0, None,
                                                     ALU.is_gt, ALU.bypass),
                 reads=["wrows"], writes=["selm"])
            if cfg.stage == "router":
                P.dma("scalar", "dout", lambda s: s.dma_start(dbg["w"], wtok[:]), reads=["wtok"])
                emit_transposed_out(Hs, dbg["h"], [f"H{c}" for c in range(8)], bf16=True)
                emit_transposed_out(Xs, out_d, [f"X{c}" for c in range(8)])
                _finish(P, block)
                return nc

            for hc in range(8):
                P.dma("scalar", "dxw", lambda s, hc=hc: s.dma_start(
                    x1_d[hc], Xs(hc, 0, S)),
                    reads=[f"X{hc}"], writes=[f"x1_{hc}_{n}" for n in range(SC)])
        P.fence()

        with ExitStack() as es_f:
            Tf = lambda nm, sh, dt=F32: es_f.enter_context(nc.sbuf_tensor(nm, sh, dt))
            tbig = Tf("tbig", [128, NI * S], BF16)
            es_f1 = es_f.enter_context(ExitStack())
            Tf1 = lambda nm, sh, dt=F32: es_f1.enter_context(nc.sbuf_tensor(nm, sh, dt))
            sgstage = [Tf1(f"sgstage{s}", [128, 2048], BF16) for s in range(2)]
            wTf = [Tf1(f"wTf{s}", [128, 2048], BF16) for s in range(2)]
            sdstage = [Tf1(f"sdstage{s}", [128, 2816], BF16) for s in range(2)]
            sdT = [Tf1(f"sdT{s}", [128, 2816], BF16) for s in range(2)]
            sil = Tf1("sil", [128, 512], F32)
            rin = [Tf1("rin0", [128, 512], F32)] * 2
            rout = [Tf1("rout0", [128, 512], F32)] * 2
            CW = C // 16
            Hcmp = Tf1("Hcmp", [128, 8 * C], BF16)
            arrb = Tf1("arrb", [16, S // 16 + CW], F32)
            tokid1 = Tf1("tokid1_s", [16, 128], F32)
            postens = Tf1("postens", [7, S], F32)
            P.op("vector", lambda v: v.tensor_tensor_scan(
                postens[0:7, :], selm[0:7, :], selm[0:7, :], 0.0, ALU.add, ALU.bypass),
                reads=["selm"], writes=["postens"])
            P.dma("sync", "dcst", lambda s: s.dma_start(tokid1[:], tokid1_d), writes=["tokid1"])
            idxf = Tf1("idxf", [16, S // 16 + CW], F32)
            nfb = es_f1.enter_context(nc.sbuf_tensor("nfb", [1, 1], mybir.dt.uint32))
            idxall = es_f1.enter_context(nc.sbuf_tensor("idxall", [128, 7 * CW], mybir.dt.int16))
            ptok = Tf1("ptok", [128, 7], F32)
            ubuf = Tf1("ubuf", [128, 7], F32)
            m2b = Tf1("m2b", [128, 7], F32)
            jnk = Tf1("jnk", [128, 7], F32)
            F0 = Tf1("F0", [128, SB], F32)
            F1 = Tf1("F1", [128, SB], F32)
            T1s = Tf1("T1s", [16, 128], F32)
            g16 = Tf1("g16", [16, 256], F32)
            idxg = es_f1.enter_context(nc.sbuf_tensor("idxg", [128, 256], mybir.dt.int16))
            ycst = [Tf1(f"ycst{s}", [128, 512], BF16) for s in range(2)]
            ydst = [Tf1(f"ydst{s}", [128, 512], BF16) for s in range(2)]

            from concourse import library_config as _lc

            if cfg.sparse and cfg.NRE > 0:
                P.op("vector", lambda v: v.memset(ycst[0][:], 0.0), writes=["ycst0"])
                for half in range(2):
                    P.dma("gpsimd", "dwb", lambda g, half=half: g.dma_start(
                        yc_d[7 * C:7 * C + 1, half * 512:(half + 1) * 512],
                        ycst[0][0:1, 0:512]),
                        reads=["ycst0"], writes=["yczero"])
                P.op("vector", lambda v: v.memset(arrb[0:16, S // 16: S // 16 + CW], 0.0),
                     writes=["arrtail"])
                P.op("vector", lambda v: v.memset(idxf[:], 0.0), writes=["idxf"])
                _pc = cfg.prep_cut
                _lvl = _pc if _pc < 20 else (2 if _pc <= 29 else 4)
                _n_comp = 0 if _lvl < 2 else (_pc - 20 if 20 < _pc < 28 else cfg.NRE)
                _do_sg = _pc not in (28, 29, 58)
                if _lvl >= 2:
                    _lib2 = _lc.attnmlp if _pc == 29 else _lc.sparse_gather
                    P.op("gpsimd", lambda g, _lib2=_lib2: (g.load_library(_lib2),
                                                           g.nop())[1],
                         reads=["yczero", "h2t"], writes=["gplib"])
                if _pc == 29:
                    ctrlg = Tf1("ctrlg", [128, 1024], BF16)
                    idxc = es_f1.enter_context(
                        nc.sbuf_tensor("idxc", [128, 8], mybir.dt.int16))
                    P.op("vector", lambda v: v.memset(idxc[:], 0), writes=["idxc"])
                    P.dma("gpsimd", "dhc0", lambda g: g.dma_gather(
                        ctrlg[:].rearrange("p (b f) -> p b f", b=1),
                        h2t_d, idxc[:], 128, 128, 1024),
                        reads=["h2t", "idxc", "gplib"], writes=["ctrlg"])
                for e in range(_n_comp):
                    selap = top2B[:].rearrange("p (s e) -> p e s", e=7)[:, e, :]
                    P.mm_group([lambda t, selap=selap: t.transpose(
                        pb[2][0:SB, 0:128], selap, identf[:])],
                        reads=["top2B", "identf"], writes=["pb2"])
                    P.op("vector", lambda v: v.tensor_copy(T1s[0:SB, :], pb[2][0:SB, 0:128]),
                         reads=["pb2"], writes=["T1s"])
                    for pd in range(8):
                        P.mm_group([lambda t, pd=pd: t.transpose(
                            pb[3][0:16, 0:SB], T1s[0:SB, pd * 16:(pd + 1) * 16],
                            identf[0:SB, 0:SB])],
                            reads=["T1s", "identf"], writes=["pb3"])
                        dstw = arrb[0:16, 0:S // 16].rearrange("q (s d) -> q s d", d=8)[:, :, pd:pd + 1]
                        P.op("vector", lambda v, pd=pd, dstw=dstw: v.tensor_copy(
                            dstw, pb[3][0:16, 0:SB].rearrange("q (s d) -> q s d", d=1)),
                            reads=["pb3"], writes=["arrb"])
                    P.op("vector", lambda v: v.tensor_tensor(
                        arrb[0:16, 0:S // 16], arrb[0:16, 0:S // 16],
                        tokid1[0:16, 0:S // 16], ALU.mult),
                        reads=["arrb", "tokid1"], writes=["arrb"])
                    P.op("vector", lambda v: v.tensor_scalar(
                        arrb[0:16, 0:S // 16], arrb[0:16, 0:S // 16], -1.0, None, ALU.add, ALU.bypass),
                        reads=["arrb"], writes=["arrb"])
                    if _do_sg:
                        P.op("gpsimd", lambda g: g.sparse_gather(
                            idxf[0:16, :], arrb[0:16, :], num_found=nfb[:]),
                            reads=["arrb", "arrtail", "gplib"], writes=["idxf", "nfb"])
                    P.op("vector", lambda v, e=e: v.tensor_copy(
                        idxall[0:16, e * CW:(e + 1) * CW], idxf[0:16, 0:CW]),
                        reads=["idxf"], writes=["idxall"])
                for gpart in range(1, 8 if _lvl >= 3 else 1):
                    P.dma("sync", "dcst", lambda s, gpart=gpart: s.dma_start(
                        idxall[gpart * 16:(gpart + 1) * 16, 0:cfg.NRE * CW],
                        idxall[0:16, 0:cfg.NRE * CW]),
                        reads=["idxall"], writes=["idxall"])
                for sb in range(SB if _lvl >= 4 else 0):
                    P.mm_group([lambda t, sb=sb: t.transpose(
                        pb[3][0:128, 0:7], postens[0:7, sb * 128:(sb + 1) * 128], identf[0:7, 0:7])],
                        reads=["postens", "identf"], writes=["pb3"])
                    P.op("vector", lambda v: v.tensor_tensor(
                        ptok[:], pb[3][0:128, 0:7], iotac2[:], ALU.add),
                        reads=["pb3", "iotac2"], writes=["ptok"])
                    P.op("vector", lambda v, sb=sb: v.tensor_tensor(
                        m2b[:], top2B[:, sb * 7:(sb + 1) * 7], ismaxB[:, sb * 7:(sb + 1) * 7],
                        ALU.subtract),
                        reads=["top2B", "ismaxB"], writes=["m2b"])
                    P.op("vector", lambda v, sb=sb: v.scalar_tensor_tensor(
                        jnk[:], ptok[:], 1.0, ismaxB[:, sb * 7:(sb + 1) * 7], ALU.mult, ALU.mult,
                        accum_out=F0[:, sb:sb + 1]),
                        reads=["ptok", "ismaxB"], writes=["jnk", "F0"])
                    P.op("vector", lambda v, sb=sb: v.scalar_tensor_tensor(
                        jnk[:], ptok[:], 1.0, m2b[:], ALU.mult, ALU.mult,
                        accum_out=F1[:, sb:sb + 1]),
                        reads=["ptok", "m2b"], writes=["jnk", "F1"])
                    P.op("vector", lambda v, sb=sb: v.scalar_tensor_tensor(
                        jnk[:], pb[3][0:128, 0:7], 1.0, ismaxB[:, sb * 7:(sb + 1) * 7],
                        ALU.mult, ALU.mult, accum_out=r1[:]),
                        reads=["pb3", "ismaxB"], writes=["jnk", "r1"])
                    P.op("vector", lambda v: v.tensor_scalar(r1[:], r1[:], C + 0.5, None,
                                                             ALU.is_le, ALU.bypass),
                         reads=["r1"], writes=["r1"])
                    P.op("vector", lambda v, sb=sb: v.tensor_scalar(
                        F0[:, sb:sb + 1], F0[:, sb:sb + 1], -7.0 * C, r1[:], ALU.add, ALU.mult),
                        reads=["F0", "r1"], writes=["F0"])
                    P.op("vector", lambda v, sb=sb: v.tensor_scalar(
                        F0[:, sb:sb + 1], F0[:, sb:sb + 1], 7.0 * C, None, ALU.add, ALU.bypass),
                        reads=["F0"], writes=["F0"])
                    P.op("vector", lambda v, sb=sb: v.scalar_tensor_tensor(
                        jnk[:], pb[3][0:128, 0:7], 1.0, m2b[:], ALU.mult, ALU.mult,
                        accum_out=r2[:]),
                        reads=["pb3", "m2b"], writes=["jnk", "r2"])
                    P.op("vector", lambda v: v.tensor_scalar(r2[:], r2[:], C + 0.5, None,
                                                             ALU.is_le, ALU.bypass),
                         reads=["r2"], writes=["r2"])
                    P.op("vector", lambda v, sb=sb: v.tensor_scalar(
                        F1[:, sb:sb + 1], F1[:, sb:sb + 1], -7.0 * C, r2[:], ALU.add, ALU.mult),
                        reads=["F1", "r2"], writes=["F1"])
                    P.op("vector", lambda v, sb=sb: v.tensor_scalar(
                        F1[:, sb:sb + 1], F1[:, sb:sb + 1], 7.0 * C, None, ALU.add, ALU.bypass),
                        reads=["F1"], writes=["F1"])
                for slot, FB in enumerate([F0, F1] if _lvl >= 4 else []):
                    fk = "F0" if slot == 0 else "F1"
                    P.mm_group([lambda t, FB=FB: t.transpose(
                        pb[2][0:SB, 0:128], FB[:, 0:SB], identf[:])],
                        reads=[fk, "identf"], writes=["pb2"])
                    P.op("vector", lambda v: v.tensor_copy(T1s[0:SB, :], pb[2][0:SB, 0:128]),
                         reads=["pb2"], writes=["T1s"])
                    for pd in range(8):
                        P.mm_group([lambda t, pd=pd: t.transpose(
                            pb[3][0:16, 0:16], T1s[0:16, pd * 16:(pd + 1) * 16], identf[0:16, 0:16])],
                            reads=["T1s", "identf"], writes=["pb3"])
                        dst = g16[0:16, slot * 128: (slot + 1) * 128].rearrange(
                            "q (s d) -> q s d", d=8)[:, :, pd:pd + 1]
                        P.op("vector", lambda v, pd=pd, dst=dst: v.tensor_copy(
                            dst, pb[3][0:16, 0:16].rearrange("q (s d) -> q s d", d=1)),
                            reads=["pb3"], writes=["g16"])
                    P.op("vector", lambda v, slot=slot: v.tensor_copy(
                        idxg[0:16, slot * 128:(slot + 1) * 128], g16[0:16, slot * 128:(slot + 1) * 128]),
                        reads=["g16"], writes=["idxg"])
                for gpart in range(1, 8 if _lvl >= 4 else 1):
                    P.dma("sync", "dcst", lambda s, gpart=gpart: s.dma_start(
                        idxg[gpart * 16:(gpart + 1) * 16, :], idxg[0:16, :]),
                        reads=["idxg"], writes=["idxg"])
                if _lvl >= 4:
                    P.op("gpsimd", lambda g: (g.load_library(_lc.attnmlp),
                                              g.nop())[1],
                         reads=["yczero", "h2t"], writes=["gplib"])

            def emit_x1_dump():
                outflat = out_d.rearrange("s f -> (s f)")
                for hc in range(8):
                    dst = outflat[hc * 128 * S:(hc + 1) * 128 * S].rearrange(
                        "(p s) -> p s", p=128)
                    P.dma("scalar", "dout", lambda s, dst=dst, hc=hc: s.dma_start(
                        dst, x1_d[hc]),
                        reads=[f"x1_{hc}_{n}" for n in range(SC)])

            if cfg.stage == "sprep":
                emit_x1_dump()
                _finish(P, block, all_sems=True)
                return nc

            n_exp = 1 + cfg.NRE
            for e in range(n_exp):
                if e == 0:
                    g_src, u_src, d_src = sg_d, su_d, sd_d
                else:
                    g_src, u_src, d_src = rg_d[e - 1], ru_d[e - 1], rd_d[e - 1]
                    if cfg.sparse:
                        for h in range(2):
                            P.dma("gpsimd", f"dhc{h}", lambda g, e=e, h=h: g.dma_gather(
                                Hcmp[:, h * 4096:(h + 1) * 4096].rearrange("p (b j) -> p b j", j=512),
                                h2t_d, idxall[:, (e - 1) * CW + h * 32: (e - 1) * CW + h * 32 + 32],
                                512, 512, 1024, transpose=True),
                                reads=["h2t", "idxall", "gplib"], writes=[f"Hcmp{h}"])
                def emit_sg_dma(i, g_src=g_src, u_src=u_src):
                    sl = i % 2
                    sgk = f"sgstage{sl}"
                    P.dma("gpsimd", f"dsg{sl}", lambda g, i=i, sl=sl, g_src=g_src: g.dma_start(
                        sgstage[sl][:, 0:1024], g_src[i * 128:(i + 1) * 128, :]), writes=[sgk])
                    P.dma("gpsimd", f"dsg{sl}", lambda g, i=i, sl=sl, u_src=u_src: g.dma_start(
                        sgstage[sl][:, 1024:2048], u_src[i * 128:(i + 1) * 128, :]), writes=[sgk])

                def emit_sg_transpose(i):
                    sl = i % 2
                    sgk = f"sgstage{sl}"
                    wtk = f"wTf{sl}"
                    for m in range(2):
                        P.mm_group([
                            (lambda t, m=m, c=c, sl=sl: t.transpose(
                                pbb[m][:, c * 128:(c + 1) * 128],
                                sgstage[sl][:, m * 1024 + c * 128: m * 1024 + (c + 1) * 128],
                                identb[:]))
                            for c in range(8)],
                            reads=[sgk, "identb"], writes=[f"pbb{m}"])
                        P.op("scalar", lambda s, m=m, sl=sl: s.copy(
                            wTf[sl][:, m * 1024:(m + 1) * 1024], pbb[m][:]),
                            reads=[f"pbb{m}"], writes=[wtk])

                sparse_e = cfg.sparse and e > 0
                NTOK = C if sparse_e else S
                NSC = NTOK // 512
                if sparse_e:
                    def h_rhs(c, n):
                        return Hcmp[:, n * 4096 + c * 512: n * 4096 + (c + 1) * 512]

                    def h_keys(n):
                        return [f"Hcmp{n}"]
                else:
                    def h_rhs(c, n):
                        return Hs(c, n * 512, 512)

                    def h_keys(n):
                        return [f"H{c}" for c in range(8)]

                def emit_gateup(i, NSC=NSC, h_rhs=h_rhs, h_keys=h_keys):
                    sl = i % 2
                    wtk = f"wTf{sl}"
                    for n in range(NSC):
                        gb, ub = n % 2, 2 + n % 2
                        P.mm_group([
                            (lambda t, c=c, n=n, sl=sl, gb=gb: t.matmul(
                                pb[gb][:], wTf[sl][:, c * 128:(c + 1) * 128],
                                h_rhs(c, n), start=(c == 0), stop=(c == 7)))
                            for c in range(8)],
                            reads=[wtk] + h_keys(n), writes=[f"pb{gb}"])
                        P.mm_group([
                            (lambda t, c=c, n=n, sl=sl, ub=ub: t.matmul(
                                pb[ub][:], wTf[sl][:, 1024 + c * 128: 1024 + (c + 1) * 128],
                                h_rhs(c, n), start=(c == 0), stop=(c == 7)))
                            for c in range(8)],
                            reads=[wtk] + h_keys(n), writes=[f"pb{ub}"])
                        P.op("scalar", lambda s, gb=gb: s.activation(sil[:], pb[gb][:], AF.Silu,
                                                                     bias=0.0, scale=1.0),
                             reads=[f"pb{gb}"], writes=["sil"])
                        P.op("vector", lambda v, i=i, n=n, ub=ub: v.tensor_tensor(
                            tbig[:, i * S + n * 512: i * S + (n + 1) * 512], sil[:], pb[ub][:],
                            ALU.mult),
                            reads=["sil", f"pb{ub}"], writes=[f"t{i}"])

                emit_sg_dma(0)
                emit_sg_dma(1)
                for i in range(NI):
                    emit_sg_transpose(i)
                    if i + 2 < NI:
                        emit_sg_dma(i + 2)
                    if i >= 1:
                        emit_gateup(i - 1)
                emit_gateup(NI - 1)
                def emit_sd_dma(hc, d_src=d_src):
                    sl = hc % 2
                    P.dma("gpsimd", f"dsd{sl}", lambda g, hc=hc, sl=sl, d_src=d_src: g.dma_start(
                        sdstage[sl][:], d_src[hc * 128:(hc + 1) * 128, :]),
                        writes=[f"sdstage{sl}"])

                def emit_sd_transpose(hc):
                    sl = hc % 2
                    sdk, sdtk = f"sdstage{sl}", f"sdT{sl}"
                    for grp in range(3):
                        lo = grp * 8
                        hi = min(lo + 8, NI)
                        bkk = grp % 2
                        P.mm_group([
                            (lambda t, i=i, lo=lo, sl=sl, bkk=bkk: t.transpose(
                                pbb[bkk][:, (i - lo) * 128:(i - lo + 1) * 128],
                                sdstage[sl][:, i * 128:(i + 1) * 128], identb[:]))
                            for i in range(lo, hi)],
                            reads=[sdk, "identb"], writes=[f"pbb{bkk}"])
                        P.op("scalar", lambda s, lo=lo, hi=hi, sl=sl, bkk=bkk: s.copy(
                            sdT[sl][:, lo * 128: hi * 128], pbb[bkk][:, 0:(hi - lo) * 128]),
                            reads=[f"pbb{bkk}"], writes=[sdtk])

                def emit_down(hc, NSC=NSC, sparse_e=sparse_e, e=e):
                    sl = hc % 2
                    sdtk = f"sdT{sl}"
                    for n in range(NSC):
                        dbk = [4, 5, 2, 3][n % 4]
                        rsl = (hc * NSC + n) % 2
                        if not sparse_e:
                            P.dma("sync", "dri0", lambda s, hc=hc, n=n: s.dma_start(
                                rin[0][:], x1_d[hc][:, n * 512:(n + 1) * 512]),
                                reads=[f"x1_{hc}_{n}"], writes=["rin0"])
                        P.mm_group([
                            (lambda t, i=i, n=n, sl=sl, dbk=dbk: t.matmul(
                                pb[dbk][:], sdT[sl][:, i * 128:(i + 1) * 128],
                                tbig[:, i * S + n * 512: i * S + (n + 1) * 512],
                                start=(i == 0), stop=(i == NI - 1)))
                            for i in range(NI)],
                            reads=[sdtk] + [f"t{i}" for i in range(NI)], writes=[f"pb{dbk}"])
                        if not sparse_e:
                            P.op("vector", lambda v, n=n, dbk=dbk: v.tensor_tensor(
                                rout[0][:], rin[0][:], pb[dbk][:], ALU.add),
                                reads=["rin0", f"pb{dbk}"], writes=["rout0"])
                            P.dma("scalar", "dro0", lambda s, hc=hc, n=n: s.dma_start(
                                x1_d[hc][:, n * 512:(n + 1) * 512], rout[0][:]),
                                reads=["rout0"], writes=[f"x1_{hc}_{n}"])
                        else:
                            P.op("scalar", lambda s, n=n, dbk=dbk, rsl=rsl: s.copy(
                                ycst[rsl][:], pb[dbk][:]),
                                reads=[f"pb{dbk}"], writes=[f"ycst{rsl}"])
                            P.mm_group([
                                (lambda t, j=j, rsl=rsl: t.transpose(
                                    pbb[0][:, j * 128:(j + 1) * 128],
                                    ycst[rsl][:, j * 128:(j + 1) * 128], identb[:]))
                                for j in range(4)],
                                reads=[f"ycst{rsl}", "identb"], writes=["pbb0"])
                            P.op("scalar", lambda s, rsl=rsl: s.copy(
                                ydst[rsl][:], pbb[0][:, 0:512]),
                                reads=["pbb0"], writes=[f"ydst{rsl}"])
                            dst = yc_d[(e - 1) * C + n * 512:(e - 1) * C + (n + 1) * 512,
                                       hc * 128:(hc + 1) * 128].rearrange(
                                "(j p) f -> p j f", p=128)
                            P.dma("scalar", f"dro{rsl}", lambda s, dst=dst, rsl=rsl: s.dma_start(
                                dst, ydst[rsl][:].rearrange("p (j f) -> p j f", f=128)),
                                reads=[f"ydst{rsl}"], writes=[f"yc_{e}_{hc}_{n}"])

                emit_sd_dma(0)
                emit_sd_dma(1)
                for hc in range(8):
                    emit_sd_transpose(hc)
                    if hc + 2 < 8:
                        emit_sd_dma(hc + 2)
                    if hc >= 1:
                        emit_down(hc - 1)
                emit_down(7)

            if cfg.stage == "sexp":
                emit_x1_dump()
                _finish(P, block, all_sems=True)
                return nc

            all_yc = ([f"yc_{e}_{hc}_{n}" for e in range(1, n_exp) for hc in range(8)
                       for n in range(C // 512)] + ["yczero"]) if (cfg.sparse and cfg.NRE > 0) else []
            es_f1.close()
            P.fence()
            rinbig = [Tf(f"rinbig{s}", [128, 1024], F32) for s in range(2)]
            routbig = [Tf(f"routbig{s}", [128, 1024], F32) for s in range(2)]
            ygat2 = [[Tf(f"yg{sl}_{s}", [128, 1024], BF16) for s in range(2)] for sl in range(2)]
            sparse_fin = cfg.sparse and cfg.NRE > 0
            for sb in range(SB):
                sl = sb % 2
                if sparse_fin:
                    for slot in range(2):
                        P.dma("gpsimd", f"dga{slot}", lambda g, sb=sb, sl=sl, slot=slot: g.dma_gather(
                            ygat2[slot][sl][:].rearrange("p (b f) -> p b f", b=1),
                            yc_d, idxg[:, slot * 128 + sb * 8: slot * 128 + (sb + 1) * 8],
                            128, 128, 1024),
                            reads=["idxg"], writes=[f"yg{slot}{sl}"])
                src_fin = x1_d.rearrange("c p f -> p c f")[:, :, sb * 128:(sb + 1) * 128]
                P.dma("sync", f"dfi{sl}", lambda s, sl=sl, src_fin=src_fin: s.dma_start(
                    rinbig[sl][:].rearrange("p (c f) -> p c f", f=128), src_fin),
                    reads=[f"x1_{hc}_{n}" for hc in range(8) for n in range(SC)],
                    writes=[f"rinbig{sl}"])
                for g in range(2):
                    P.mm_group([
                        (lambda t, hc=hc, sl=sl: t.transpose(
                            pb[hc // 4][:, (hc % 4) * 128:(hc % 4 + 1) * 128],
                            rinbig[sl][:, hc * 128:(hc + 1) * 128], identf[:]))
                        for hc in range(4 * g, 4 * g + 4)],
                        reads=[f"rinbig{sl}", "identf"], writes=[f"pb{g}"])
                for g in range(2):
                    P.op("vector", lambda v, g=g, sl=sl: v.tensor_copy(
                        routbig[sl][:, g * 512:(g + 1) * 512], pb[g][:]),
                        reads=[f"pb{g}"], writes=[f"routbig{sl}"])
                if sparse_fin:
                    P.op("vector", lambda v, sb=sb, sl=sl: v.scalar_tensor_tensor(
                        routbig[sl][:], ygat2[0][sl][:], W0B[:, sb:sb + 1], routbig[sl][:],
                        ALU.mult, ALU.add),
                        reads=[f"yg0{sl}", "W0B", f"routbig{sl}"], writes=[f"routbig{sl}"])
                    P.op("vector", lambda v, sb=sb, sl=sl: v.scalar_tensor_tensor(
                        routbig[sl][:], ygat2[1][sl][:], W1B[:, sb:sb + 1], routbig[sl][:],
                        ALU.mult, ALU.add),
                        reads=[f"yg1{sl}", "W1B", f"routbig{sl}"], writes=[f"routbig{sl}"])
                P.dma("scalar", "dout", lambda s, sb=sb, sl=sl: s.dma_start(
                    out_d[sb * 128:(sb + 1) * 128, :], routbig[sl][:]),
                    reads=[f"routbig{sl}"])

            _finish(P, block)
    return nc


def _finish(P, block, all_sems=False):
    n_out = P.dma_count.get("dout", 0)
    waits = {"dout": n_out}
    keys = list(P.dma_count.keys()) if all_sems else ["dro0", "dro1"]
    for k in keys:
        if k != "dout" and P.dma_count.get(k):
            waits[k] = P.dma_count[k]
    P.ops["scalar"].append((dict(waits), lambda s: s.nop(), None, 0))
    P.finalize(block)


import numpy as np
from concourse.bass_utils import run_bass_kernel_spmd

_NC = 8
_S = 2048
_C = 1024


def _consts():
    i = np.arange(128)
    maskA = np.where(((i[None, :] <= i[:, None]) & (i[:, None] - i[None, :] <= 64))
                     | (i[None, :] == 0), 0.0, -1e30).astype(np.float32)
    mB = np.full((128, 384), -1e30, np.float32)
    mB[:, 0] = 0.0
    jj = np.arange(128)
    mB[:, 128:256] = np.where(jj[None, :] >= i[:, None] + 64, 0.0, -1e30)
    mB[:, 256:384] = np.where((jj[None, :] <= i[:, None]) & (i[:, None] - jj[None, :] <= 64),
                              0.0, -1e30)
    return {
        "identf": np.eye(128, dtype=np.float32),
        "identb": np.eye(128, dtype=np.float32),
        "maskA": np.concatenate([maskA, maskA], axis=1),
        "maskB": np.concatenate([mB, mB], axis=1),
        "onesrow": np.ones((1, 128), np.float32),
        "tokid1": (np.arange(128)[None, :] * 16 + np.arange(16)[:, None] + 1).astype(np.float32),
        "iotac2": np.broadcast_to((np.arange(7) * _C - 1).astype(np.float32), (128, 7)).copy(),
    }


_nc_cache = {}


def _get_nc():
    if "nc" not in _nc_cache:
        _nc_cache["nc"] = build(Cfg(S=_S, NRE=7, stage="full", NP=8, C=_C, sparse=True))
    return _nc_cache["nc"]


def kernel(L0_attn_hidden_states, L0_attn_Wq, L0_attn_bq, L0_attn_Wk, L0_attn_bk,
           L0_attn_Wv, L0_attn_bv, L0_attn_Wo, L0_attn_bo, L0_ffn_sg, L0_ffn_su,
           L0_ffn_sd, L0_ffn_rg, L0_ffn_ru, L0_ffn_rd, L0_ffn_wr, L0_ffn_rb):
    x = np.ascontiguousarray(np.asarray(L0_attn_hidden_states, dtype=np.float32))
    c = lambda a: np.ascontiguousarray(np.asarray(a, dtype=np.float32))
    shared = {
        "wq": c(L0_attn_Wq), "wk": c(L0_attn_Wk), "wv": c(L0_attn_Wv), "wo": c(L0_attn_Wo),
        "bq": c(L0_attn_bq), "bk": c(L0_attn_bk), "bv": c(L0_attn_bv), "bo": c(L0_attn_bo),
        "sg": c(L0_ffn_sg), "su": c(L0_ffn_su), "sd": c(L0_ffn_sd),
        "rg": c(L0_ffn_rg), "ru": c(L0_ffn_ru), "rd": c(L0_ffn_rd),
        "wr": c(L0_ffn_wr), "rb": c(L0_ffn_rb),
    }
    shared.update(_consts())
    nc = _get_nc()
    in_maps = [dict(shared, x=np.ascontiguousarray(x[b])) for b in range(_NC)]
    res = run_bass_kernel_spmd(nc, in_maps, core_ids=list(range(_NC)))
    return np.stack([res.results[b]["out"] for b in range(_NC)], axis=0)
